# version 19
# baseline (speedup 1.0000x reference)
import numpy as np
import concourse.bass as bass
import concourse.mybir as mybir
from concourse.bass_utils import run_bass_kernel_spmd
from contextlib import ExitStack

F32 = mybir.dt.float32
I32 = mybir.dt.int32
Act = mybir.ActivationFunctionType
Alu = mybir.AluOpType

D = 1024
NCH = 8
DH = 4096
DM = 2048
NE = 8
CAP = 2560.0
NB = 2
T = 8192
NTOK = NB * T
NC = 8
OWN = 2048
NH = 512
NT = NH + OWN
BLK = 512
NBLK = NT // BLK
TPB = BLK // 128
NTILE = NT // 128
GT = NTOK // 128
EPS = 1e-5
R1024 = 1.0 / 1024.0

ENGINES = ("sync", "gpsimd", "tensor", "vector", "scalar")
FCONSTS = (1e-5, -0.5, -1.0, 1.0 / 1024.0, 1.0, 2560.5, 1e-6, -1e30)

TAPS = ()
DEV_SKIP = ()
DEV_STOP = None


class Flow:

    def __init__(self, nc, ctx, semnames):
        self.nc = nc
        self.prog = []
        self.sem = {}
        self.cnt = {}
        self.lastw = {e: {} for e in ENGINES}
        for n in semnames:
            self.sem[n] = ctx.enter_context(nc.semaphore(n))
            self.cnt[n] = 0

    def now(self, *names):
        return [(n, self.cnt[n]) for n in names]

    AUTOSELF = {"vector": "v", "scalar": "s"}

    def on(self, eng, fn, inc=None, k=1, wait=None):
        wait = list(wait or [])
        aself = self.AUTOSELF.get(eng)
        if aself is not None:
            wait.append((aself, self.cnt[aself]))
        w = []
        for sname, val in wait:
            if val > self.lastw[eng].get(sname, 0):
                w.append((sname, val))
                self.lastw[eng][sname] = val
        if inc is not None:
            self.cnt[inc] += k
        sems = self.sem

        def run(e, _fn=fn, _w=w, _inc=inc, _k=k):
            for sname, val in _w:
                e.wait_ge(sems[sname], val)
            ins = _fn(e)
            if _inc is not None:
                assert ins is not None
                ins.then_inc(sems[_inc], _k)
        self.prog.append((eng, run))

    def barrier_from(self, eng, *sems):
        self.on(eng, lambda e: None, wait=self.now(*sems))


def build_program(taps=()):
    nc = bass.Bass("TRN2", target_bir_lowering=False, debug=False)
    ctx = ExitStack()

    def dram_in(name, shape, dt=F32):
        return nc.dram_tensor(name, shape, dt, kind="ExternalInput").ap()

    def dram_out(name, shape, dt=F32):
        return nc.dram_tensor(name, shape, dt, kind="ExternalOutput").ap()

    def sb(name, shape, dt=F32):
        return ctx.enter_context(nc.sbuf_tensor(name, shape, dt)).ap()

    def ps(name, shape, dt=F32):
        return ctx.enter_context(nc.psum_tensor(name, shape, dt)).ap()

    xall = dram_in("xall", [NTOK, D])
    xph = dram_in("xph", [NT, D])
    mh_d = dram_in("mh", [128, 1])
    mhinv_d = dram_in("mhinv", [128, 1])
    rank_d = dram_in("rank", [1, 1], I32)
    gsel_d = dram_in("gsel", [128, 8])
    idn_d = dram_in("idn", [128, 128])
    triu_d = dram_in("triu", [128, 128])
    onesc_d = dram_in("onesc", [128, 1])
    onesr_d = dram_in("onesr", [1, 128])
    fconst_d = dram_in("fconst", [128, 8])
    wkey_d = dram_in("wkey", [D, D])
    wval_d = dram_in("wval", [D, D])
    wrec_d = dram_in("wrec", [D, D])
    wout_d = dram_in("wout", [D, D])
    fwkey_d = dram_in("fwkey", [D, DH])
    fwrec_d = dram_in("fwrec", [D, D])
    fwval_d = dram_in("fwval", [DH, D])
    wgateT_d = dram_in("wgateT", [D, DM])
    wdownT_d = dram_in("wdownT", [NE, DM, D])
    wrouter_d = dram_in("wrouter", [NE, D])
    pvec_d = dram_in("pvec", [128, 12 * NCH])

    y_out = dram_out("y", [OWN, D])
    BF16 = mybir.dt.bfloat16
    fwkeyHL_d = nc.dram_tensor("fwkeyHL", [D, 2 * DH], BF16, kind="Internal").ap()
    fwvalHL_d = nc.dram_tensor("fwvalHL", [DH, 2 * D], BF16, kind="Internal").ap()
    wgateHL_d = nc.dram_tensor("wgateHL", [D, 2 * DM], BF16, kind="Internal").ap()
    wdownHL_d = nc.dram_tensor("wdownHL", [NE, DM, 2 * D], BF16, kind="Internal").ap()

    idn = sb("idn_sb", [128, 128])
    triu = sb("triu_sb", [128, 128])
    onesc = sb("onesc_sb", [128, 1])
    onesr = sb("onesr_sb", [1, 128])
    mh = sb("mh_s", [128, 1])
    mhinv = sb("mhinv_s", [128, 1])
    rank_s = sb("rank_s", [1, 1], I32)
    gsel = sb("gsel_s", [128, 8])
    pvec = sb("pvec_s", [128, 12 * NCH])
    fconst = sb("fconst_s", [128, 8])
    for _i, _v in enumerate(FCONSTS):
        nc.const_aps.register(F32, _v, fconst[:, _i:_i + 1])
    wder = sb("wder", [128, 3 * NCH])
    negbig = sb("negbig", [128, 8])
    wr_raw = sb("wr_raw", [NE, D])
    wr_sum = sb("wr_sum", [NE, 1])
    wrcT = sb("wrcT", [128, NCH * NE])

    z_all = sb("z_all", [128, GT * NE])
    oh1k_all = sb("oh1k_all", [128, GT * NE], BF16)
    oh2k_all = sb("oh2k_all", [128, GT * NE], BF16)
    offrow = sb("offrow", [1, NE])
    z_own = sb("z_own", [128, 16 * NE])
    oh1k_own = sb("oh1k_own", [128, 16 * NE])
    oh2k_own = sb("oh2k_own", [128, 16 * NE])
    rxt = [sb(f"rxt{i}", [128, D]) for i in range(2)]
    rxT = [sb(f"rxT{i}", [128, D]) for i in range(2)]
    rsc = sb("rsc", [128, 64])
    rt_s = sb("rt_s", [128, 5 * 128])
    cbuf = sb("cbuf", [1, 136])
    gsel_d2 = None

    rsig0_all = sb("rsig0_all", [128, NTILE])
    m0_all = sb("m0_all", [128, NTILE])
    xt = [sb(f"xt{i}", [128, D]) for i in range(TPB)]
    xnt = sb("xnt", [128, D])
    u1t = sb("u1t", [128, D])
    stat = sb("stat", [128, 16])
    A_x1T = sb("A_x1T", [128, NCH * BLK])
    B_mix = sb("B_mix", [128, NCH * BLK])
    C_k = sb("C_k", [128, NCH * BLK])
    D_v = sb("D_v", [128, NCH * BLK])
    F_S = sb("F_S", [128, NCH * (BLK + 1)])
    G_Z = sb("G_Z", [128, NCH * (BLK + 1)])
    x2t = rxt + rxT
    bnd1 = sb("bnd1", [128, NCH])
    bnd2 = sb("bnd2", [128, NCH])
    XNTs = A_x1T[:, 0:NCH * 256]
    Hms = C_k[:, 0:16 * 256]

    wbuf = [sb(f"wbuf{i}", [128, D]) for i in range(2)]
    wbuf2 = [sb(f"wbuf2_{i}", [128, D]) for i in range(2)]
    wbufB = [w.bitcast(BF16) for w in wbuf]
    wbuf2B = [w.bitcast(BF16) for w in wbuf2]
    absF = sb("absF", [128, DM])
    absB = absF.bitcast(BF16)
    cnt_sb = sb("cnt_sb", [1, 1040])
    cnt2_sb = sb("cnt2_sb", [1, 1040])

    sgx = sb("sgx", [128, D])
    frs = [D_v[:, i * D:(i + 1) * D] for i in range(TPB)]
    yt = [B_mix[:, i * D:(i + 1) * D] for i in range(TPB)]
    cmb = sb("cmb", [128, 16 * NE])
    gsc = sb("gsc", [128, 16])

    P_A = ps("P_A", [128, 512])
    P_B = ps("P_B", [128, 512])
    P_C = ps("P_C", [128, 512])
    P_D = ps("P_D", [128, 512])
    P_E = ps("P_E", [128, 1024])
    P_F = ps("P_F", [128, 1024])

    fl = Flow(nc, ctx, ["qc", "qr0", "qr1", "qx0", "qx1", "qx2", "qx3",
                        "qw0", "qw1", "qv0", "qv1", "qo0", "qo1", "qo2", "qo3",
                        "qsi0", "qsi1", "qsi2", "qsi3", "qso0", "qso1", "qw2", "qv2", "t", "v", "s"])
    on, now = fl.on, fl.now

    def dma(e, dst, src):
        return e.dma_start(out=dst, in_=src)

    for dst, src in [(fconst, fconst_d), (gsel, gsel_d), (idn, idn_d), (triu, triu_d), (onesc, onesc_d),
                     (onesr, onesr_d), (mh, mh_d), (mhinv, mhinv_d),
                     (rank_s, rank_d), (wr_raw, wrouter_d)]:
        on("sync", lambda e, d=dst, s=src: dma(e, d, s), inc="qc", k=16)
    on("sync", lambda e: dma(e, pvec[:, 0:12 * NCH], pvec_d[:]), inc="qc", k=16)

    def pv(p, c):
        return pvec[:, p * NCH + c: p * NCH + c + 1]

    def der(i, c):
        return wder[:, i * NCH + c: i * NCH + c + 1]

    w_qx = now("qc")
    for c in range(NCH):
        on("scalar", lambda e, c=c: e.activation(der(2, c), pv(4, c), Act.Exp), wait=w_qx, inc="s")
        on("scalar", lambda e, c=c: e.activation(der(1, c), pv(5, c), Act.Exp), inc="s")
    on("scalar", lambda e: None, wait=now("s"))
    for c in range(NCH):
        on("scalar", lambda e, c=c: e.activation(der(0, c), der(2, c), Act.Exp, scale=-1.0), inc="s")
    on("vector", lambda e: e.memset(negbig[:], -1e30), inc="v")

    on("vector", lambda e: e.reduce_sum(wr_sum[:], wr_raw[:], axis=mybir.AxisListType.X), wait=w_qx, inc="v")
    on("scalar", lambda e: e.mul(wr_sum[:], wr_sum[:], R1024), wait=now("v"), inc="s")
    on("vector", lambda e: e.tensor_scalar(wr_raw[:], wr_raw[:], wr_sum[:, 0:1], None, Alu.subtract),
       wait=now("s"), inc="v")
    w_v = now("v", "qc")
    for c in range(NCH):
        on("tensor", lambda e, c=c: e.matmul(
            P_A[:, c * 8:(c + 1) * 8], wr_raw[:, c * 128:(c + 1) * 128], idn[0:NE, 0:NE],
            is_transpose=True, start=(c == 0), stop=(c == NCH - 1)),
           wait=w_v if c == 0 else None, inc="t" if c == NCH - 1 else None)
    on("scalar", lambda e: e.copy(wrcT[:, 0:NCH * NE], P_A[:, 0:NCH * NE]), wait=now("t"), inc="s")

    skip_routing = "routing" in DEV_SKIP
    GB = 16
    NG = GT // GB
    def rslot(i):
        return rt_s[:, i * 128:(i + 1) * 128]

    def rview(i):
        return rt_s[:, i * 128:(i + 1) * 128].rearrange("p (t e) -> p t e", e=NE)

    t_after_tr = {}
    s_after_ev = {}
    v_after_zev = {}

    def vsw():
        on("vector", lambda e: None, wait=now("v"))

    def _colhl(hl_ap, rs, cc, which):
        v = hl_ap[rs, cc * 2048:(cc + 1) * 2048].rearrange("p (co o) -> p co o", o=256)
        return v[:, :, 0:128] if which == 0 else v[:, :, 128:256]

    split_pieces = []
    for r in range(8):
        rs = slice(r * 128, (r + 1) * 128)
        for cc in range(4):
            sl = (rs, slice(cc * 1024, (cc + 1) * 1024))
            split_pieces.append((fwkey_d[sl], _colhl(fwkeyHL_d, rs, cc, 0),
                                 _colhl(fwkeyHL_d, rs, cc, 1)))
    for r in range(32):
        rs = slice(r * 128, (r + 1) * 128)
        split_pieces.append((fwval_d[rs, 0:D], fwvalHL_d[rs, 0:D], fwvalHL_d[rs, D:2 * D]))
    for r in range(8):
        rs = slice(r * 128, (r + 1) * 128)
        for cc in range(2):
            sl = (rs, slice(cc * 1024, (cc + 1) * 1024))
            split_pieces.append((wgateT_d[sl], _colhl(wgateHL_d, rs, cc, 0),
                                 _colhl(wgateHL_d, rs, cc, 1)))
    for e_ in range(NE):
        for mo in range(16):
            ms = slice(mo * 128, (mo + 1) * 128)
            split_pieces.append((wdownT_d[e_, ms, 0:D], wdownHL_d[e_, ms, 0:D],
                                 wdownHL_d[e_, ms, D:2 * D]))
    split_state = {"i": 0}
    SPL = 4

    def _issue_in(j, w):
        sl = j % SPL
        srcp = split_pieces[j][0]
        on("gpsimd", lambda e, sl=sl, srcp=srcp: e.dma_start(
            out=D_v[:, sl * D:(sl + 1) * D], in_=srcp),
           wait=w, inc=f"qsi{sl}", k=16)

    def emit_splits(n):
        for _ in range(n):
            i = split_state["i"]
            if i >= len(split_pieces):
                return
            split_state["i"] = i + 1
            if i == 0:
                for j in range(min(SPL, len(split_pieces))):
                    _issue_in(j, None)
            sl = i % SPL
            db = i % 2
            a0 = db * 2 * D
            _, hdst, ldst = split_pieces[i]
            on("scalar", lambda e, sl=sl, a0=a0: e.copy(absB[:, a0:a0 + D],
                                                        D_v[:, sl * D:(sl + 1) * D]),
               wait=now(f"qsi{sl}", f"qso{db}"), inc="s")
            on("vector", lambda e, sl=sl, a0=a0: e.tensor_tensor(
                absB[:, a0 + D:a0 + 2 * D], D_v[:, sl * D:(sl + 1) * D],
                absB[:, a0:a0 + D], Alu.subtract),
               wait=now("s"), inc="v")
            wv = now("v")
            on("gpsimd", lambda e, hdst=hdst, a0=a0: e.dma_start(out=hdst, in_=absB[:, a0:a0 + D]),
               wait=wv, inc=f"qso{db}", k=16)
            on("gpsimd", lambda e, ldst=ldst, a0=a0: e.dma_start(out=ldst, in_=absB[:, a0 + D:a0 + 2 * D]),
               inc=f"qso{db}", k=16)
            if i + SPL < len(split_pieces):
                _issue_in(i + SPL, wv)

    for g in range(GT if not skip_routing else 0):
        gi, ti = g // GB, g % GB
        db = g % 2
        on("sync" if db == 0 else "scalar",
           lambda e, g=g, db=db: dma(e, rxt[db][:], xall[g * 128:(g + 1) * 128, :]),
           wait=[("t", t_after_tr[g - 2])] if g >= 2 else None, inc=f"qr{db}", k=16)
        PT = P_E if g % 2 == 0 else P_F
        wt = now(f"qr{db}") + ([("s", s_after_ev[g - 2])] if g >= 2
                               else (now("qc") if g == 0 else []))
        for c in range(NCH):
            on("tensor", lambda e, c=c, db=db, PT=PT: e.matmul(
                PT[:, c * 128:(c + 1) * 128], rxt[db][:, c * 128:(c + 1) * 128], idn[:],
                is_transpose=True,
                start=(c % 4 == 0), stop=(c % 4 == 3)),
               wait=wt if c == 0 else None, inc="t" if c == NCH - 1 else None)
        t_after_tr[g] = fl.cnt["t"]
        wt = now("t")
        on("scalar", lambda e, db=db, PT=PT: e.copy(rxT[db][:, 0:512], PT[:, 0:512]), wait=wt, inc="s")
        on("scalar", lambda e, db=db, PT=PT: e.copy(rxT[db][:, 512:1024], PT[:, 512:1024]), inc="s")
        s_after_ev[g] = fl.cnt["s"]
        zb = P_A if gi % 2 == 0 else P_B
        ws = now("s") + ([("v", v_after_zev.get(gi - 2, 0))] if ti == 0 else [])
        for c in range(NCH):
            on("tensor", lambda e, c=c, db=db, ti=ti, zb=zb: e.matmul(
                zb[:, ti * NE:(ti + 1) * NE], rxT[db][:, c * 128:(c + 1) * 128],
                wrcT[:, c * 8:(c + 1) * 8],
                start=(ti == 0 and c == 0), stop=(ti == GB - 1 and c == NCH - 1)),
               wait=ws if c == 0 else None,
               inc="t" if (ti == GB - 1 and c == NCH - 1) else None)
        if ti == GB - 1:
            on("vector", lambda e, gi=gi, zb=zb: e.tensor_copy(
                z_all[:, gi * 128:(gi + 1) * 128], zb[:, 0:GB * NE]), wait=now("t"), inc="v")
            v_after_zev[gi] = fl.cnt["v"]
        emit_splits(2)

    if not skip_routing:
        O1 = A_x1T[:, 0:1024]
        S2 = A_x1T[:, 1024:2048]
        S3 = A_x1T[:, 2048:3072]
        ZM = A_x1T[:, 3072:4096]
        O2 = B_mix[:, 0:1024]
        SP = B_mix[:, 1024:2048]

        def wv3(ap):
            return ap.rearrange("p (t e) -> p t e", e=NE)

        def top1_wide(src, dst):
            vsw()
            on("vector", lambda e, src=src: e.tensor_reduce(
                rt_s[:, 0:GT], wv3(src), mybir.AxisListType.X, Alu.max), inc="v")
            vsw()
            on("vector", lambda e, src=src, dst=dst: e.tensor_tensor(
                wv3(dst), wv3(src),
                rt_s[:, 0:GT].unsqueeze(2).broadcast_to((128, GT, NE)), Alu.is_ge), inc="v")
            vsw()
            on("vector", lambda e, dst=dst: e.tensor_copy(S2, dst), inc="v")
            vsw()
            on("vector", lambda e, dst=dst: e.tensor_add(
                wv3(S2)[:, :, 1:8], wv3(dst)[:, :, 1:8], wv3(dst)[:, :, 0:7]), inc="v")
            vsw()
            on("vector", lambda e: e.tensor_copy(S3, S2), inc="v")
            vsw()
            on("vector", lambda e: e.tensor_add(
                wv3(S3)[:, :, 2:8], wv3(S2)[:, :, 2:8], wv3(S2)[:, :, 0:6]), inc="v")
            vsw()
            on("vector", lambda e: e.tensor_copy(S2, S3), inc="v")
            vsw()
            on("vector", lambda e: e.tensor_add(
                wv3(S2)[:, :, 4:8], wv3(S3)[:, :, 4:8], wv3(S3)[:, :, 0:4]), inc="v")
            vsw()
            on("vector", lambda e: e.tensor_single_scalar(S2, S2, 1.0, Alu.is_equal), inc="v")
            vsw()
            on("vector", lambda e, dst=dst: e.tensor_mul(dst, dst, S2), inc="v")

        top1_wide(z_all[:, 0:GT * NE], O1)
        vsw()
        on("vector", lambda e: e.scalar_tensor_tensor(
            ZM, O1, -1e30, z_all[:, 0:GT * NE], Alu.mult, Alu.add), inc="v")
        top1_wide(ZM, O2)
        vsw()
        on("vector", lambda e: e.tensor_add(S3, O1, O2), inc="v")
        wv = now("v")
        on("tensor", lambda e: e.matmul(P_C[0:1, 0:512], onesc[:], S3[:, 0:512],
                                        start=True, stop=True), wait=wv, inc="t")
        on("tensor", lambda e: e.matmul(P_D[0:1, 0:512], onesc[:], S3[:, 512:1024],
                                        start=True, stop=True), inc="t")
        wt = now("t")
        on("vector", lambda e: e.memset(cnt_sb[0:1, 0:8], 0.0), inc="v")
        on("vector", lambda e: e.tensor_copy(cnt_sb[0:1, 8:520], P_C[0:1, 0:512]), wait=wt, inc="v")
        on("vector", lambda e: e.tensor_copy(cnt_sb[0:1, 520:1032], P_D[0:1, 0:512]), inc="v")
        pp = [cnt_sb, cnt2_sb]
        for k, stp in enumerate((8, 16, 32, 64, 128, 256, 512)):
            sb_, db_ = pp[k % 2], pp[1 - k % 2]
            vsw()
            on("vector", lambda e, sb_=sb_, db_=db_, stp=stp: e.tensor_copy(
                db_[0:1, 0:8 + stp], sb_[0:1, 0:8 + stp]), inc="v")
            on("vector", lambda e, sb_=sb_, db_=db_, stp=stp: e.tensor_add(
                db_[0:1, 8 + stp:1032], sb_[0:1, 8 + stp:1032],
                sb_[0:1, 8:1032 - stp]), inc="v")
        cntF = pp[7 % 2]
        vsw()
        wv = now("v")
        firstb = {0: True, 1: True}
        for t in range(GT):
            bi = 0 if t < 64 else 1
            bk = P_A if bi == 0 else P_B
            col0 = (t % 64) * 8
            on("tensor", lambda e, t=t, bk=bk, col0=col0: e.matmul(
                bk[:, col0:col0 + 8], triu[:], S3[:, t * NE:(t + 1) * NE],
                start=(t % 64 == 0), stop=False),
               wait=wv if firstb[bi] else None)
            firstb[bi] = False
            on("tensor", lambda e, t=t, bk=bk, col0=col0: e.matmul(
                bk[:, col0:col0 + 8], onesr[:], cntF[0:1, t * NE:(t + 1) * NE],
                start=False, stop=(t % 64 == 63)),
               inc="t" if t % 64 == 63 else None)
        wt = now("t")
        on("vector", lambda e: e.tensor_single_scalar(
            S2[:, 0:512], P_A[:, 0:512], CAP + 0.5, Alu.is_le), wait=wt, inc="v")
        on("vector", lambda e: e.tensor_single_scalar(
            S2[:, 512:1024], P_B[:, 0:512], CAP + 0.5, Alu.is_le), inc="v")
        vsw()
        on("vector", lambda e: e.tensor_mul(O1, O1, S2), inc="v")
        on("vector", lambda e: e.tensor_mul(O2, O2, S2), inc="v")
        vsw()
        on("vector", lambda e: e.tensor_copy(oh1k_all[:, 0:GT * NE], O1), inc="v")
        on("vector", lambda e: e.tensor_copy(oh2k_all[:, 0:GT * NE], O2), inc="v")
        for srcb, dstb in ((z_all, z_own), (O1, oh1k_own), (O2, oh2k_own)):
            vsw()
            for gi in range(NG):
                on("vector", lambda e, srcb=srcb, gi=gi: e.tensor_scalar(
                    SP[:, gi * 128:(gi + 1) * 128], srcb[:, gi * 128:(gi + 1) * 128],
                    gsel[:, gi:gi + 1], None, Alu.mult), inc="v")
            vsw()
            on("vector", lambda e: e.tensor_add(ZM[:, 0:512], SP[:, 0:512], SP[:, 512:1024]), inc="v")
            vsw()
            on("vector", lambda e: e.tensor_add(ZM[:, 512:768], ZM[:, 0:256], ZM[:, 256:512]), inc="v")
            vsw()
            on("vector", lambda e, dstb=dstb: e.tensor_add(
                dstb[:, 0:128], ZM[:, 512:640], ZM[:, 640:768]), inc="v")

    emit_splits(10 ** 9)
    SPLITG = now("s", "v", "qso0", "qso1")

    _emit_blocks(nc, fl, locals())

    for tname in taps:
        if ":" in tname:
            nm, idx = tname.split(":")
            buf = locals()[nm][int(idx)]
        else:
            buf = locals()[tname]
        td = dram_out("tap_" + tname.replace(":", "_"), list(buf.shape), buf.dtype)
        on("sync", lambda e, b=buf, t=td: dma(e, t[:], b[:]),
           wait=now("v", "s", "t", "qw0", "qw1", "qv0", "qv1"), inc="qo0", k=16)

    on("sync", lambda e: None, wait=now("qo0", "qo1", "qo2", "qo3"))

    with nc.Block() as block:
        @block.sync
        def _(e):
            for eng, run in fl.prog:
                if eng == "sync":
                    run(e)

        @block.gpsimd
        def _(e):
            for eng, run in fl.prog:
                if eng == "gpsimd":
                    run(e)

        @block.tensor
        def _(e):
            for eng, run in fl.prog:
                if eng == "tensor":
                    run(e)

        @block.vector
        def _(e):
            for eng, run in fl.prog:
                if eng == "vector":
                    run(e)

        @block.scalar
        def _(e):
            for eng, run in fl.prog:
                if eng == "scalar":
                    run(e)

    return nc, ctx


def _emit_blocks(nc, fl, L):
    on, now = fl.on, fl.now
    g = lambda n: L[n]
    (xph, xt, xnt, u1t, stat, A_x1T, B_mix, C_k, D_v, F_S, G_Z, x2t, bnd1, bnd2,
     rsig0_all, m0_all, pvec, wder, mh, mhinv, idn, y_out) = (
        g("xph"), g("xt"), g("xnt"), g("u1t"), g("stat"), g("A_x1T"), g("B_mix"),
        g("C_k"), g("D_v"), g("F_S"), g("G_Z"), g("x2t"), g("bnd1"), g("bnd2"),
        g("rsig0_all"), g("m0_all"), g("pvec"), g("wder"), g("mh"), g("mhinv"),
        g("idn"), g("y_out"))
    P_A, P_B, P_C, P_D, P_E, P_F = g("P_A"), g("P_B"), g("P_C"), g("P_D"), g("P_E"), g("P_F")

    def pv(p, c):
        return pvec[:, p * NCH + c: p * NCH + c + 1]

    def der(i, c):
        return wder[:, i * NCH + c: i * NCH + c + 1]

    def st(i):
        return stat[:, i:i + 1]

    on("vector", lambda e: e.memset(bnd1[:], 0.0), inc="v")
    on("vector", lambda e: e.memset(bnd2[:], 0.0), inc="v")
    on("vector", lambda e: e.memset(stat[:], 0.0), inc="v")
    mark0_v = fl.cnt["v"]

    mark = {"x_users": 0, "pe_evac": fl.cnt["s"], "pf_evac": fl.cnt["v"],
            "tile_v": mark0_v, "tile_t": 0}

    def stageA(b):
        for tl in range(TPB):
            t = b * TPB + tl
            wprev = [("v", mark["x_users"])] if b > 0 else None
            on("sync", lambda e, t=t, tl=tl: e.dma_start(
                out=xt[tl][:], in_=xph[t * 128:(t + 1) * 128, :]),
               wait=wprev, inc=f"qx{tl}", k=16)
            wq = now(f"qx{tl}", "qc") + [("v", mark["tile_v"]), ("t", mark["tile_t"])]
            on("scalar", lambda e, tl=tl: e.activation(u1t[:], xt[tl][:], Act.Copy,
                                                       accum_out=st(0)), wait=wq, inc="s")
            on("scalar", lambda e, tl=tl: e.activation(u1t[:], xt[tl][:], Act.Square,
                                                       accum_out=st(1)), inc="s")
            on("scalar", lambda e, t=t: e.mul(m0_all[:, t:t + 1], st(0), R1024), inc="s")
            on("scalar", lambda e: e.mul(st(3), st(1), R1024), inc="s")
            on("scalar", lambda e, t=t: e.mul(st(4), m0_all[:, t:t + 1], m0_all[:, t:t + 1]), inc="s")
            on("scalar", lambda e: e.activation(st(5), st(4), Act.Identity,
                                                scale=-1.0, bias=st(3)), inc="s")
            on("scalar", lambda e: e.activation(st(6), st(5), Act.Ln, bias=EPS), inc="s")
            on("scalar", lambda e, t=t: e.activation(rsig0_all[:, t:t + 1], st(6), Act.Exp,
                                                     scale=-0.5), inc="s")
            ws = now("s")
            on("vector", lambda e, tl=tl, t=t: e.tensor_scalar(
                xnt[:], xt[tl][:], m0_all[:, t:t + 1], rsig0_all[:, t:t + 1],
                Alu.subtract, Alu.mult), wait=ws, inc="v")
            wv = now("v")
            on("scalar", lambda e: e.activation(u1t[:], xnt[:], Act.Copy, accum_out=st(8)),
               wait=wv, inc="s")
            on("scalar", lambda e: e.activation(u1t[:], xnt[:], Act.Square, accum_out=st(9)), inc="s")
            on("scalar", lambda e: e.mul(st(10), st(8), R1024), inc="s")
            on("scalar", lambda e: e.mul(st(11), st(9), R1024), inc="s")
            on("scalar", lambda e: e.mul(st(12), st(10), st(10)), inc="s")
            on("scalar", lambda e: e.activation(st(13), st(12), Act.Identity,
                                                scale=-1.0, bias=st(11)), inc="s")
            on("scalar", lambda e: e.activation(st(14), st(13), Act.Ln, bias=EPS), inc="s")
            on("scalar", lambda e: e.activation(st(15), st(14), Act.Exp, scale=-0.5), inc="s")
            ws = now("s")
            on("vector", lambda e: e.tensor_scalar(u1t[:], xnt[:], st(10), st(15),
                                                   Alu.subtract, Alu.mult), wait=ws, inc="v")
            mark["tile_v"] = fl.cnt["v"]
            wv = now("v") + [("v", mark["pf_evac"])]
            for c in range(NCH):
                on("tensor", lambda e, c=c: e.matmul(
                    P_F[:, c * 128:(c + 1) * 128], u1t[:, c * 128:(c + 1) * 128], idn[:],
                    is_transpose=True, start=(c % 4 == 0), stop=(c % 4 == 3)),
                   wait=wv if c == 0 else None, inc="t" if c == NCH - 1 else None)
            mark["tile_t"] = fl.cnt["t"]
            wt = now("t")
            for c in range(NCH):
                on("vector", lambda e, c=c, tl=tl: e.tensor_scalar(
                    A_x1T[:, c * BLK + tl * 128: c * BLK + tl * 128 + 128],
                    P_F[:, c * 128:(c + 1) * 128], pv(0, c), pv(1, c),
                    Alu.mult, Alu.add), wait=wt if c == 0 else None, inc="v")
            mark["pf_evac"] = fl.cnt["v"]

    wkey_d, wval_d, wrec_d, wout_d = g("wkey_d"), g("wval_d"), g("wrec_d"), g("wout_d")
    fwkey_d, fwrec_d, fwval_d = g("fwkey_d"), g("fwrec_d"), g("fwval_d")
    wgateT_d, wdownT_d = g("wgateT_d"), g("wdownT_d")
    wbuf, wbuf2 = g("wbuf"), g("wbuf2")
    sgx, frs, yt = g("sgx"), g("frs"), g("yt")
    XNTs, Hms = g("XNTs"), g("Hms")
    z_own, oh1k_own, oh2k_own, cmb, gsc, rsc = (g("z_own"), g("oh1k_own"),
                                                g("oh2k_own"), g("cmb"), g("gsc"), g("rsc"))
    wbufB, wbuf2B, absB, SPLITG = g("wbufB"), g("wbuf2B"), g("absB"), g("SPLITG")
    absF = g("absF")
    fwkeyHL_d, fwvalHL_d, wgateHL_d, wdownHL_d = (
        g("fwkeyHL_d"), g("fwvalHL_d"), g("wgateHL_d"), g("wdownHL_d"))
    BF16 = mybir.dt.bfloat16
    A_B = A_x1T.bitcast(BF16)
    C_B = C_k.bitcast(BF16)
    sgxB = sgx.bitcast(BF16)

    pfree = {n: ("v", 0) for n in ("P_A", "P_B", "P_C", "P_D", "P_E", "P_F")}
    PS = {"P_A": P_A, "P_B": P_B, "P_C": P_C, "P_D": P_D, "P_E": P_E, "P_F": P_F}
    wguard = {0: 0, 1: 0}
    wguard2 = {0: 0, 1: 0}
    wguard3 = [0]
    wguard4 = [0]
    wgF3k = [0]
    wgF3v = [0]
    WQE = {0: "gpsimd", 1: "sync", 2: "scalar"}
    wcnt = [0]
    wcnt2 = [0]

    def gs(i):
        return gsc[:, i:i + 1]

    def mix_into_B(pidx, c0, c1, wextra=None, src=None, bnd=None):
        src = A_x1T if src is None else src
        bnd = bnd1 if bnd is None else bnd
        for c in range(NCH):
            b0 = c * BLK
            on("vector", lambda e, c=c, b0=b0: e.tensor_sub(
                B_mix[:, b0 + max(c0, 1):b0 + c1], src[:, b0 + max(c0, 1):b0 + c1],
                src[:, b0 + max(c0, 1) - 1:b0 + c1 - 1]),
               wait=wextra if c == 0 else None, inc="v")
        if c0 == 0:
            for c in range(NCH):
                b0 = c * BLK
                on("vector", lambda e, c=c, b0=b0: e.tensor_sub(
                    B_mix[:, b0:b0 + 1], src[:, b0:b0 + 1], bnd[:, c:c + 1]), inc="v")
        for c in range(NCH):
            b0 = c * BLK
            on("vector", lambda e, c=c, b0=b0: e.scalar_tensor_tensor(
                B_mix[:, b0 + max(c0, 1):b0 + c1], B_mix[:, b0 + max(c0, 1):b0 + c1],
                pv(pidx, c), src[:, b0 + max(c0, 1) - 1:b0 + c1 - 1],
                Alu.mult, Alu.add), inc="v")
        if c0 == 0:
            on("vector", lambda e: None, wait=now("v"))
            for c in range(NCH):
                b0 = c * BLK
                on("vector", lambda e, c=c, b0=b0: e.scalar_tensor_tensor(
                    B_mix[:, b0:b0 + 1], B_mix[:, b0:b0 + 1], pv(pidx, c),
                    bnd[:, c:c + 1], Alu.mult, Alu.add), inc="v")

    def proj_from_B(Wd, dst, c0, c1, sigmoid=False, evac_wait=None, nsl=3):
        w = c1 - c0
        wsrc = now("v")
        for co in range(NCH):
            sl = wcnt[0] % nsl
            wcnt[0] += 1
            wb_ap = wbuf[sl] if sl < 2 else absF[:, 0:D]
            g0 = wguard[sl] if sl < 2 else wguard3[0]
            qn = "qw2" if sl == 2 else f"qw{sl}"
            on(WQE[sl], lambda e, co=co, wb_ap=wb_ap, Wd=Wd: e.dma_start(
                out=wb_ap.rearrange("p (ci o) -> p ci o", o=128),
                in_=Wd.rearrange("(ci p) o -> p ci o", p=128)[:, :, co * 128:(co + 1) * 128]),
               wait=[("t", g0)] + (SPLITG if sl == 2 else []), inc=qn, k=16)
            pname = "P_A" if co % 2 == 0 else "P_B"
            pa = PS[pname]
            wt = now(qn) + wsrc + [pfree[pname]]
            for ci in range(NCH):
                on("tensor", lambda e, ci=ci, co=co, wb_ap=wb_ap, pa=pa, w=w: e.matmul(
                    pa[:, 0:w], wb_ap[:, ci * 128:(ci + 1) * 128],
                    B_mix[:, ci * BLK + c0:ci * BLK + c1],
                    start=(ci == 0), stop=(ci == NCH - 1)),
                   wait=wt if ci == 0 else None, inc="t" if ci == NCH - 1 else None)
            if sl < 2:
                wguard[sl] = fl.cnt["t"]
            else:
                wguard3[0] = fl.cnt["t"]
            wte = now("t") + (evac_wait or [])
            if sigmoid:
                on("scalar", lambda e, co=co, pa=pa, w=w: e.activation(
                    dst[:, co * BLK + c0:co * BLK + c1], pa[:, 0:w], Act.Sigmoid),
                   wait=wte, inc="s")
            else:
                on("scalar", lambda e, co=co, pa=pa, w=w: e.copy(
                    dst[:, co * BLK + c0:co * BLK + c1], pa[:, 0:w]), wait=wte, inc="s")
            pfree[pname] = ("s", fl.cnt["s"])

    def stageB(b):
        rng = (384, 512) if b == 0 else (0, 512)
        wA = [("v", mark["pf_evac"])]
        nsl = 2
        mix_into_B(6, 0, BLK, wextra=wA + now("t"))
        proj_from_B(wkey_d, C_k, 0, BLK, nsl=nsl)
        mix_into_B(7, 0, BLK, wextra=now("t"))
        proj_from_B(wval_d, D_v, 0, BLK, nsl=nsl)
        if rng is not None:
            mix_into_B(8, rng[0], rng[1], wextra=now("t"))
        on("vector", lambda e: e.tensor_copy(bnd1[:, 0:NCH], A_x1T[:, BLK - 1::BLK]), inc="v")
        if b == 0:
            on("vector", lambda e: e.tensor_scalar(bnd1[:, 0:NCH], bnd1[:, 0:NCH],
                                                   mh[:, 0:1], None, Alu.mult),
               wait=now("v"), inc="v")
        mark["bnd1"] = fl.cnt["v"]
        if rng is not None:
            proj_from_B(wrec_d, A_x1T, rng[0], rng[1], sigmoid=True,
                        evac_wait=[("v", mark["bnd1"])], nsl=nsl)

    def stageS(b):
        rng = (384, 512) if b == 0 else (0, 512)
        wfree = [("t", wguard[0]), ("t", wguard[1])]
        ws_sr = now("s")
        if b == 0:
            on("vector", lambda e: e.memset(F_S[:, 0::BLK + 1], 0.0), inc="v")
            on("vector", lambda e: e.memset(G_Z[:, 0::BLK + 1], 0.0), inc="v")
        for c in range(NCH):
            f0 = c * (BLK + 1)
            on("scalar", lambda e, c=c: e.activation(
                B_mix[:, c * BLK:(c + 1) * BLK], C_k[:, c * BLK:(c + 1) * BLK], Act.Exp),
               wait=wfree if c == 0 else None, inc="s")
            ws = now("s")
            if b < 1:
                on("vector", lambda e, c=c: e.tensor_scalar(
                    B_mix[:, c * BLK:(c + 1) * BLK], B_mix[:, c * BLK:(c + 1) * BLK],
                    mh[:, 0:1], None, Alu.mult), wait=ws, inc="v")
                ws = None
            on("vector", lambda e, c=c: e.tensor_mul(
                D_v[:, c * BLK:(c + 1) * BLK], B_mix[:, c * BLK:(c + 1) * BLK],
                D_v[:, c * BLK:(c + 1) * BLK]), wait=ws, inc="v")
            if b > 0:
                on("vector", lambda e, f0=f0: e.tensor_copy(F_S[:, f0:f0 + 1],
                                                            F_S[:, f0 + BLK:f0 + BLK + 1]), inc="v")
                on("vector", lambda e, f0=f0: e.tensor_copy(G_Z[:, f0:f0 + 1],
                                                            G_Z[:, f0 + BLK:f0 + BLK + 1]), inc="v")
            initF = 0.0 if b == 0 else F_S[:, f0 + BLK:f0 + BLK + 1]
            initG = 0.0 if b == 0 else G_Z[:, f0 + BLK:f0 + BLK + 1]
            on("vector", lambda e, c=c, f0=f0, initF=initF: e.tensor_tensor_scan(
                F_S[:, f0 + 1:f0 + 1 + BLK],
                der(0, c).broadcast_to((128, BLK)),
                D_v[:, c * BLK:(c + 1) * BLK], initF, Alu.mult, Alu.add), inc="v")
            on("vector", lambda e, c=c, f0=f0, initG=initG: e.tensor_tensor_scan(
                G_Z[:, f0 + 1:f0 + 1 + BLK],
                der(0, c).broadcast_to((128, BLK)),
                B_mix[:, c * BLK:(c + 1) * BLK], initG, Alu.mult, Alu.add), inc="v")
            if rng is None:
                continue
            c0, c1 = rng
            on("vector", lambda e, c=c, f0=f0, c0=c0, c1=c1: e.scalar_tensor_tensor(
                D_v[:, c * BLK + c0:c * BLK + c1], D_v[:, c * BLK + c0:c * BLK + c1],
                der(1, c), F_S[:, f0 + c0:f0 + c1], Alu.mult, Alu.add), inc="v")
            on("vector", lambda e, c=c, f0=f0, c0=c0, c1=c1: e.scalar_tensor_tensor(
                B_mix[:, c * BLK + c0:c * BLK + c1], B_mix[:, c * BLK + c0:c * BLK + c1],
                der(1, c), G_Z[:, f0 + c0:f0 + c1], Alu.mult, Alu.add), inc="v")
            if b == 0:
                on("vector", lambda e, c=c, c0=c0, c1=c1: e.tensor_scalar(
                    B_mix[:, c * BLK + c0:c * BLK + c1], B_mix[:, c * BLK + c0:c * BLK + c1],
                    mhinv[:, 0:1], None, Alu.add), inc="v")
            on("vector", lambda e, c=c, c0=c0, c1=c1: e.reciprocal(
                C_k[:, c * BLK + c0:c * BLK + c1], B_mix[:, c * BLK + c0:c * BLK + c1]), inc="v")
            on("vector", lambda e, c=c, c0=c0, c1=c1: e.tensor_mul(
                D_v[:, c * BLK + c0:c * BLK + c1], D_v[:, c * BLK + c0:c * BLK + c1],
                C_k[:, c * BLK + c0:c * BLK + c1]), inc="v")
            on("vector", lambda e, c=c, c0=c0, c1=c1: e.tensor_mul(
                D_v[:, c * BLK + c0:c * BLK + c1], D_v[:, c * BLK + c0:c * BLK + c1],
                A_x1T[:, c * BLK + c0:c * BLK + c1]), wait=ws_sr if c == 0 else None, inc="v")
            mark[f"osr{c}"] = fl.cnt["v"]

    OPS = {0: [("P_E", 0), ("P_E", 512)], 1: [("P_F", 0), ("P_F", 512)],
           2: [("P_A", 0), ("P_B", 0)], 3: [("P_C", 0), ("P_D", 0)]}

    def stageO(b):
        rng = (384, 512) if b == 0 else (0, 512)
        jjs = [3] if b == 0 else [0, 1, 2, 3]
        wps = [pfree[p] for p in ("P_A", "P_B", "P_C", "P_D", "P_E", "P_F")]
        for ci in range(NCH):
            sl = wcnt2[0] % 2
            wcnt2[0] += 1
            wb_ap = wbuf2[sl] if sl < 2 else absF[:, D:2 * D]
            g0 = wguard2[sl] if sl < 2 else wguard4[0]
            qn = "qv2" if sl == 2 else f"qv{sl}"
            on(WQE[sl], lambda e, ci=ci, wb_ap=wb_ap: e.dma_start(
                out=wb_ap[:], in_=wout_d[ci * 128:(ci + 1) * 128, :]),
               wait=[("t", g0)] + (SPLITG if sl == 2 else []), inc=qn, k=16)
            wt = now(qn) + [("v", mark[f"osr{ci}"])] + (wps if ci == 0 else [])
            first = True
            for jj in jjs:
                for half in (0, 1):
                    pn, off = OPS[jj][half]
                    pa = PS[pn]
                    on("tensor", lambda e, ci=ci, jj=jj, half=half, pa=pa, off=off, wb_ap=wb_ap: e.matmul(
                        pa[:, off:off + 512], D_v[:, ci * BLK + jj * 128:ci * BLK + jj * 128 + 128],
                        wb_ap[:, half * 512:(half + 1) * 512],
                        start=(ci == 0), stop=(ci == NCH - 1)),
                       wait=wt if first else None,
                       inc="t" if (jj == jjs[-1] and half == 1) else None)
                    first = False
            if sl < 2:
                wguard2[sl] = fl.cnt["t"]
            else:
                wguard4[0] = fl.cnt["t"]
        wte = now("t")
        for jj in jjs:
            t = b * TPB + jj
            (pn0, off0), (pn1, off1) = OPS[jj]
            on("vector", lambda e, jj=jj, t=t: e.tensor_scalar(
                sgx[:], xt[jj][:], m0_all[:, t:t + 1], rsig0_all[:, t:t + 1],
                Alu.subtract, Alu.mult), wait=wte, inc="v")
            on("vector", lambda e, jj=jj, pn0=pn0, off0=off0: e.tensor_add(
                x2t[jj][:, 0:512], PS[pn0][:, off0:off0 + 512], sgx[:, 0:512]), inc="v")
            on("vector", lambda e, jj=jj, pn1=pn1, off1=off1: e.tensor_add(
                x2t[jj][:, 512:1024], PS[pn1][:, off1:off1 + 512], sgx[:, 512:1024]), inc="v")
            pfree[OPS[jj][0][0]] = ("v", fl.cnt["v"])
            pfree[OPS[jj][1][0]] = ("v", fl.cnt["v"])

    def ln_norm(src_ap, wslot, bslot, dst_cols, mask_last=False):
        on("scalar", lambda e: e.activation(u1t[:], src_ap, Act.Copy, accum_out=st(0)),
           wait=[("v", mark["tile_v"]), ("t", mark["tile_t"])] + now("v"), inc="s")
        on("scalar", lambda e: e.activation(u1t[:], src_ap, Act.Square, accum_out=st(1)), inc="s")
        on("scalar", lambda e: e.mul(st(2), st(0), R1024), inc="s")
        on("scalar", lambda e: e.mul(st(3), st(1), R1024), inc="s")
        on("scalar", lambda e: e.mul(st(4), st(2), st(2)), inc="s")
        on("scalar", lambda e: e.activation(st(5), st(4), Act.Identity,
                                            scale=-1.0, bias=st(3)), inc="s")
        on("scalar", lambda e: e.activation(st(6), st(5), Act.Ln, bias=EPS), inc="s")
        on("scalar", lambda e: e.activation(st(7), st(6), Act.Exp, scale=-0.5), inc="s")
        ws = now("s")
        on("vector", lambda e: e.tensor_scalar(xnt[:], src_ap, st(2), st(7),
                                               Alu.subtract, Alu.mult), wait=ws, inc="v")
        mark["tile_v"] = fl.cnt["v"]
        wv = now("v") + [("v", mark["pf_evac"])] + [pfree["P_F"]]
        for c in range(NCH):
            on("tensor", lambda e, c=c: e.matmul(
                P_F[:, c * 128:(c + 1) * 128], xnt[:, c * 128:(c + 1) * 128], idn[:],
                is_transpose=True, start=(c % 4 == 0), stop=(c % 4 == 3)),
               wait=wv if c == 0 else None, inc="t" if c == NCH - 1 else None)
        mark["tile_t"] = fl.cnt["t"]
        wt = now("t")
        for c in range(NCH):
            on("vector", lambda e, c=c: e.tensor_scalar(
                C_k[:, c * BLK + dst_cols[0]: c * BLK + dst_cols[1]],
                P_F[:, c * 128:(c + 1) * 128], pv(wslot, c), pv(bslot, c),
                Alu.mult, Alu.add), wait=wt if c == 0 else None, inc="v")
        mark["pf_evac"] = fl.cnt["v"]
        pfree["P_F"] = ("v", fl.cnt["v"])
        if mask_last:
            on("vector", lambda e: None, wait=now("v"))
            for c in range(NCH):
                on("vector", lambda e, c=c: e.tensor_scalar(
                    C_k[:, c * BLK + 511:c * BLK + 512], C_k[:, c * BLK + 511:c * BLK + 512],
                    mh[:, 0:1], None, Alu.mult), inc="v")

    def stageL2(b):
        jjs = [3] if b == 0 else [0, 1, 2, 3]
        for jj in jjs:
            ln_norm(x2t[jj][:], 2, 3, (jj * 128, jj * 128 + 128),
                    mask_last=(b == 0 and jj == 3))
        if b == 0:
            on("vector", lambda e: e.tensor_copy(bnd2[:, 0:NCH], C_k[:, BLK - 1::BLK]),
               wait=now("v"), inc="v")

    def stageF(b):
        wgF3k[0] = max(wgF3k[0], fl.cnt["t"])
        wgF3v[0] = max(wgF3v[0], fl.cnt["t"])
        mix_into_B(10, 0, BLK, wextra=now("t"), src=C_k, bnd=bnd2)
        wsrc = now("v")
        for ci in range(NCH):
            sl = wcnt2[0] % 2
            wcnt2[0] += 1
            wb_ap = wbuf2[sl] if sl < 2 else absF[:, D:2 * D]
            g0 = wguard2[sl] if sl < 2 else wguard4[0]
            qn = "qv2" if sl == 2 else f"qv{sl}"
            on(WQE[sl], lambda e, ci=ci, wb_ap=wb_ap: e.dma_start(
                out=wb_ap[:], in_=fwrec_d[ci * 128:(ci + 1) * 128, :]),
               wait=[("t", g0)] + (SPLITG if sl == 2 else []), inc=qn, k=16)
            wt = now(qn) + wsrc + ([pfree[p] for p in ("P_A", "P_B", "P_C", "P_D", "P_E", "P_F")] if ci == 0 else [])
            first = True
            for jj in range(TPB):
                for half in (0, 1):
                    pn, off = OPS[jj][half]
                    on("tensor", lambda e, ci=ci, jj=jj, half=half, pn=pn, off=off, wb_ap=wb_ap: e.matmul(
                        PS[pn][:, off:off + 512], B_mix[:, ci * BLK + jj * 128:ci * BLK + jj * 128 + 128],
                        wb_ap[:, half * 512:(half + 1) * 512],
                        start=(ci == 0), stop=(ci == NCH - 1)),
                       wait=wt if first else None, inc="t" if (jj == TPB - 1 and half == 1) else None)
                    first = False
            if sl < 2:
                wguard2[sl] = fl.cnt["t"]
            else:
                wguard4[0] = fl.cnt["t"]
        wte = now("t")
        for jj in range(TPB):
            (pn0, off0), (pn1, off1) = OPS[jj]
            on("scalar", lambda e, jj=jj, pn0=pn0, off0=off0: e.activation(
                frs[jj][:, 0:512], PS[pn0][:, off0:off0 + 512], Act.Sigmoid),
               wait=wte, inc="s")
            on("scalar", lambda e, jj=jj, pn1=pn1, off1=off1: e.activation(
                frs[jj][:, 512:1024], PS[pn1][:, off1:off1 + 512], Act.Sigmoid), inc="s")
            pfree[OPS[jj][0][0]] = ("s", fl.cnt["s"])
            pfree[OPS[jj][1][0]] = ("s", fl.cnt["s"])
        mix_into_B(9, 0, BLK, wextra=now("t"), src=C_k, bnd=bnd2)
        on("vector", lambda e: e.tensor_copy(bnd2[:, 0:NCH], C_k[:, BLK - 1::BLK]), inc="v")
        mark["vC"] = fl.cnt["v"]
        u1tB = u1t.bitcast(BF16)
        SCR = {0: (sgx[:, 0:256], sgx[:, 256:512], sgxB, 1024),
               1: (sgx[:, 768:1024], u1t[:, 0:256], u1tB, 512)}
        NCO = DH // 128
        for hh in (0, 1):
            h0 = hh * 256
            wb = now("t", "v") + SPLITG
            for c in range(NCH):
                on("scalar", lambda e, c=c, h0=h0: e.copy(
                    absB[:, c * 256:(c + 1) * 256], B_mix[:, c * BLK + h0:c * BLK + h0 + 256]),
                   wait=wb, inc="s")
                on("vector", lambda e, c=c, h0=h0: e.tensor_tensor(
                    absB[:, DM + c * 256:DM + (c + 1) * 256],
                    B_mix[:, c * BLK + h0:c * BLK + h0 + 256],
                    absB[:, c * 256:(c + 1) * 256], Alu.subtract),
                   wait=now("s"), inc="v")
            ws_x = now("s", "v")
            tfk = {}
            pm = {}

            def emit_fk(co):
                sl = wcnt[0] % 3
                wcnt[0] += 1
                wb_ap = wbufB[sl] if sl < 2 else C_B[:, 0:2 * D]
                g0 = wguard[sl] if sl < 2 else wgF3k[0]
                qn = "qw2" if sl == 2 else f"qw{sl}"
                gw = [("t", g0)] + SPLITG + ([("v", mark["vC"])] if sl == 2 else [])
                on(WQE[sl], lambda e, co=co, wb_ap=wb_ap: e.dma_start(
                    out=wb_ap[:, 0:2 * D].rearrange("p (ci o) -> p ci o", o=256),
                    in_=fwkeyHL_d.rearrange("(ci p) o -> p ci o", p=128)[:, :, co * 256:(co + 1) * 256]),
                   wait=gw, inc=qn, k=16)
                p = co % 2
                bk = P_A if p == 0 else P_B
                wt = now(qn) + (ws_x if co < 2 else []) \
                    + [pm.get(co - 2, pfree["P_A" if p == 0 else "P_B"])]
                first = True
                for ci in range(NCH):
                    for wo, ao in ((0, 0), (0, DM), (128, 0)):
                        on("tensor", lambda e, ci=ci, wb_ap=wb_ap, bk=bk, wo=wo, ao=ao: e.matmul(
                            bk[:, 0:256], wb_ap[:, ci * 256 + wo:ci * 256 + wo + 128],
                            absB[:, ao + ci * 256:ao + ci * 256 + 256],
                            start=(ci == 0 and wo == 0 and ao == 0),
                            stop=(ci == NCH - 1 and wo == 128)),
                           wait=wt if first else None,
                           inc="t" if (ci == NCH - 1 and wo == 128) else None)
                        first = False
                if sl < 2:
                    wguard[sl] = fl.cnt["t"]
                else:
                    wgF3k[0] = fl.cnt["t"]
                tfk[co] = fl.cnt["t"]

            def emit_chain_vk(co):
                p = co % 2
                R, M, PB, base = SCR[p]
                bk = P_A if p == 0 else P_B
                on("scalar", lambda e, R=R, bk=bk: e.activation(R, bk[:, 0:256], Act.Relu),
                   wait=[("t", tfk[co]), ("t", mark.get(f"fkscr{p}", 0))], inc="s")
                on("vector", lambda e, R=R, M=M, bk=bk: e.tensor_mul(M, R, bk[:, 0:256]),
                   wait=now("s"), inc="v")
                pm[co] = ("v", fl.cnt["v"])
                on("scalar", lambda e, M=M, PB=PB, base=base: e.copy(PB[:, base:base + 256], M),
                   wait=now("v"), inc="s")
                on("vector", lambda e, M=M, PB=PB, base=base: e.tensor_tensor(
                    PB[:, base + 256:base + 512], M, PB[:, base:base + 256], Alu.subtract),
                   wait=now("s"), inc="v")
                wv = now("v")
                sl2 = wcnt2[0] % 3
                wcnt2[0] += 1
                wb2_ap = wbuf2B[sl2] if sl2 < 2 else C_B[:, 2 * D:4 * D]
                g0 = wguard2[sl2] if sl2 < 2 else wgF3v[0]
                qn = "qv2" if sl2 == 2 else f"qv{sl2}"
                gw = [("t", g0)] + SPLITG + ([("v", mark["vC"])] if sl2 == 2 else [])
                on(WQE[sl2], lambda e, co=co, wb2_ap=wb2_ap: e.dma_start(
                    out=wb2_ap[:, 0:2 * D], in_=fwvalHL_d[co * 128:(co + 1) * 128, :]),
                   wait=gw, inc=qn, k=16)
                first = True
                for jl in (0, 1):
                    pvk = P_E if jl == 0 else P_F
                    for half in (0, 1):
                        for which, wo in ((0, 0), (0, D), (1, 0)):
                            on("tensor", lambda e, jl=jl, half=half, pvk=pvk, wb2_ap=wb2_ap,
                               which=which, wo=wo, PB=PB, base=base: e.matmul(
                                pvk[:, half * 512:(half + 1) * 512],
                                PB[:, base + which * 256 + jl * 128:base + which * 256 + jl * 128 + 128],
                                wb2_ap[:, wo + half * 512:wo + half * 512 + 512],
                                start=(co == 0 and which == 0 and wo == 0),
                                stop=(co == NCO - 1 and which == 1)),
                               wait=(wv + now(qn) + ([pfree["P_E"], pfree["P_F"]] if co == 0 else [])) if first else None,
                               inc="t" if (jl == 1 and half == 1 and which == 1) else None)
                            first = False
                if sl2 < 2:
                    wguard2[sl2] = fl.cnt["t"]
                else:
                    wgF3v[0] = fl.cnt["t"]
                mark[f"fkscr{p}"] = fl.cnt["t"]

            for co in range(NCO):
                emit_fk(co)
                if co >= 1:
                    emit_chain_vk(co - 1)
            emit_chain_vk(NCO - 1)
            pfree["P_A"], pfree["P_B"] = pm[NCO - 2], pm[NCO - 1]
            wte = now("t")
            for jl in (0, 1):
                jj = hh * 2 + jl
                pvk = P_E if jl == 0 else P_F
                on("vector", lambda e, jj=jj, pvk=pvk: e.tensor_mul(
                    u1t[:], frs[jj][:], pvk[:]), wait=wte, inc="v")
                on("vector", lambda e, jj=jj: e.tensor_add(
                    x2t[jj][:], u1t[:], x2t[jj][:]), wait=now("v"), inc="v")
                pfree["P_E" if jl == 0 else "P_F"] = ("v", fl.cnt["v"])

    def gates_combine(b):
        def slot(i):
            return rsc[:, i * 8:(i + 1) * 8]

        def vsw():
            on("vector", lambda e: None, wait=now("v"))

        def ssw():
            on("scalar", lambda e: None, wait=now("s"))
        for jj in range(TPB):
            t = b * TPB + jj
            to = (b - 1) * TPB + jj
            on("vector", lambda e, to=to, t=t: e.tensor_scalar(
                slot(0), z_own[:, to * 8:(to + 1) * 8], rsig0_all[:, t:t + 1], None,
                Alu.mult), wait=now("s"), inc="v")
            vsw()
            on("vector", lambda e: e.tensor_reduce(gs(0), slot(0), mybir.AxisListType.X,
                                                   Alu.max, negate=True), inc="v")
            wv = now("v")
            on("scalar", lambda e: e.activation(slot(1), slot(0), Act.Exp, bias=gs(0),
                                                accum_out=gs(1)), wait=wv, inc="s")
            ws = now("s")
            on("vector", lambda e: e.reciprocal(gs(2), gs(1)), wait=ws, inc="v")
            wv = now("v")
            on("scalar", lambda e: e.mul(slot(2), slot(1), gs(2)), wait=wv, inc="s")
            ws = now("s")
            on("vector", lambda e, to=to: e.tensor_mul(slot(3), oh1k_own[:, to * 8:(to + 1) * 8],
                                                       slot(2)), wait=ws, inc="v")
            vsw()
            on("vector", lambda e: e.tensor_reduce(gs(3), slot(3), mybir.AxisListType.X, Alu.add), inc="v")
            vsw()
            on("vector", lambda e, to=to: e.tensor_mul(slot(4), oh2k_own[:, to * 8:(to + 1) * 8],
                                                       slot(2)), inc="v")
            vsw()
            on("vector", lambda e: e.tensor_reduce(gs(4), slot(4), mybir.AxisListType.X, Alu.add), inc="v")
            wv = now("v")
            on("scalar", lambda e: e.activation(gs(5), gs(3), Act.Identity, bias=gs(4)),
               wait=wv, inc="s")
            ssw()
            on("scalar", lambda e: e.activation(gs(6), gs(5), Act.Identity, bias=1e-6), inc="s")
            ws = now("s")
            on("vector", lambda e: e.reciprocal(gs(7), gs(6)), wait=ws, inc="v")
            vsw()
            on("vector", lambda e: e.tensor_scalar(gs(8), gs(3), gs(7), None, Alu.mult), inc="v")
            vsw()
            on("vector", lambda e: e.tensor_scalar(gs(9), gs(4), gs(7), None, Alu.mult), inc="v")
            wv = now("v")
            on("scalar", lambda e, to=to: e.mul(slot(5), oh1k_own[:, to * 8:(to + 1) * 8], gs(8)),
               wait=wv, inc="s")
            ssw()
            on("scalar", lambda e, to=to: e.mul(slot(6), oh2k_own[:, to * 8:(to + 1) * 8], gs(9)), inc="s")
            ws = now("s")
            on("vector", lambda e, to=to: e.tensor_add(cmb[:, to * 8:(to + 1) * 8],
                                                       slot(5), slot(6)), wait=ws, inc="v")
            vsw()

    def stageM(b):
        wguard3[0] = max(wguard3[0], fl.cnt["t"])
        absB3 = absB[:, 0:2 * D]
        gates_combine(b)
        for hh in (0, 1):
            for jl in (0, 1):
                jj = hh * 2 + jl
                t = b * TPB + jj
                on("vector", lambda e, jj=jj, t=t: e.tensor_scalar(
                    xnt[:], xt[jj][:], m0_all[:, t:t + 1], rsig0_all[:, t:t + 1],
                    Alu.subtract, Alu.mult), wait=[("s", mark.get("xnt_used_s", 0))], inc="v")
                wv = now("v") + [pfree["P_F"]]
                for c in range(NCH):
                    on("tensor", lambda e, c=c: e.matmul(
                        P_F[:, c * 128:(c + 1) * 128], xnt[:, c * 128:(c + 1) * 128], idn[:],
                        is_transpose=True, start=(c % 4 == 0), stop=(c % 4 == 3)),
                       wait=wv if c == 0 else None, inc="t" if c == NCH - 1 else None)
                wt = now("t")
                for c in range(NCH):
                    on("scalar", lambda e, c=c, jl=jl: e.copy(
                        XNTs[:, c * 256 + jl * 128:c * 256 + jl * 128 + 128],
                        P_F[:, c * 128:(c + 1) * 128]), wait=wt if c == 0 else None, inc="s")
                mark["xnt_used_s"] = fl.cnt["s"]
                pfree["P_F"] = ("s", fl.cnt["s"])
            ws_x = now("s")
            for c in range(NCH):
                on("scalar", lambda e, c=c: e.copy(
                    A_B[:, 4096 + c * 256:4096 + (c + 1) * 256], XNTs[:, c * 256:(c + 1) * 256]),
                   wait=ws_x, inc="s")
                on("vector", lambda e, c=c: e.tensor_tensor(
                    A_B[:, 6144 + c * 256:6144 + (c + 1) * 256], XNTs[:, c * 256:(c + 1) * 256],
                    A_B[:, 4096 + c * 256:4096 + (c + 1) * 256], Alu.subtract),
                   wait=now("s"), inc="v")
            ws_x = now("s", "v")
            for mo in range(DM // 128):
                sl = mo % 3
                wb_ap = wbufB[sl] if sl < 2 else absB3
                g0 = wguard[sl] if sl < 2 else wguard3[0]
                on(WQE[sl], lambda e, mo=mo, wb_ap=wb_ap: e.dma_start(
                    out=wb_ap[:, 0:2 * D].rearrange("p (ci o) -> p ci o", o=256),
                    in_=wgateHL_d.rearrange("(ci p) o -> p ci o", p=128)[:, :, mo * 256:(mo + 1) * 256]),
                   wait=[("t", g0)] + SPLITG, inc=("qw2" if sl == 2 else f"qw{sl}"), k=16)
                pn = "P_A" if mo % 2 == 0 else "P_B"
                wt = now("qw2" if sl == 2 else f"qw{sl}") + ws_x + [pfree[pn]]
                first = True
                for ci in range(NCH):
                    for wo, ao in ((0, 4096), (0, 6144), (128, 4096)):
                        on("tensor", lambda e, ci=ci, wb_ap=wb_ap, pn=pn, wo=wo, ao=ao: e.matmul(
                            PS[pn][:, 0:256], wb_ap[:, ci * 256 + wo:ci * 256 + wo + 128],
                            A_B[:, ao + ci * 256:ao + ci * 256 + 256],
                            start=(ci == 0 and wo == 0 and ao == 4096),
                            stop=(ci == NCH - 1 and wo == 128)),
                           wait=wt if first else None,
                           inc="t" if (ci == NCH - 1 and wo == 128) else None)
                        first = False
                if sl < 2:
                    wguard[sl] = fl.cnt["t"]
                else:
                    wguard3[0] = fl.cnt["t"]
                wte = now("t")
                on("scalar", lambda e, pn=pn: e.activation(sgx[:, 0:256], PS[pn][:, 0:256],
                                                           Act.Sigmoid),
                   wait=wte + [("v", mark.get("sgm_used", 0))], inc="s")
                on("vector", lambda e, pn=pn: e.tensor_mul(
                    sgx[:, 256:512], sgx[:, 0:256], PS[pn][:, 0:256]),
                   wait=now("s"), inc="v")
                pfree[pn] = ("v", fl.cnt["v"])
                on("scalar", lambda e, mo=mo: e.copy(
                    C_B[:, mo * 256:(mo + 1) * 256], sgx[:, 256:512]), wait=now("v"), inc="s")
                on("vector", lambda e, mo=mo: e.tensor_tensor(
                    C_B[:, 4096 + mo * 256:4096 + (mo + 1) * 256], sgx[:, 256:512],
                    C_B[:, mo * 256:(mo + 1) * 256], Alu.subtract),
                   wait=now("s"), inc="v")
                mark["sgm_used"] = fl.cnt["v"]
                mark[f"hms{mo}"] = fl.cnt["v"]
            DB = {0: [["P_E", "P_E"], ["P_F", "P_F"]],
                  1: [["P_A", "P_B"], ["P_C", "P_D"]]}

            def dslice(p, jl, half):
                nm = DB[p][jl][half]
                off = half * 512 if p == 0 else 0
                return PS[nm][:, off:off + 512]
            for e_ in range(NE):
                p = e_ % 2
                bset = ["P_E", "P_F"] if p == 0 else ["P_A", "P_B", "P_C", "P_D"]
                for mo in range(DM // 128):
                    k = e_ * (DM // 128) + mo
                    sl = k % 3
                    wb_ap = wbuf2B[sl] if sl < 2 else absB3
                    g0 = wguard2[sl] if sl < 2 else wguard3[0]
                    on(WQE[sl], lambda e, e_=e_, mo=mo, wb_ap=wb_ap: e.dma_start(
                        out=wb_ap[:, 0:2 * D], in_=wdownHL_d[e_, mo * 128:(mo + 1) * 128, :]),
                       wait=[("t", g0)] + SPLITG, inc=("qv2" if sl == 2 else f"qv{sl}"), k=16)
                    wt = now("qv2" if sl == 2 else f"qv{sl}") + [("v", mark[f"hms{mo}"])] \
                        + ([pfree[nm] for nm in bset] if mo == 0 else [])
                    first = True
                    for jl in (0, 1):
                        for half in (0, 1):
                            for ho, wo in ((0, 0), (0, D), (4096, 0)):
                                on("tensor", lambda e, jl=jl, half=half, p=p, wb_ap=wb_ap, mo=mo, ho=ho, wo=wo: e.matmul(
                                    dslice(p, jl, half),
                                    C_B[:, ho + mo * 256 + jl * 128:ho + mo * 256 + jl * 128 + 128],
                                    wb_ap[:, wo + half * 512:wo + half * 512 + 512],
                                    start=(mo == 0 and ho == 0 and wo == 0),
                                    stop=(mo == DM // 128 - 1 and ho == 4096)),
                                   wait=wt if first else None,
                                   inc="t" if (jl == 1 and half == 1 and ho == 4096) else None)
                                first = False
                    if sl < 2:
                        wguard2[sl] = fl.cnt["t"]
                    else:
                        wguard3[0] = fl.cnt["t"]
                wte = now("t")
                for jl in (0, 1):
                    jj = hh * 2 + jl
                    to = (b - 1) * TPB + jj
                    cc = cmb[:, to * 8 + e_:to * 8 + e_ + 1]
                    if p == 0:
                        pvk = P_E if jl == 0 else P_F
                        if e_ == 0:
                            on("vector", lambda e, jj=jj, pvk=pvk, cc=cc: e.tensor_scalar(
                                yt[jj], pvk[:], cc, None, Alu.mult), wait=wte, inc="v")
                        else:
                            on("vector", lambda e, jj=jj, pvk=pvk, cc=cc: e.scalar_tensor_tensor(
                                yt[jj], pvk[:], cc, yt[jj], Alu.mult, Alu.add), wait=wte, inc="v")
                        pfree[DB[0][jl][0]] = ("v", fl.cnt["v"])
                    else:
                        for half in (0, 1):
                            on("vector", lambda e, jj=jj, jl=jl, half=half, cc=cc: e.scalar_tensor_tensor(
                                yt[jj][:, half * 512:(half + 1) * 512], dslice(1, jl, half), cc,
                                yt[jj][:, half * 512:(half + 1) * 512], Alu.mult, Alu.add),
                               wait=wte if (jl == 0 and half == 0) else None, inc="v")
                            pfree[DB[1][jl][half]] = ("v", fl.cnt["v"])
            for jl in (0, 1):
                jj = hh * 2 + jl
                wv = now("v")
                on("scalar", lambda e, jj=jj: e.activation(sgx[:], yt[jj], Act.Sigmoid),
                   wait=wv + [("v", mark.get("sgm_used", 0))], inc="s")
                ws = now("s")
                on("vector", lambda e, jj=jj: e.tensor_mul(yt[jj], sgx[:], yt[jj]),
                   wait=ws, inc="v")
                mark["sgm_used"] = fl.cnt["v"]
                on("vector", lambda e, jj=jj: e.tensor_mul(yt[jj], yt[jj], x2t[jj][:]), inc="v")
                on("vector", lambda e, jj=jj: e.tensor_add(yt[jj], yt[jj], xt[jj][:]), inc="v")
                wv = now("v")
                row0 = (b - 1) * BLK + jj * 128
                on("sync", lambda e, jj=jj, row0=row0: e.dma_start(
                    out=y_out[row0:row0 + 128, :], in_=yt[jj]), wait=wv, inc=f"qo{jj}", k=16)

    for b in range(NBLK):
        stageA(b)
        stageB(b)
        if DEV_STOP == "B":
            mark["x_users"] = fl.cnt["v"]
            continue
        stageS(b)
        if DEV_STOP == "S":
            mark["x_users"] = fl.cnt["v"]
            continue
        stageO(b)
        if DEV_STOP != "O":
            stageL2(b)
        if b >= 1 and DEV_STOP is None:
            stageF(b)
            stageM(b)
            on("vector", lambda e: None, wait=now("qo0", "qo1", "qo2", "qo3"))
        mark["x_users"] = fl.cnt["v"]





def _host_prep(inputs):
    x = np.ascontiguousarray(np.asarray(inputs["L0_attn_x"], np.float32).reshape(NTOK, D))
    pvec = np.stack([
        np.asarray(inputs["L0_attn_ln1_w"], np.float32),
        np.asarray(inputs["L0_attn_ln1_b"], np.float32),
        np.asarray(inputs["L0_attn_ln2_w"], np.float32),
        np.asarray(inputs["L0_attn_ln2_b"], np.float32),
        np.asarray(inputs["L0_attn_time_decay"], np.float32),
        np.asarray(inputs["L0_attn_time_first"], np.float32),
        np.asarray(inputs["L0_attn_time_mix_k"], np.float32).reshape(D),
        np.asarray(inputs["L0_attn_time_mix_v"], np.float32).reshape(D),
        np.asarray(inputs["L0_attn_time_mix_r"], np.float32).reshape(D),
        np.asarray(inputs["L0_attn_f_time_mix_k"], np.float32).reshape(D),
        np.asarray(inputs["L0_attn_f_time_mix_r"], np.float32).reshape(D),
        np.zeros(D, np.float32),
    ], axis=0)
    pvec = pvec.reshape(12, NCH, 128).transpose(2, 0, 1).reshape(128, 12 * NCH)
    shared = {
        "xall": x,
        "idn": np.eye(128, dtype=np.float32),
        "triu": np.triu(np.ones((128, 128), np.float32)),
        "onesc": np.ones((128, 1), np.float32),
        "onesr": np.ones((1, 128), np.float32),
        "wkey": np.ascontiguousarray(np.asarray(inputs["L0_attn_w_key"], np.float32)),
        "wval": np.ascontiguousarray(np.asarray(inputs["L0_attn_w_value"], np.float32)),
        "wrec": np.ascontiguousarray(np.asarray(inputs["L0_attn_w_recept"], np.float32)),
        "wout": np.ascontiguousarray(np.asarray(inputs["L0_attn_w_output"], np.float32)),
        "fwkey": np.ascontiguousarray(np.asarray(inputs["L0_attn_f_w_key"], np.float32)),
        "fwrec": np.ascontiguousarray(np.asarray(inputs["L0_attn_f_w_recept"], np.float32)),
        "fwval": np.ascontiguousarray(np.asarray(inputs["L0_attn_f_w_value"], np.float32)),
        "wgateT": np.ascontiguousarray(np.asarray(inputs["L0_ffn_w_gate"], np.float32).T),
        "wdownT": np.ascontiguousarray(np.asarray(inputs["L0_ffn_w_down"], np.float32).transpose(0, 2, 1)),
        "wrouter": np.ascontiguousarray(np.asarray(inputs["L0_ffn_w_router"], np.float32)),
        "pvec": np.ascontiguousarray(pvec),
        "fconst": np.tile(np.array(FCONSTS, np.float32), (128, 1)),
    }
    in_maps = []
    for c in range(NC):
        q = c % 4
        gs = c * OWN
        if q == 0:
            xph = np.concatenate([np.zeros((NH, D), np.float32), x[gs:gs + OWN]], axis=0)
            m = 0.0
        else:
            xph = x[gs - NH:gs + OWN]
            m = 1.0
        im = dict(shared)
        im["xph"] = np.ascontiguousarray(xph)
        im["mh"] = np.full((128, 1), m, np.float32)
        im["mhinv"] = np.full((128, 1), 1.0 - m, np.float32)
        im["rank"] = np.full((1, 1), c, np.int32)
        gs_ = np.zeros((128, 8), np.float32)
        gs_[:, c] = 1.0
        im["gsel"] = gs_
        in_maps.append(im)
    return in_maps


def kernel(**inputs):
    in_maps = _host_prep(inputs)
    nc, ctx = build_program(taps=TAPS)
    try:
        res = run_bass_kernel_spmd(nc, in_maps, core_ids=list(range(NC)))
    finally:
        ctx.close()
    parts = [res.results[c]["y"] for c in range(NC)]
    out = np.concatenate(parts, axis=0).reshape(NB, T, D).astype(np.float32)
    if TAPS:
        kernel.last_results = res.results
    return out


# version 20
# speedup vs baseline: 1.0205x; 1.0205x over previous
import numpy as np
import concourse.bass as bass
import concourse.mybir as mybir
from concourse.bass_utils import run_bass_kernel_spmd
from contextlib import ExitStack

F32 = mybir.dt.float32
I32 = mybir.dt.int32
Act = mybir.ActivationFunctionType
Alu = mybir.AluOpType

D = 1024
NCH = 8
DH = 4096
DM = 2048
NE = 8
CAP = 2560.0
NB = 2
T = 8192
NTOK = NB * T
NC = 8
OWN = 2048
NH = 512
NT = NH + OWN
BLK = 512
NBLK = NT // BLK
TPB = BLK // 128
NTILE = NT // 128
GT = NTOK // 128
EPS = 1e-5
R1024 = 1.0 / 1024.0

ENGINES = ("sync", "gpsimd", "tensor", "vector", "scalar")
FCONSTS = (1e-5, -0.5, -1.0, 1.0 / 1024.0, 1.0, 2560.5, 1e-6, -1e30)

TAPS = ()
DEV_SKIP = ()
DEV_STOP = None


class Flow:

    def __init__(self, nc, ctx, semnames):
        self.nc = nc
        self.prog = []
        self.sem = {}
        self.cnt = {}
        self.lastw = {e: {} for e in ENGINES}
        for n in semnames:
            self.sem[n] = ctx.enter_context(nc.semaphore(n))
            self.cnt[n] = 0

    def now(self, *names):
        return [(n, self.cnt[n]) for n in names]

    AUTOSELF = {"vector": "v", "scalar": "s"}

    def on(self, eng, fn, inc=None, k=1, wait=None):
        wait = list(wait or [])
        aself = self.AUTOSELF.get(eng)
        if aself is not None:
            wait.append((aself, self.cnt[aself]))
        w = []
        for sname, val in wait:
            if val > self.lastw[eng].get(sname, 0):
                w.append((sname, val))
                self.lastw[eng][sname] = val
        if inc is not None:
            self.cnt[inc] += k
        sems = self.sem

        def run(e, _fn=fn, _w=w, _inc=inc, _k=k):
            for sname, val in _w:
                e.wait_ge(sems[sname], val)
            ins = _fn(e)
            if _inc is not None:
                assert ins is not None
                ins.then_inc(sems[_inc], _k)
        self.prog.append((eng, run))

    def barrier_from(self, eng, *sems):
        self.on(eng, lambda e: None, wait=self.now(*sems))


def build_program(taps=()):
    nc = bass.Bass("TRN2", target_bir_lowering=False, debug=False)
    ctx = ExitStack()

    def dram_in(name, shape, dt=F32):
        return nc.dram_tensor(name, shape, dt, kind="ExternalInput").ap()

    def dram_out(name, shape, dt=F32):
        return nc.dram_tensor(name, shape, dt, kind="ExternalOutput").ap()

    def sb(name, shape, dt=F32):
        return ctx.enter_context(nc.sbuf_tensor(name, shape, dt)).ap()

    def ps(name, shape, dt=F32):
        return ctx.enter_context(nc.psum_tensor(name, shape, dt)).ap()

    xall = dram_in("xall", [NTOK, D])
    xph = dram_in("xph", [NT, D])
    mh_d = dram_in("mh", [128, 1])
    mhinv_d = dram_in("mhinv", [128, 1])
    rank_d = dram_in("rank", [1, 1], I32)
    gsel_d = dram_in("gsel", [128, 8])
    idn_d = dram_in("idn", [128, 128])
    triu_d = dram_in("triu", [128, 128])
    onesc_d = dram_in("onesc", [128, 1])
    onesr_d = dram_in("onesr", [1, 128])
    fconst_d = dram_in("fconst", [128, 8])
    wkey_d = dram_in("wkey", [D, D])
    wval_d = dram_in("wval", [D, D])
    wrec_d = dram_in("wrec", [D, D])
    wout_d = dram_in("wout", [D, D])
    fwkey_d = dram_in("fwkey", [D, DH])
    fwrec_d = dram_in("fwrec", [D, D])
    fwval_d = dram_in("fwval", [DH, D])
    wgateT_d = dram_in("wgateT", [D, DM])
    wdownT_d = dram_in("wdownT", [NE, DM, D])
    wrouter_d = dram_in("wrouter", [NE, D])
    pvec_d = dram_in("pvec", [128, 12 * NCH])

    y_out = dram_out("y", [OWN, D])
    BF16 = mybir.dt.bfloat16
    fwkeyHL_d = nc.dram_tensor("fwkeyHL", [D, 2 * DH], BF16, kind="Internal").ap()
    fwvalHL_d = nc.dram_tensor("fwvalHL", [DH, 2 * D], BF16, kind="Internal").ap()
    wgateHL_d = nc.dram_tensor("wgateHL", [D, 2 * DM], BF16, kind="Internal").ap()
    wdownHL_d = nc.dram_tensor("wdownHL", [NE, DM, 2 * D], BF16, kind="Internal").ap()

    idn = sb("idn_sb", [128, 128])
    triu = sb("triu_sb", [128, 128])
    onesc = sb("onesc_sb", [128, 1])
    onesr = sb("onesr_sb", [1, 128])
    mh = sb("mh_s", [128, 1])
    mhinv = sb("mhinv_s", [128, 1])
    rank_s = sb("rank_s", [1, 1], I32)
    gsel = sb("gsel_s", [128, 8])
    pvec = sb("pvec_s", [128, 12 * NCH])
    fconst = sb("fconst_s", [128, 8])
    for _i, _v in enumerate(FCONSTS):
        nc.const_aps.register(F32, _v, fconst[:, _i:_i + 1])
    wder = sb("wder", [128, 3 * NCH])
    negbig = sb("negbig", [128, 8])
    wr_raw = sb("wr_raw", [NE, D])
    wr_sum = sb("wr_sum", [NE, 1])
    wrcT = sb("wrcT", [128, NCH * NE])

    z_all = sb("z_all", [128, GT * NE])
    oh1k_all = sb("oh1k_all", [128, GT * NE], BF16)
    oh2k_all = sb("oh2k_all", [128, GT * NE], BF16)
    offrow = sb("offrow", [1, NE])
    z_own = sb("z_own", [128, 16 * NE])
    oh1k_own = sb("oh1k_own", [128, 16 * NE])
    oh2k_own = sb("oh2k_own", [128, 16 * NE])
    rxt = [sb(f"rxt{i}", [128, D]) for i in range(2)]
    rxT = [sb(f"rxT{i}", [128, D]) for i in range(2)]
    rsc = sb("rsc", [128, 64])
    rt_s = sb("rt_s", [128, 5 * 128])
    cbuf = sb("cbuf", [1, 136])
    gsel_d2 = None

    rsig0_all = sb("rsig0_all", [128, NTILE])
    m0_all = sb("m0_all", [128, NTILE])
    xt = [sb(f"xt{i}", [128, D]) for i in range(TPB)]
    xnt = sb("xnt", [128, D])
    xnt2 = sb("xnt2", [128, D])
    u1t = sb("u1t", [128, D])
    u1t2 = sb("u1t2", [128, D])
    stat = sb("stat", [128, 32])
    A_x1T = sb("A_x1T", [128, NCH * BLK])
    B_mix = sb("B_mix", [128, NCH * BLK])
    C_k = sb("C_k", [128, NCH * BLK])
    D_v = sb("D_v", [128, NCH * BLK])
    F_S = sb("F_S", [128, NCH * (BLK + 1)])
    G_Z = sb("G_Z", [128, NCH * (BLK + 1)])
    x2t = rxt + rxT
    bnd1 = sb("bnd1", [128, NCH])
    bnd2 = sb("bnd2", [128, NCH])
    XNTs = A_x1T[:, 0:NCH * 256]
    Hms = C_k[:, 0:16 * 256]

    wbuf = [sb(f"wbuf{i}", [128, D]) for i in range(2)]
    wbuf2 = [sb(f"wbuf2_{i}", [128, D]) for i in range(2)]
    wbufB = [w.bitcast(BF16) for w in wbuf]
    wbuf2B = [w.bitcast(BF16) for w in wbuf2]
    absF = sb("absF", [128, DM])
    absB = absF.bitcast(BF16)
    cnt_sb = sb("cnt_sb", [1, 1040])
    cnt2_sb = sb("cnt2_sb", [1, 1040])

    sgx = sb("sgx", [128, D])
    frs = [D_v[:, i * D:(i + 1) * D] for i in range(TPB)]
    yt = [B_mix[:, i * D:(i + 1) * D] for i in range(TPB)]
    cmb = sb("cmb", [128, 16 * NE])
    gsc = sb("gsc", [128, 16])

    P_A = ps("P_A", [128, 512])
    P_B = ps("P_B", [128, 512])
    P_C = ps("P_C", [128, 512])
    P_D = ps("P_D", [128, 512])
    P_E = ps("P_E", [128, 1024])
    P_F = ps("P_F", [128, 1024])

    fl = Flow(nc, ctx, ["qc", "qr0", "qr1", "qx0", "qx1", "qx2", "qx3",
                        "qw0", "qw1", "qv0", "qv1", "qo0", "qo1", "qo2", "qo3",
                        "qsi0", "qsi1", "qsi2", "qsi3", "qso0", "qso1", "qw2", "qv2", "t", "v", "s"])
    on, now = fl.on, fl.now

    def dma(e, dst, src):
        return e.dma_start(out=dst, in_=src)

    for dst, src in [(fconst, fconst_d), (gsel, gsel_d), (idn, idn_d), (triu, triu_d), (onesc, onesc_d),
                     (onesr, onesr_d), (mh, mh_d), (mhinv, mhinv_d),
                     (rank_s, rank_d), (wr_raw, wrouter_d)]:
        on("sync", lambda e, d=dst, s=src: dma(e, d, s), inc="qc", k=16)
    on("sync", lambda e: dma(e, pvec[:, 0:12 * NCH], pvec_d[:]), inc="qc", k=16)

    def pv(p, c):
        return pvec[:, p * NCH + c: p * NCH + c + 1]

    def der(i, c):
        return wder[:, i * NCH + c: i * NCH + c + 1]

    w_qx = now("qc")
    for c in range(NCH):
        on("scalar", lambda e, c=c: e.activation(der(2, c), pv(4, c), Act.Exp), wait=w_qx, inc="s")
        on("scalar", lambda e, c=c: e.activation(der(1, c), pv(5, c), Act.Exp), inc="s")
    on("scalar", lambda e: None, wait=now("s"))
    for c in range(NCH):
        on("scalar", lambda e, c=c: e.activation(der(0, c), der(2, c), Act.Exp, scale=-1.0), inc="s")
    on("vector", lambda e: e.memset(negbig[:], -1e30), inc="v")

    on("vector", lambda e: e.reduce_sum(wr_sum[:], wr_raw[:], axis=mybir.AxisListType.X), wait=w_qx, inc="v")
    on("scalar", lambda e: e.mul(wr_sum[:], wr_sum[:], R1024), wait=now("v"), inc="s")
    on("vector", lambda e: e.tensor_scalar(wr_raw[:], wr_raw[:], wr_sum[:, 0:1], None, Alu.subtract),
       wait=now("s"), inc="v")
    w_v = now("v", "qc")
    for c in range(NCH):
        on("tensor", lambda e, c=c: e.matmul(
            P_A[:, c * 8:(c + 1) * 8], wr_raw[:, c * 128:(c + 1) * 128], idn[0:NE, 0:NE],
            is_transpose=True, start=(c == 0), stop=(c == NCH - 1)),
           wait=w_v if c == 0 else None, inc="t" if c == NCH - 1 else None)
    on("scalar", lambda e: e.copy(wrcT[:, 0:NCH * NE], P_A[:, 0:NCH * NE]), wait=now("t"), inc="s")

    skip_routing = "routing" in DEV_SKIP
    GB = 16
    NG = GT // GB
    def rslot(i):
        return rt_s[:, i * 128:(i + 1) * 128]

    def rview(i):
        return rt_s[:, i * 128:(i + 1) * 128].rearrange("p (t e) -> p t e", e=NE)

    t_after_tr = {}
    s_after_ev = {}
    v_after_zev = {}

    def vsw():
        on("vector", lambda e: None, wait=now("v"))

    def _colhl(hl_ap, rs, cc, which):
        v = hl_ap[rs, cc * 2048:(cc + 1) * 2048].rearrange("p (co o) -> p co o", o=256)
        return v[:, :, 0:128] if which == 0 else v[:, :, 128:256]

    split_pieces = []
    for r in range(8):
        rs = slice(r * 128, (r + 1) * 128)
        for cc in range(4):
            sl = (rs, slice(cc * 1024, (cc + 1) * 1024))
            split_pieces.append((fwkey_d[sl], _colhl(fwkeyHL_d, rs, cc, 0),
                                 _colhl(fwkeyHL_d, rs, cc, 1)))
    for r in range(32):
        rs = slice(r * 128, (r + 1) * 128)
        split_pieces.append((fwval_d[rs, 0:D], fwvalHL_d[rs, 0:D], fwvalHL_d[rs, D:2 * D]))
    for r in range(8):
        rs = slice(r * 128, (r + 1) * 128)
        for cc in range(2):
            sl = (rs, slice(cc * 1024, (cc + 1) * 1024))
            split_pieces.append((wgateT_d[sl], _colhl(wgateHL_d, rs, cc, 0),
                                 _colhl(wgateHL_d, rs, cc, 1)))
    for e_ in range(NE):
        for mo in range(16):
            ms = slice(mo * 128, (mo + 1) * 128)
            split_pieces.append((wdownT_d[e_, ms, 0:D], wdownHL_d[e_, ms, 0:D],
                                 wdownHL_d[e_, ms, D:2 * D]))
    split_state = {"i": 0}
    SPL = 4

    def _issue_in(j, w):
        sl = j % SPL
        srcp = split_pieces[j][0]
        on("gpsimd", lambda e, sl=sl, srcp=srcp: e.dma_start(
            out=D_v[:, sl * D:(sl + 1) * D], in_=srcp),
           wait=w, inc=f"qsi{sl}", k=16)

    def emit_splits(n):
        for _ in range(n):
            i = split_state["i"]
            if i >= len(split_pieces):
                return
            split_state["i"] = i + 1
            if i == 0:
                for j in range(min(SPL, len(split_pieces))):
                    _issue_in(j, None)
            sl = i % SPL
            db = i % 2
            a0 = db * 2 * D
            _, hdst, ldst = split_pieces[i]
            on("scalar", lambda e, sl=sl, a0=a0: e.copy(absB[:, a0:a0 + D],
                                                        D_v[:, sl * D:(sl + 1) * D]),
               wait=now(f"qsi{sl}", f"qso{db}"), inc="s")
            on("vector", lambda e, sl=sl, a0=a0: e.tensor_tensor(
                absB[:, a0 + D:a0 + 2 * D], D_v[:, sl * D:(sl + 1) * D],
                absB[:, a0:a0 + D], Alu.subtract),
               wait=now("s"), inc="v")
            wv = now("v")
            on("gpsimd", lambda e, hdst=hdst, a0=a0: e.dma_start(out=hdst, in_=absB[:, a0:a0 + D]),
               wait=wv, inc=f"qso{db}", k=16)
            on("gpsimd", lambda e, ldst=ldst, a0=a0: e.dma_start(out=ldst, in_=absB[:, a0 + D:a0 + 2 * D]),
               inc=f"qso{db}", k=16)
            if i + SPL < len(split_pieces):
                _issue_in(i + SPL, wv)

    for g in range(GT if not skip_routing else 0):
        gi, ti = g // GB, g % GB
        db = g % 2
        on("sync" if db == 0 else "scalar",
           lambda e, g=g, db=db: dma(e, rxt[db][:], xall[g * 128:(g + 1) * 128, :]),
           wait=[("t", t_after_tr[g - 2])] if g >= 2 else None, inc=f"qr{db}", k=16)
        PT = P_E if g % 2 == 0 else P_F
        wt = now(f"qr{db}") + ([("s", s_after_ev[g - 2])] if g >= 2
                               else (now("qc") if g == 0 else []))
        for c in range(NCH):
            on("tensor", lambda e, c=c, db=db, PT=PT: e.matmul(
                PT[:, c * 128:(c + 1) * 128], rxt[db][:, c * 128:(c + 1) * 128], idn[:],
                is_transpose=True,
                start=(c % 4 == 0), stop=(c % 4 == 3)),
               wait=wt if c == 0 else None, inc="t" if c == NCH - 1 else None)
        t_after_tr[g] = fl.cnt["t"]
        wt = now("t")
        on("scalar", lambda e, db=db, PT=PT: e.copy(rxT[db][:, 0:512], PT[:, 0:512]), wait=wt, inc="s")
        on("scalar", lambda e, db=db, PT=PT: e.copy(rxT[db][:, 512:1024], PT[:, 512:1024]), inc="s")
        s_after_ev[g] = fl.cnt["s"]
        zb = P_A if gi % 2 == 0 else P_B
        ws = now("s") + ([("v", v_after_zev.get(gi - 2, 0))] if ti == 0 else [])
        for c in range(NCH):
            on("tensor", lambda e, c=c, db=db, ti=ti, zb=zb: e.matmul(
                zb[:, ti * NE:(ti + 1) * NE], rxT[db][:, c * 128:(c + 1) * 128],
                wrcT[:, c * 8:(c + 1) * 8],
                start=(ti == 0 and c == 0), stop=(ti == GB - 1 and c == NCH - 1)),
               wait=ws if c == 0 else None,
               inc="t" if (ti == GB - 1 and c == NCH - 1) else None)
        if ti == GB - 1:
            on("vector", lambda e, gi=gi, zb=zb: e.tensor_copy(
                z_all[:, gi * 128:(gi + 1) * 128], zb[:, 0:GB * NE]), wait=now("t"), inc="v")
            v_after_zev[gi] = fl.cnt["v"]
        emit_splits(2)

    if not skip_routing:
        O1 = A_x1T[:, 0:1024]
        S2 = A_x1T[:, 1024:2048]
        S3 = A_x1T[:, 2048:3072]
        ZM = A_x1T[:, 3072:4096]
        O2 = B_mix[:, 0:1024]
        SP = B_mix[:, 1024:2048]

        def wv3(ap):
            return ap.rearrange("p (t e) -> p t e", e=NE)

        def top1_wide(src, dst):
            vsw()
            on("vector", lambda e, src=src: e.tensor_reduce(
                rt_s[:, 0:GT], wv3(src), mybir.AxisListType.X, Alu.max), inc="v")
            vsw()
            on("vector", lambda e, src=src, dst=dst: e.tensor_tensor(
                wv3(dst), wv3(src),
                rt_s[:, 0:GT].unsqueeze(2).broadcast_to((128, GT, NE)), Alu.is_ge), inc="v")
            vsw()
            on("vector", lambda e, dst=dst: e.tensor_copy(S2, dst), inc="v")
            vsw()
            on("vector", lambda e, dst=dst: e.tensor_add(
                wv3(S2)[:, :, 1:8], wv3(dst)[:, :, 1:8], wv3(dst)[:, :, 0:7]), inc="v")
            vsw()
            on("vector", lambda e: e.tensor_copy(S3, S2), inc="v")
            vsw()
            on("vector", lambda e: e.tensor_add(
                wv3(S3)[:, :, 2:8], wv3(S2)[:, :, 2:8], wv3(S2)[:, :, 0:6]), inc="v")
            vsw()
            on("vector", lambda e: e.tensor_copy(S2, S3), inc="v")
            vsw()
            on("vector", lambda e: e.tensor_add(
                wv3(S2)[:, :, 4:8], wv3(S3)[:, :, 4:8], wv3(S3)[:, :, 0:4]), inc="v")
            vsw()
            on("vector", lambda e: e.tensor_single_scalar(S2, S2, 1.0, Alu.is_equal), inc="v")
            vsw()
            on("vector", lambda e, dst=dst: e.tensor_mul(dst, dst, S2), inc="v")

        top1_wide(z_all[:, 0:GT * NE], O1)
        vsw()
        on("vector", lambda e: e.scalar_tensor_tensor(
            ZM, O1, -1e30, z_all[:, 0:GT * NE], Alu.mult, Alu.add), inc="v")
        top1_wide(ZM, O2)
        vsw()
        on("vector", lambda e: e.tensor_add(S3, O1, O2), inc="v")
        wv = now("v")
        on("tensor", lambda e: e.matmul(P_C[0:1, 0:512], onesc[:], S3[:, 0:512],
                                        start=True, stop=True), wait=wv, inc="t")
        on("tensor", lambda e: e.matmul(P_D[0:1, 0:512], onesc[:], S3[:, 512:1024],
                                        start=True, stop=True), inc="t")
        wt = now("t")
        on("vector", lambda e: e.memset(cnt_sb[0:1, 0:8], 0.0), inc="v")
        on("vector", lambda e: e.tensor_copy(cnt_sb[0:1, 8:520], P_C[0:1, 0:512]), wait=wt, inc="v")
        on("vector", lambda e: e.tensor_copy(cnt_sb[0:1, 520:1032], P_D[0:1, 0:512]), inc="v")
        pp = [cnt_sb, cnt2_sb]
        for k, stp in enumerate((8, 16, 32, 64, 128, 256, 512)):
            sb_, db_ = pp[k % 2], pp[1 - k % 2]
            vsw()
            on("vector", lambda e, sb_=sb_, db_=db_, stp=stp: e.tensor_copy(
                db_[0:1, 0:8 + stp], sb_[0:1, 0:8 + stp]), inc="v")
            on("vector", lambda e, sb_=sb_, db_=db_, stp=stp: e.tensor_add(
                db_[0:1, 8 + stp:1032], sb_[0:1, 8 + stp:1032],
                sb_[0:1, 8:1032 - stp]), inc="v")
        cntF = pp[7 % 2]
        vsw()
        wv = now("v")
        firstb = {0: True, 1: True}
        for t in range(GT):
            bi = 0 if t < 64 else 1
            bk = P_A if bi == 0 else P_B
            col0 = (t % 64) * 8
            on("tensor", lambda e, t=t, bk=bk, col0=col0: e.matmul(
                bk[:, col0:col0 + 8], triu[:], S3[:, t * NE:(t + 1) * NE],
                start=(t % 64 == 0), stop=False),
               wait=wv if firstb[bi] else None)
            firstb[bi] = False
            on("tensor", lambda e, t=t, bk=bk, col0=col0: e.matmul(
                bk[:, col0:col0 + 8], onesr[:], cntF[0:1, t * NE:(t + 1) * NE],
                start=False, stop=(t % 64 == 63)),
               inc="t" if t % 64 == 63 else None)
        wt = now("t")
        on("vector", lambda e: e.tensor_single_scalar(
            S2[:, 0:512], P_A[:, 0:512], CAP + 0.5, Alu.is_le), wait=wt, inc="v")
        on("vector", lambda e: e.tensor_single_scalar(
            S2[:, 512:1024], P_B[:, 0:512], CAP + 0.5, Alu.is_le), inc="v")
        vsw()
        on("vector", lambda e: e.tensor_mul(O1, O1, S2), inc="v")
        on("vector", lambda e: e.tensor_mul(O2, O2, S2), inc="v")
        vsw()
        on("vector", lambda e: e.tensor_copy(oh1k_all[:, 0:GT * NE], O1), inc="v")
        on("vector", lambda e: e.tensor_copy(oh2k_all[:, 0:GT * NE], O2), inc="v")
        for srcb, dstb in ((z_all, z_own), (O1, oh1k_own), (O2, oh2k_own)):
            vsw()
            for gi in range(NG):
                on("vector", lambda e, srcb=srcb, gi=gi: e.tensor_scalar(
                    SP[:, gi * 128:(gi + 1) * 128], srcb[:, gi * 128:(gi + 1) * 128],
                    gsel[:, gi:gi + 1], None, Alu.mult), inc="v")
            vsw()
            on("vector", lambda e: e.tensor_add(ZM[:, 0:512], SP[:, 0:512], SP[:, 512:1024]), inc="v")
            vsw()
            on("vector", lambda e: e.tensor_add(ZM[:, 512:768], ZM[:, 0:256], ZM[:, 256:512]), inc="v")
            vsw()
            on("vector", lambda e, dstb=dstb: e.tensor_add(
                dstb[:, 0:128], ZM[:, 512:640], ZM[:, 640:768]), inc="v")

    emit_splits(10 ** 9)
    SPLITG = now("s", "v", "qso0", "qso1")

    _emit_blocks(nc, fl, locals())

    for tname in taps:
        if ":" in tname:
            nm, idx = tname.split(":")
            buf = locals()[nm][int(idx)]
        else:
            buf = locals()[tname]
        td = dram_out("tap_" + tname.replace(":", "_"), list(buf.shape), buf.dtype)
        on("sync", lambda e, b=buf, t=td: dma(e, t[:], b[:]),
           wait=now("v", "s", "t", "qw0", "qw1", "qv0", "qv1"), inc="qo0", k=16)

    on("sync", lambda e: None, wait=now("qo0", "qo1", "qo2", "qo3"))

    with nc.Block() as block:
        @block.sync
        def _(e):
            for eng, run in fl.prog:
                if eng == "sync":
                    run(e)

        @block.gpsimd
        def _(e):
            for eng, run in fl.prog:
                if eng == "gpsimd":
                    run(e)

        @block.tensor
        def _(e):
            for eng, run in fl.prog:
                if eng == "tensor":
                    run(e)

        @block.vector
        def _(e):
            for eng, run in fl.prog:
                if eng == "vector":
                    run(e)

        @block.scalar
        def _(e):
            for eng, run in fl.prog:
                if eng == "scalar":
                    run(e)

    return nc, ctx


def _emit_blocks(nc, fl, L):
    on, now = fl.on, fl.now
    g = lambda n: L[n]
    (xph, xt, xnt, u1t, stat, A_x1T, B_mix, C_k, D_v, F_S, G_Z, x2t, bnd1, bnd2,
     rsig0_all, m0_all, pvec, wder, mh, mhinv, idn, y_out) = (
        g("xph"), g("xt"), g("xnt"), g("u1t"), g("stat"), g("A_x1T"), g("B_mix"),
        g("C_k"), g("D_v"), g("F_S"), g("G_Z"), g("x2t"), g("bnd1"), g("bnd2"),
        g("rsig0_all"), g("m0_all"), g("pvec"), g("wder"), g("mh"), g("mhinv"),
        g("idn"), g("y_out"))
    P_A, P_B, P_C, P_D, P_E, P_F = g("P_A"), g("P_B"), g("P_C"), g("P_D"), g("P_E"), g("P_F")
    xnt2, u1t2 = g("xnt2"), g("u1t2")

    def pv(p, c):
        return pvec[:, p * NCH + c: p * NCH + c + 1]

    def der(i, c):
        return wder[:, i * NCH + c: i * NCH + c + 1]

    def st(i):
        return stat[:, i:i + 1]

    on("vector", lambda e: e.memset(bnd1[:], 0.0), inc="v")
    on("vector", lambda e: e.memset(bnd2[:], 0.0), inc="v")
    on("vector", lambda e: e.memset(stat[:], 0.0), inc="v")
    mark0_v = fl.cnt["v"]

    mark = {"x_users": 0, "pe_evac": fl.cnt["s"],
            "pf_evac0": fl.cnt["v"], "pf_evac1": fl.cnt["v"],
            "tile_v0": mark0_v, "tile_v1": mark0_v, "tile_t0": 0, "tile_t1": 0}

    def stageA(b):
        for tl in range(TPB):
            t = b * TPB + tl
            p = tl % 2
            xn_p = xnt if p == 0 else xnt2
            u1_p = u1t if p == 0 else u1t2
            PT = P_F if p == 0 else P_E
            pfn = "P_F" if p == 0 else "P_E"
            sto = p * 16

            def stp(i, sto=sto):
                return stat[:, sto + i:sto + i + 1]
            wprev = [("v", mark["x_users"])] if b > 0 else None
            on("sync", lambda e, t=t, tl=tl: e.dma_start(
                out=xt[tl][:], in_=xph[t * 128:(t + 1) * 128, :]),
               wait=wprev, inc=f"qx{tl}", k=16)
            wq = now(f"qx{tl}", "qc") + [("v", mark[f"tile_v{p}"]), ("t", mark[f"tile_t{p}"])]
            on("scalar", lambda e, tl=tl, u1_p=u1_p, stp=stp: e.activation(
                u1_p[:], xt[tl][:], Act.Copy, accum_out=stp(0)), wait=wq, inc="s")
            on("scalar", lambda e, tl=tl, u1_p=u1_p, stp=stp: e.activation(
                u1_p[:], xt[tl][:], Act.Square, accum_out=stp(1)), inc="s")
            on("scalar", lambda e, t=t, stp=stp: e.mul(m0_all[:, t:t + 1], stp(0), R1024), inc="s")
            on("scalar", lambda e, stp=stp: e.mul(stp(3), stp(1), R1024), inc="s")
            on("scalar", lambda e, t=t, stp=stp: e.mul(stp(4), m0_all[:, t:t + 1],
                                                       m0_all[:, t:t + 1]), inc="s")
            on("scalar", lambda e, stp=stp: e.activation(stp(5), stp(4), Act.Identity,
                                                         scale=-1.0, bias=stp(3)), inc="s")
            on("scalar", lambda e, stp=stp: e.activation(stp(6), stp(5), Act.Ln, bias=EPS), inc="s")
            on("scalar", lambda e, t=t, stp=stp: e.activation(rsig0_all[:, t:t + 1], stp(6), Act.Exp,
                                                              scale=-0.5), inc="s")
            ws = now("s")
            on("vector", lambda e, tl=tl, t=t, xn_p=xn_p: e.tensor_scalar(
                xn_p[:], xt[tl][:], m0_all[:, t:t + 1], rsig0_all[:, t:t + 1],
                Alu.subtract, Alu.mult), wait=ws, inc="v")
            wv = now("v")
            on("scalar", lambda e, u1_p=u1_p, xn_p=xn_p, stp=stp: e.activation(
                u1_p[:], xn_p[:], Act.Copy, accum_out=stp(8)), wait=wv, inc="s")
            on("scalar", lambda e, u1_p=u1_p, xn_p=xn_p, stp=stp: e.activation(
                u1_p[:], xn_p[:], Act.Square, accum_out=stp(9)), inc="s")
            on("scalar", lambda e, stp=stp: e.mul(stp(10), stp(8), R1024), inc="s")
            on("scalar", lambda e, stp=stp: e.mul(stp(11), stp(9), R1024), inc="s")
            on("scalar", lambda e, stp=stp: e.mul(stp(12), stp(10), stp(10)), inc="s")
            on("scalar", lambda e, stp=stp: e.activation(stp(13), stp(12), Act.Identity,
                                                         scale=-1.0, bias=stp(11)), inc="s")
            on("scalar", lambda e, stp=stp: e.activation(stp(14), stp(13), Act.Ln, bias=EPS), inc="s")
            on("scalar", lambda e, stp=stp: e.activation(stp(15), stp(14), Act.Exp, scale=-0.5), inc="s")
            ws = now("s")
            on("vector", lambda e, u1_p=u1_p, xn_p=xn_p, stp=stp: e.tensor_scalar(
                u1_p[:], xn_p[:], stp(10), stp(15), Alu.subtract, Alu.mult), wait=ws, inc="v")
            mark[f"tile_v{p}"] = fl.cnt["v"]
            wv = now("v") + [("v", mark[f"pf_evac{p}"]), pfree[pfn]]
            for c in range(NCH):
                on("tensor", lambda e, c=c, u1_p=u1_p, PT=PT: e.matmul(
                    PT[:, c * 128:(c + 1) * 128], u1_p[:, c * 128:(c + 1) * 128], idn[:],
                    is_transpose=True, start=(c % 4 == 0), stop=(c % 4 == 3)),
                   wait=wv if c == 0 else None, inc="t" if c == NCH - 1 else None)
            mark[f"tile_t{p}"] = fl.cnt["t"]
            wt = now("t")
            for c in range(NCH):
                on("vector", lambda e, c=c, tl=tl, PT=PT: e.tensor_scalar(
                    A_x1T[:, c * BLK + tl * 128: c * BLK + tl * 128 + 128],
                    PT[:, c * 128:(c + 1) * 128], pv(0, c), pv(1, c),
                    Alu.mult, Alu.add), wait=wt if c == 0 else None, inc="v")
            mark[f"pf_evac{p}"] = fl.cnt["v"]
            pfree[pfn] = ("v", fl.cnt["v"])

    wkey_d, wval_d, wrec_d, wout_d = g("wkey_d"), g("wval_d"), g("wrec_d"), g("wout_d")
    fwkey_d, fwrec_d, fwval_d = g("fwkey_d"), g("fwrec_d"), g("fwval_d")
    wgateT_d, wdownT_d = g("wgateT_d"), g("wdownT_d")
    wbuf, wbuf2 = g("wbuf"), g("wbuf2")
    sgx, frs, yt = g("sgx"), g("frs"), g("yt")
    XNTs, Hms = g("XNTs"), g("Hms")
    z_own, oh1k_own, oh2k_own, cmb, gsc, rsc = (g("z_own"), g("oh1k_own"),
                                                g("oh2k_own"), g("cmb"), g("gsc"), g("rsc"))
    wbufB, wbuf2B, absB, SPLITG = g("wbufB"), g("wbuf2B"), g("absB"), g("SPLITG")
    absF = g("absF")
    fwkeyHL_d, fwvalHL_d, wgateHL_d, wdownHL_d = (
        g("fwkeyHL_d"), g("fwvalHL_d"), g("wgateHL_d"), g("wdownHL_d"))
    BF16 = mybir.dt.bfloat16
    A_B = A_x1T.bitcast(BF16)
    C_B = C_k.bitcast(BF16)
    sgxB = sgx.bitcast(BF16)

    pfree = {n: ("v", 0) for n in ("P_A", "P_B", "P_C", "P_D", "P_E", "P_F")}
    PS = {"P_A": P_A, "P_B": P_B, "P_C": P_C, "P_D": P_D, "P_E": P_E, "P_F": P_F}
    wguard = {0: 0, 1: 0}
    wguard2 = {0: 0, 1: 0}
    wguard3 = [0]
    wguard4 = [0]
    wgF3k = [0]
    wgF3v = [0]
    WQE = {0: "gpsimd", 1: "sync", 2: "scalar"}
    wcnt = [0]
    wcnt2 = [0]

    def gs(i):
        return gsc[:, i:i + 1]

    def mix_into_B(pidx, c0, c1, wextra=None, src=None, bnd=None):
        src = A_x1T if src is None else src
        bnd = bnd1 if bnd is None else bnd
        for c in range(NCH):
            b0 = c * BLK
            on("vector", lambda e, c=c, b0=b0: e.tensor_sub(
                B_mix[:, b0 + max(c0, 1):b0 + c1], src[:, b0 + max(c0, 1):b0 + c1],
                src[:, b0 + max(c0, 1) - 1:b0 + c1 - 1]),
               wait=wextra if c == 0 else None, inc="v")
        if c0 == 0:
            for c in range(NCH):
                b0 = c * BLK
                on("vector", lambda e, c=c, b0=b0: e.tensor_sub(
                    B_mix[:, b0:b0 + 1], src[:, b0:b0 + 1], bnd[:, c:c + 1]), inc="v")
        for c in range(NCH):
            b0 = c * BLK
            on("vector", lambda e, c=c, b0=b0: e.scalar_tensor_tensor(
                B_mix[:, b0 + max(c0, 1):b0 + c1], B_mix[:, b0 + max(c0, 1):b0 + c1],
                pv(pidx, c), src[:, b0 + max(c0, 1) - 1:b0 + c1 - 1],
                Alu.mult, Alu.add), inc="v")
        if c0 == 0:
            on("vector", lambda e: None, wait=now("v"))
            for c in range(NCH):
                b0 = c * BLK
                on("vector", lambda e, c=c, b0=b0: e.scalar_tensor_tensor(
                    B_mix[:, b0:b0 + 1], B_mix[:, b0:b0 + 1], pv(pidx, c),
                    bnd[:, c:c + 1], Alu.mult, Alu.add), inc="v")

    def proj_from_B(Wd, dst, c0, c1, sigmoid=False, evac_wait=None, nsl=3):
        w = c1 - c0
        wsrc = now("v")
        for co in range(NCH):
            sl = wcnt[0] % nsl
            wcnt[0] += 1
            wb_ap = wbuf[sl] if sl < 2 else absF[:, 0:D]
            g0 = wguard[sl] if sl < 2 else wguard3[0]
            qn = "qw2" if sl == 2 else f"qw{sl}"
            on(WQE[sl], lambda e, co=co, wb_ap=wb_ap, Wd=Wd: e.dma_start(
                out=wb_ap.rearrange("p (ci o) -> p ci o", o=128),
                in_=Wd.rearrange("(ci p) o -> p ci o", p=128)[:, :, co * 128:(co + 1) * 128]),
               wait=[("t", g0)] + (SPLITG if sl == 2 else []), inc=qn, k=16)
            pname = "P_A" if co % 2 == 0 else "P_B"
            pa = PS[pname]
            wt = now(qn) + wsrc + [pfree[pname]]
            for ci in range(NCH):
                on("tensor", lambda e, ci=ci, co=co, wb_ap=wb_ap, pa=pa, w=w: e.matmul(
                    pa[:, 0:w], wb_ap[:, ci * 128:(ci + 1) * 128],
                    B_mix[:, ci * BLK + c0:ci * BLK + c1],
                    start=(ci == 0), stop=(ci == NCH - 1)),
                   wait=wt if ci == 0 else None, inc="t" if ci == NCH - 1 else None)
            if sl < 2:
                wguard[sl] = fl.cnt["t"]
            else:
                wguard3[0] = fl.cnt["t"]
            wte = now("t") + (evac_wait or [])
            if sigmoid:
                on("scalar", lambda e, co=co, pa=pa, w=w: e.activation(
                    dst[:, co * BLK + c0:co * BLK + c1], pa[:, 0:w], Act.Sigmoid),
                   wait=wte, inc="s")
            else:
                on("scalar", lambda e, co=co, pa=pa, w=w: e.copy(
                    dst[:, co * BLK + c0:co * BLK + c1], pa[:, 0:w]), wait=wte, inc="s")
            pfree[pname] = ("s", fl.cnt["s"])

    def stageB(b):
        rng = (384, 512) if b == 0 else (0, 512)
        wA = [("v", mark["pf_evac0"]), ("v", mark["pf_evac1"])]
        nsl = 2
        mix_into_B(6, 0, BLK, wextra=wA + now("t"))
        proj_from_B(wkey_d, C_k, 0, BLK, nsl=nsl)
        mix_into_B(7, 0, BLK, wextra=now("t"))
        proj_from_B(wval_d, D_v, 0, BLK, nsl=nsl)
        if rng is not None:
            mix_into_B(8, rng[0], rng[1], wextra=now("t"))
        on("vector", lambda e: e.tensor_copy(bnd1[:, 0:NCH], A_x1T[:, BLK - 1::BLK]), inc="v")
        if b == 0:
            on("vector", lambda e: e.tensor_scalar(bnd1[:, 0:NCH], bnd1[:, 0:NCH],
                                                   mh[:, 0:1], None, Alu.mult),
               wait=now("v"), inc="v")
        mark["bnd1"] = fl.cnt["v"]
        if rng is not None:
            proj_from_B(wrec_d, A_x1T, rng[0], rng[1], sigmoid=True,
                        evac_wait=[("v", mark["bnd1"])], nsl=nsl)

    def stageS(b):
        rng = (384, 512) if b == 0 else (0, 512)
        wfree = [("t", wguard[0]), ("t", wguard[1])]
        ws_sr = now("s")
        if b == 0:
            on("vector", lambda e: e.memset(F_S[:, 0::BLK + 1], 0.0), inc="v")
            on("vector", lambda e: e.memset(G_Z[:, 0::BLK + 1], 0.0), inc="v")
        for c in range(NCH):
            f0 = c * (BLK + 1)
            on("scalar", lambda e, c=c: e.activation(
                B_mix[:, c * BLK:(c + 1) * BLK], C_k[:, c * BLK:(c + 1) * BLK], Act.Exp),
               wait=wfree if c == 0 else None, inc="s")
            ws = now("s")
            if b < 1:
                on("vector", lambda e, c=c: e.tensor_scalar(
                    B_mix[:, c * BLK:(c + 1) * BLK], B_mix[:, c * BLK:(c + 1) * BLK],
                    mh[:, 0:1], None, Alu.mult), wait=ws, inc="v")
                ws = None
            on("vector", lambda e, c=c: e.tensor_mul(
                D_v[:, c * BLK:(c + 1) * BLK], B_mix[:, c * BLK:(c + 1) * BLK],
                D_v[:, c * BLK:(c + 1) * BLK]), wait=ws, inc="v")
            if b > 0:
                on("vector", lambda e, f0=f0: e.tensor_copy(F_S[:, f0:f0 + 1],
                                                            F_S[:, f0 + BLK:f0 + BLK + 1]), inc="v")
                on("vector", lambda e, f0=f0: e.tensor_copy(G_Z[:, f0:f0 + 1],
                                                            G_Z[:, f0 + BLK:f0 + BLK + 1]), inc="v")
            initF = 0.0 if b == 0 else F_S[:, f0 + BLK:f0 + BLK + 1]
            initG = 0.0 if b == 0 else G_Z[:, f0 + BLK:f0 + BLK + 1]
            on("vector", lambda e, c=c, f0=f0, initF=initF: e.tensor_tensor_scan(
                F_S[:, f0 + 1:f0 + 1 + BLK],
                der(0, c).broadcast_to((128, BLK)),
                D_v[:, c * BLK:(c + 1) * BLK], initF, Alu.mult, Alu.add), inc="v")
            on("vector", lambda e, c=c, f0=f0, initG=initG: e.tensor_tensor_scan(
                G_Z[:, f0 + 1:f0 + 1 + BLK],
                der(0, c).broadcast_to((128, BLK)),
                B_mix[:, c * BLK:(c + 1) * BLK], initG, Alu.mult, Alu.add), inc="v")
            if rng is None:
                continue
            c0, c1 = rng
            on("vector", lambda e, c=c, f0=f0, c0=c0, c1=c1: e.scalar_tensor_tensor(
                D_v[:, c * BLK + c0:c * BLK + c1], D_v[:, c * BLK + c0:c * BLK + c1],
                der(1, c), F_S[:, f0 + c0:f0 + c1], Alu.mult, Alu.add), inc="v")
            on("vector", lambda e, c=c, f0=f0, c0=c0, c1=c1: e.scalar_tensor_tensor(
                B_mix[:, c * BLK + c0:c * BLK + c1], B_mix[:, c * BLK + c0:c * BLK + c1],
                der(1, c), G_Z[:, f0 + c0:f0 + c1], Alu.mult, Alu.add), inc="v")
            if b == 0:
                on("vector", lambda e, c=c, c0=c0, c1=c1: e.tensor_scalar(
                    B_mix[:, c * BLK + c0:c * BLK + c1], B_mix[:, c * BLK + c0:c * BLK + c1],
                    mhinv[:, 0:1], None, Alu.add), inc="v")
            on("vector", lambda e, c=c, c0=c0, c1=c1: e.reciprocal(
                C_k[:, c * BLK + c0:c * BLK + c1], B_mix[:, c * BLK + c0:c * BLK + c1]), inc="v")
            on("vector", lambda e, c=c, c0=c0, c1=c1: e.tensor_mul(
                D_v[:, c * BLK + c0:c * BLK + c1], D_v[:, c * BLK + c0:c * BLK + c1],
                C_k[:, c * BLK + c0:c * BLK + c1]), inc="v")
            on("vector", lambda e, c=c, c0=c0, c1=c1: e.tensor_mul(
                D_v[:, c * BLK + c0:c * BLK + c1], D_v[:, c * BLK + c0:c * BLK + c1],
                A_x1T[:, c * BLK + c0:c * BLK + c1]), wait=ws_sr if c == 0 else None, inc="v")
            mark[f"osr{c}"] = fl.cnt["v"]

    OPS = {0: [("P_E", 0), ("P_E", 512)], 1: [("P_F", 0), ("P_F", 512)],
           2: [("P_A", 0), ("P_B", 0)], 3: [("P_C", 0), ("P_D", 0)]}

    def stageO(b):
        rng = (384, 512) if b == 0 else (0, 512)
        jjs = [3] if b == 0 else [0, 1, 2, 3]
        wps = [pfree[p] for p in ("P_A", "P_B", "P_C", "P_D", "P_E", "P_F")]
        for ci in range(NCH):
            sl = wcnt2[0] % 2
            wcnt2[0] += 1
            wb_ap = wbuf2[sl] if sl < 2 else absF[:, D:2 * D]
            g0 = wguard2[sl] if sl < 2 else wguard4[0]
            qn = "qv2" if sl == 2 else f"qv{sl}"
            on(WQE[sl], lambda e, ci=ci, wb_ap=wb_ap: e.dma_start(
                out=wb_ap[:], in_=wout_d[ci * 128:(ci + 1) * 128, :]),
               wait=[("t", g0)] + (SPLITG if sl == 2 else []), inc=qn, k=16)
            wt = now(qn) + [("v", mark[f"osr{ci}"])] + (wps if ci == 0 else [])
            first = True
            for jj in jjs:
                for half in (0, 1):
                    pn, off = OPS[jj][half]
                    pa = PS[pn]
                    on("tensor", lambda e, ci=ci, jj=jj, half=half, pa=pa, off=off, wb_ap=wb_ap: e.matmul(
                        pa[:, off:off + 512], D_v[:, ci * BLK + jj * 128:ci * BLK + jj * 128 + 128],
                        wb_ap[:, half * 512:(half + 1) * 512],
                        start=(ci == 0), stop=(ci == NCH - 1)),
                       wait=wt if first else None,
                       inc="t" if (jj == jjs[-1] and half == 1) else None)
                    first = False
            if sl < 2:
                wguard2[sl] = fl.cnt["t"]
            else:
                wguard4[0] = fl.cnt["t"]
        wte = now("t")
        for jj in jjs:
            t = b * TPB + jj
            (pn0, off0), (pn1, off1) = OPS[jj]
            on("vector", lambda e, jj=jj, t=t: e.tensor_scalar(
                sgx[:], xt[jj][:], m0_all[:, t:t + 1], rsig0_all[:, t:t + 1],
                Alu.subtract, Alu.mult), wait=wte, inc="v")
            on("vector", lambda e, jj=jj, pn0=pn0, off0=off0: e.tensor_add(
                x2t[jj][:, 0:512], PS[pn0][:, off0:off0 + 512], sgx[:, 0:512]), inc="v")
            on("vector", lambda e, jj=jj, pn1=pn1, off1=off1: e.tensor_add(
                x2t[jj][:, 512:1024], PS[pn1][:, off1:off1 + 512], sgx[:, 512:1024]), inc="v")
            pfree[OPS[jj][0][0]] = ("v", fl.cnt["v"])
            pfree[OPS[jj][1][0]] = ("v", fl.cnt["v"])

    def ln_norm(src_ap, wslot, bslot, dst_cols, mask_last=False, p=0):
        xn_p = xnt if p == 0 else xnt2
        u1_p = u1t if p == 0 else u1t2
        PT = P_F if p == 0 else P_E
        pfn = "P_F" if p == 0 else "P_E"
        sto = p * 16

        def stp(i, sto=sto):
            return stat[:, sto + i:sto + i + 1]
        on("scalar", lambda e: e.activation(u1_p[:], src_ap, Act.Copy, accum_out=stp(0)),
           wait=[("v", mark[f"tile_v{p}"]), ("t", mark[f"tile_t{p}"])] + now("v"), inc="s")
        on("scalar", lambda e: e.activation(u1_p[:], src_ap, Act.Square, accum_out=stp(1)), inc="s")
        on("scalar", lambda e: e.mul(stp(2), stp(0), R1024), inc="s")
        on("scalar", lambda e: e.mul(stp(3), stp(1), R1024), inc="s")
        on("scalar", lambda e: e.mul(stp(4), stp(2), stp(2)), inc="s")
        on("scalar", lambda e: e.activation(stp(5), stp(4), Act.Identity,
                                            scale=-1.0, bias=stp(3)), inc="s")
        on("scalar", lambda e: e.activation(stp(6), stp(5), Act.Ln, bias=EPS), inc="s")
        on("scalar", lambda e: e.activation(stp(7), stp(6), Act.Exp, scale=-0.5), inc="s")
        ws = now("s")
        on("vector", lambda e: e.tensor_scalar(xn_p[:], src_ap, stp(2), stp(7),
                                               Alu.subtract, Alu.mult), wait=ws, inc="v")
        mark[f"tile_v{p}"] = fl.cnt["v"]
        wv = now("v") + [("v", mark[f"pf_evac{p}"])] + [pfree[pfn]]
        for c in range(NCH):
            on("tensor", lambda e, c=c: e.matmul(
                PT[:, c * 128:(c + 1) * 128], xn_p[:, c * 128:(c + 1) * 128], idn[:],
                is_transpose=True, start=(c % 4 == 0), stop=(c % 4 == 3)),
               wait=wv if c == 0 else None, inc="t" if c == NCH - 1 else None)
        mark[f"tile_t{p}"] = fl.cnt["t"]
        wt = now("t")
        for c in range(NCH):
            on("vector", lambda e, c=c: e.tensor_scalar(
                C_k[:, c * BLK + dst_cols[0]: c * BLK + dst_cols[1]],
                PT[:, c * 128:(c + 1) * 128], pv(wslot, c), pv(bslot, c),
                Alu.mult, Alu.add), wait=wt if c == 0 else None, inc="v")
        mark[f"pf_evac{p}"] = fl.cnt["v"]
        pfree[pfn] = ("v", fl.cnt["v"])
        if mask_last:
            on("vector", lambda e: None, wait=now("v"))
            for c in range(NCH):
                on("vector", lambda e, c=c: e.tensor_scalar(
                    C_k[:, c * BLK + 511:c * BLK + 512], C_k[:, c * BLK + 511:c * BLK + 512],
                    mh[:, 0:1], None, Alu.mult), inc="v")

    def stageL2(b):
        jjs = [3] if b == 0 else [0, 1, 2, 3]
        for jj in jjs:
            ln_norm(x2t[jj][:], 2, 3, (jj * 128, jj * 128 + 128),
                    mask_last=(b == 0 and jj == 3), p=jj % 2)
        if b == 0:
            on("vector", lambda e: e.tensor_copy(bnd2[:, 0:NCH], C_k[:, BLK - 1::BLK]),
               wait=now("v"), inc="v")

    def stageF(b):
        wgF3k[0] = max(wgF3k[0], fl.cnt["t"])
        wgF3v[0] = max(wgF3v[0], fl.cnt["t"])
        mix_into_B(10, 0, BLK, wextra=now("t"), src=C_k, bnd=bnd2)
        wsrc = now("v")
        for ci in range(NCH):
            sl = wcnt2[0] % 2
            wcnt2[0] += 1
            wb_ap = wbuf2[sl] if sl < 2 else absF[:, D:2 * D]
            g0 = wguard2[sl] if sl < 2 else wguard4[0]
            qn = "qv2" if sl == 2 else f"qv{sl}"
            on(WQE[sl], lambda e, ci=ci, wb_ap=wb_ap: e.dma_start(
                out=wb_ap[:], in_=fwrec_d[ci * 128:(ci + 1) * 128, :]),
               wait=[("t", g0)] + (SPLITG if sl == 2 else []), inc=qn, k=16)
            wt = now(qn) + wsrc + ([pfree[p] for p in ("P_A", "P_B", "P_C", "P_D", "P_E", "P_F")] if ci == 0 else [])
            first = True
            for jj in range(TPB):
                for half in (0, 1):
                    pn, off = OPS[jj][half]
                    on("tensor", lambda e, ci=ci, jj=jj, half=half, pn=pn, off=off, wb_ap=wb_ap: e.matmul(
                        PS[pn][:, off:off + 512], B_mix[:, ci * BLK + jj * 128:ci * BLK + jj * 128 + 128],
                        wb_ap[:, half * 512:(half + 1) * 512],
                        start=(ci == 0), stop=(ci == NCH - 1)),
                       wait=wt if first else None, inc="t" if (jj == TPB - 1 and half == 1) else None)
                    first = False
            if sl < 2:
                wguard2[sl] = fl.cnt["t"]
            else:
                wguard4[0] = fl.cnt["t"]
        wte = now("t")
        for jj in range(TPB):
            (pn0, off0), (pn1, off1) = OPS[jj]
            on("scalar", lambda e, jj=jj, pn0=pn0, off0=off0: e.activation(
                frs[jj][:, 0:512], PS[pn0][:, off0:off0 + 512], Act.Sigmoid),
               wait=wte, inc="s")
            on("scalar", lambda e, jj=jj, pn1=pn1, off1=off1: e.activation(
                frs[jj][:, 512:1024], PS[pn1][:, off1:off1 + 512], Act.Sigmoid), inc="s")
            pfree[OPS[jj][0][0]] = ("s", fl.cnt["s"])
            pfree[OPS[jj][1][0]] = ("s", fl.cnt["s"])
        mix_into_B(9, 0, BLK, wextra=now("t"), src=C_k, bnd=bnd2)
        on("vector", lambda e: e.tensor_copy(bnd2[:, 0:NCH], C_k[:, BLK - 1::BLK]), inc="v")
        mark["vC"] = fl.cnt["v"]
        u1tB = u1t.bitcast(BF16)
        SCR = {0: (sgx[:, 0:256], sgx[:, 256:512], sgxB, 1024),
               1: (sgx[:, 768:1024], u1t[:, 0:256], u1tB, 512)}
        NCO = DH // 128
        for hh in (0, 1):
            h0 = hh * 256
            wb = now("t", "v") + SPLITG
            for c in range(NCH):
                on("scalar", lambda e, c=c, h0=h0: e.copy(
                    absB[:, c * 256:(c + 1) * 256], B_mix[:, c * BLK + h0:c * BLK + h0 + 256]),
                   wait=wb, inc="s")
                on("vector", lambda e, c=c, h0=h0: e.tensor_tensor(
                    absB[:, DM + c * 256:DM + (c + 1) * 256],
                    B_mix[:, c * BLK + h0:c * BLK + h0 + 256],
                    absB[:, c * 256:(c + 1) * 256], Alu.subtract),
                   wait=now("s"), inc="v")
            ws_x = now("s", "v")
            tfk = {}
            pm = {}

            def emit_fk(co):
                sl = wcnt[0] % 2
                wcnt[0] += 1
                wb_ap = wbufB[sl] if sl < 2 else C_B[:, 0:2 * D]
                g0 = wguard[sl] if sl < 2 else wgF3k[0]
                qn = "qw2" if sl == 2 else f"qw{sl}"
                gw = [("t", g0)] + SPLITG + ([("v", mark["vC"])] if sl == 2 else [])
                on(WQE[sl], lambda e, co=co, wb_ap=wb_ap: e.dma_start(
                    out=wb_ap[:, 0:2 * D].rearrange("p (ci o) -> p ci o", o=256),
                    in_=fwkeyHL_d.rearrange("(ci p) o -> p ci o", p=128)[:, :, co * 256:(co + 1) * 256]),
                   wait=gw, inc=qn, k=16)
                p = co % 2
                bk = P_A if p == 0 else P_B
                wt = now(qn) + (ws_x if co < 2 else []) \
                    + [pm.get(co - 2, pfree["P_A" if p == 0 else "P_B"])]
                first = True
                for ci in range(NCH):
                    for wo, ao in ((0, 0), (0, DM), (128, 0)):
                        on("tensor", lambda e, ci=ci, wb_ap=wb_ap, bk=bk, wo=wo, ao=ao: e.matmul(
                            bk[:, 0:256], wb_ap[:, ci * 256 + wo:ci * 256 + wo + 128],
                            absB[:, ao + ci * 256:ao + ci * 256 + 256],
                            start=(ci == 0 and wo == 0 and ao == 0),
                            stop=(ci == NCH - 1 and wo == 128)),
                           wait=wt if first else None,
                           inc="t" if (ci == NCH - 1 and wo == 128) else None)
                        first = False
                if sl < 2:
                    wguard[sl] = fl.cnt["t"]
                else:
                    wgF3k[0] = fl.cnt["t"]
                tfk[co] = fl.cnt["t"]

            def emit_chain_vk(co):
                p = co % 2
                R, M, PB, base = SCR[p]
                bk = P_A if p == 0 else P_B
                on("scalar", lambda e, R=R, bk=bk: e.activation(R, bk[:, 0:256], Act.Relu),
                   wait=[("t", tfk[co]), ("t", mark.get(f"fkscr{p}", 0))], inc="s")
                on("vector", lambda e, R=R, M=M, bk=bk: e.tensor_mul(M, R, bk[:, 0:256]),
                   wait=now("s"), inc="v")
                pm[co] = ("v", fl.cnt["v"])
                on("scalar", lambda e, M=M, PB=PB, base=base: e.copy(PB[:, base:base + 256], M),
                   wait=now("v"), inc="s")
                on("vector", lambda e, M=M, PB=PB, base=base: e.tensor_tensor(
                    PB[:, base + 256:base + 512], M, PB[:, base:base + 256], Alu.subtract),
                   wait=now("s"), inc="v")
                wv = now("v")
                sl2 = wcnt2[0] % 2
                wcnt2[0] += 1
                wb2_ap = wbuf2B[sl2] if sl2 < 2 else C_B[:, 2 * D:4 * D]
                g0 = wguard2[sl2] if sl2 < 2 else wgF3v[0]
                qn = "qv2" if sl2 == 2 else f"qv{sl2}"
                gw = [("t", g0)] + SPLITG + ([("v", mark["vC"])] if sl2 == 2 else [])
                on(WQE[sl2], lambda e, co=co, wb2_ap=wb2_ap: e.dma_start(
                    out=wb2_ap[:, 0:2 * D], in_=fwvalHL_d[co * 128:(co + 1) * 128, :]),
                   wait=gw, inc=qn, k=16)
                first = True
                for jl in (0, 1):
                    pvk = P_E if jl == 0 else P_F
                    for half in (0, 1):
                        for which, wo in ((0, 0), (0, D), (1, 0)):
                            on("tensor", lambda e, jl=jl, half=half, pvk=pvk, wb2_ap=wb2_ap,
                               which=which, wo=wo, PB=PB, base=base: e.matmul(
                                pvk[:, half * 512:(half + 1) * 512],
                                PB[:, base + which * 256 + jl * 128:base + which * 256 + jl * 128 + 128],
                                wb2_ap[:, wo + half * 512:wo + half * 512 + 512],
                                start=(co == 0 and which == 0 and wo == 0),
                                stop=(co == NCO - 1 and which == 1)),
                               wait=(wv + now(qn) + ([pfree["P_E"], pfree["P_F"]] if co == 0 else [])) if first else None,
                               inc="t" if (jl == 1 and half == 1 and which == 1) else None)
                            first = False
                if sl2 < 2:
                    wguard2[sl2] = fl.cnt["t"]
                else:
                    wgF3v[0] = fl.cnt["t"]
                mark[f"fkscr{p}"] = fl.cnt["t"]

            for co in range(NCO):
                emit_fk(co)
                if co >= 1:
                    emit_chain_vk(co - 1)
            emit_chain_vk(NCO - 1)
            pfree["P_A"], pfree["P_B"] = pm[NCO - 2], pm[NCO - 1]
            wte = now("t")
            for jl in (0, 1):
                jj = hh * 2 + jl
                pvk = P_E if jl == 0 else P_F
                on("vector", lambda e, jj=jj, pvk=pvk: e.tensor_mul(
                    u1t[:], frs[jj][:], pvk[:]), wait=wte, inc="v")
                on("vector", lambda e, jj=jj: e.tensor_add(
                    x2t[jj][:], u1t[:], x2t[jj][:]), wait=now("v"), inc="v")
                pfree["P_E" if jl == 0 else "P_F"] = ("v", fl.cnt["v"])

    def gates_combine(b):
        def slot(i):
            return rsc[:, i * 8:(i + 1) * 8]

        def vsw():
            on("vector", lambda e: None, wait=now("v"))

        def ssw():
            on("scalar", lambda e: None, wait=now("s"))
        for jj in range(TPB):
            t = b * TPB + jj
            to = (b - 1) * TPB + jj
            on("vector", lambda e, to=to, t=t: e.tensor_scalar(
                slot(0), z_own[:, to * 8:(to + 1) * 8], rsig0_all[:, t:t + 1], None,
                Alu.mult), wait=now("s"), inc="v")
            vsw()
            on("vector", lambda e: e.tensor_reduce(gs(0), slot(0), mybir.AxisListType.X,
                                                   Alu.max, negate=True), inc="v")
            wv = now("v")
            on("scalar", lambda e: e.activation(slot(1), slot(0), Act.Exp, bias=gs(0),
                                                accum_out=gs(1)), wait=wv, inc="s")
            ws = now("s")
            on("vector", lambda e: e.reciprocal(gs(2), gs(1)), wait=ws, inc="v")
            wv = now("v")
            on("scalar", lambda e: e.mul(slot(2), slot(1), gs(2)), wait=wv, inc="s")
            ws = now("s")
            on("vector", lambda e, to=to: e.tensor_mul(slot(3), oh1k_own[:, to * 8:(to + 1) * 8],
                                                       slot(2)), wait=ws, inc="v")
            vsw()
            on("vector", lambda e: e.tensor_reduce(gs(3), slot(3), mybir.AxisListType.X, Alu.add), inc="v")
            vsw()
            on("vector", lambda e, to=to: e.tensor_mul(slot(4), oh2k_own[:, to * 8:(to + 1) * 8],
                                                       slot(2)), inc="v")
            vsw()
            on("vector", lambda e: e.tensor_reduce(gs(4), slot(4), mybir.AxisListType.X, Alu.add), inc="v")
            wv = now("v")
            on("scalar", lambda e: e.activation(gs(5), gs(3), Act.Identity, bias=gs(4)),
               wait=wv, inc="s")
            ssw()
            on("scalar", lambda e: e.activation(gs(6), gs(5), Act.Identity, bias=1e-6), inc="s")
            ws = now("s")
            on("vector", lambda e: e.reciprocal(gs(7), gs(6)), wait=ws, inc="v")
            vsw()
            on("vector", lambda e: e.tensor_scalar(gs(8), gs(3), gs(7), None, Alu.mult), inc="v")
            vsw()
            on("vector", lambda e: e.tensor_scalar(gs(9), gs(4), gs(7), None, Alu.mult), inc="v")
            wv = now("v")
            on("scalar", lambda e, to=to: e.mul(slot(5), oh1k_own[:, to * 8:(to + 1) * 8], gs(8)),
               wait=wv, inc="s")
            ssw()
            on("scalar", lambda e, to=to: e.mul(slot(6), oh2k_own[:, to * 8:(to + 1) * 8], gs(9)), inc="s")
            ws = now("s")
            on("vector", lambda e, to=to: e.tensor_add(cmb[:, to * 8:(to + 1) * 8],
                                                       slot(5), slot(6)), wait=ws, inc="v")
            vsw()

    def stageM(b):
        wguard3[0] = max(wguard3[0], fl.cnt["t"])
        absB3 = absB[:, 0:2 * D]
        gates_combine(b)
        for hh in (0, 1):
            for jl in (0, 1):
                jj = hh * 2 + jl
                t = b * TPB + jj
                on("vector", lambda e, jj=jj, t=t: e.tensor_scalar(
                    xnt[:], xt[jj][:], m0_all[:, t:t + 1], rsig0_all[:, t:t + 1],
                    Alu.subtract, Alu.mult), wait=[("s", mark.get("xnt_used_s", 0))], inc="v")
                wv = now("v") + [pfree["P_F"]]
                for c in range(NCH):
                    on("tensor", lambda e, c=c: e.matmul(
                        P_F[:, c * 128:(c + 1) * 128], xnt[:, c * 128:(c + 1) * 128], idn[:],
                        is_transpose=True, start=(c % 4 == 0), stop=(c % 4 == 3)),
                       wait=wv if c == 0 else None, inc="t" if c == NCH - 1 else None)
                wt = now("t")
                for c in range(NCH):
                    on("scalar", lambda e, c=c, jl=jl: e.copy(
                        XNTs[:, c * 256 + jl * 128:c * 256 + jl * 128 + 128],
                        P_F[:, c * 128:(c + 1) * 128]), wait=wt if c == 0 else None, inc="s")
                mark["xnt_used_s"] = fl.cnt["s"]
                pfree["P_F"] = ("s", fl.cnt["s"])
            ws_x = now("s")
            for c in range(NCH):
                on("scalar", lambda e, c=c: e.copy(
                    A_B[:, 4096 + c * 256:4096 + (c + 1) * 256], XNTs[:, c * 256:(c + 1) * 256]),
                   wait=ws_x, inc="s")
                on("vector", lambda e, c=c: e.tensor_tensor(
                    A_B[:, 6144 + c * 256:6144 + (c + 1) * 256], XNTs[:, c * 256:(c + 1) * 256],
                    A_B[:, 4096 + c * 256:4096 + (c + 1) * 256], Alu.subtract),
                   wait=now("s"), inc="v")
            ws_x = now("s", "v")
            for mo in range(DM // 128):
                sl = mo % 3
                wb_ap = wbufB[sl] if sl < 2 else absB3
                g0 = wguard[sl] if sl < 2 else wguard3[0]
                on(WQE[sl], lambda e, mo=mo, wb_ap=wb_ap: e.dma_start(
                    out=wb_ap[:, 0:2 * D].rearrange("p (ci o) -> p ci o", o=256),
                    in_=wgateHL_d.rearrange("(ci p) o -> p ci o", p=128)[:, :, mo * 256:(mo + 1) * 256]),
                   wait=[("t", g0)] + SPLITG, inc=("qw2" if sl == 2 else f"qw{sl}"), k=16)
                pn = "P_A" if mo % 2 == 0 else "P_B"
                wt = now("qw2" if sl == 2 else f"qw{sl}") + ws_x + [pfree[pn]]
                first = True
                for ci in range(NCH):
                    for wo, ao in ((0, 4096), (0, 6144), (128, 4096)):
                        on("tensor", lambda e, ci=ci, wb_ap=wb_ap, pn=pn, wo=wo, ao=ao: e.matmul(
                            PS[pn][:, 0:256], wb_ap[:, ci * 256 + wo:ci * 256 + wo + 128],
                            A_B[:, ao + ci * 256:ao + ci * 256 + 256],
                            start=(ci == 0 and wo == 0 and ao == 4096),
                            stop=(ci == NCH - 1 and wo == 128)),
                           wait=wt if first else None,
                           inc="t" if (ci == NCH - 1 and wo == 128) else None)
                        first = False
                if sl < 2:
                    wguard[sl] = fl.cnt["t"]
                else:
                    wguard3[0] = fl.cnt["t"]
                wte = now("t")
                on("scalar", lambda e, pn=pn: e.activation(sgx[:, 0:256], PS[pn][:, 0:256],
                                                           Act.Sigmoid),
                   wait=wte + [("v", mark.get("sgm_used", 0))], inc="s")
                on("vector", lambda e, pn=pn: e.tensor_mul(
                    sgx[:, 256:512], sgx[:, 0:256], PS[pn][:, 0:256]),
                   wait=now("s"), inc="v")
                pfree[pn] = ("v", fl.cnt["v"])
                on("scalar", lambda e, mo=mo: e.copy(
                    C_B[:, mo * 256:(mo + 1) * 256], sgx[:, 256:512]), wait=now("v"), inc="s")
                on("vector", lambda e, mo=mo: e.tensor_tensor(
                    C_B[:, 4096 + mo * 256:4096 + (mo + 1) * 256], sgx[:, 256:512],
                    C_B[:, mo * 256:(mo + 1) * 256], Alu.subtract),
                   wait=now("s"), inc="v")
                mark["sgm_used"] = fl.cnt["v"]
                mark[f"hms{mo}"] = fl.cnt["v"]
            DB = {0: [["P_E", "P_E"], ["P_F", "P_F"]],
                  1: [["P_A", "P_B"], ["P_C", "P_D"]]}

            def dslice(p, jl, half):
                nm = DB[p][jl][half]
                off = half * 512 if p == 0 else 0
                return PS[nm][:, off:off + 512]
            for e_ in range(NE):
                p = e_ % 2
                bset = ["P_E", "P_F"] if p == 0 else ["P_A", "P_B", "P_C", "P_D"]
                for mo in range(DM // 128):
                    k = e_ * (DM // 128) + mo
                    sl = k % 3
                    wb_ap = wbuf2B[sl] if sl < 2 else absB3
                    g0 = wguard2[sl] if sl < 2 else wguard3[0]
                    on(WQE[sl], lambda e, e_=e_, mo=mo, wb_ap=wb_ap: e.dma_start(
                        out=wb_ap[:, 0:2 * D], in_=wdownHL_d[e_, mo * 128:(mo + 1) * 128, :]),
                       wait=[("t", g0)] + SPLITG, inc=("qv2" if sl == 2 else f"qv{sl}"), k=16)
                    wt = now("qv2" if sl == 2 else f"qv{sl}") + [("v", mark[f"hms{mo}"])] \
                        + ([pfree[nm] for nm in bset] if mo == 0 else [])
                    first = True
                    for jl in (0, 1):
                        for half in (0, 1):
                            for ho, wo in ((0, 0), (0, D), (4096, 0)):
                                on("tensor", lambda e, jl=jl, half=half, p=p, wb_ap=wb_ap, mo=mo, ho=ho, wo=wo: e.matmul(
                                    dslice(p, jl, half),
                                    C_B[:, ho + mo * 256 + jl * 128:ho + mo * 256 + jl * 128 + 128],
                                    wb_ap[:, wo + half * 512:wo + half * 512 + 512],
                                    start=(mo == 0 and ho == 0 and wo == 0),
                                    stop=(mo == DM // 128 - 1 and ho == 4096)),
                                   wait=wt if first else None,
                                   inc="t" if (jl == 1 and half == 1 and ho == 4096) else None)
                                first = False
                    if sl < 2:
                        wguard2[sl] = fl.cnt["t"]
                    else:
                        wguard3[0] = fl.cnt["t"]
                wte = now("t")
                for jl in (0, 1):
                    jj = hh * 2 + jl
                    to = (b - 1) * TPB + jj
                    cc = cmb[:, to * 8 + e_:to * 8 + e_ + 1]
                    if p == 0:
                        pvk = P_E if jl == 0 else P_F
                        if e_ == 0:
                            on("vector", lambda e, jj=jj, pvk=pvk, cc=cc: e.tensor_scalar(
                                yt[jj], pvk[:], cc, None, Alu.mult), wait=wte, inc="v")
                        else:
                            on("vector", lambda e, jj=jj, pvk=pvk, cc=cc: e.scalar_tensor_tensor(
                                yt[jj], pvk[:], cc, yt[jj], Alu.mult, Alu.add), wait=wte, inc="v")
                        pfree[DB[0][jl][0]] = ("v", fl.cnt["v"])
                    else:
                        for half in (0, 1):
                            on("vector", lambda e, jj=jj, jl=jl, half=half, cc=cc: e.scalar_tensor_tensor(
                                yt[jj][:, half * 512:(half + 1) * 512], dslice(1, jl, half), cc,
                                yt[jj][:, half * 512:(half + 1) * 512], Alu.mult, Alu.add),
                               wait=wte if (jl == 0 and half == 0) else None, inc="v")
                            pfree[DB[1][jl][half]] = ("v", fl.cnt["v"])
            for jl in (0, 1):
                jj = hh * 2 + jl
                wv = now("v")
                on("scalar", lambda e, jj=jj: e.activation(sgx[:], yt[jj], Act.Sigmoid),
                   wait=wv + [("v", mark.get("sgm_used", 0))], inc="s")
                ws = now("s")
                on("vector", lambda e, jj=jj: e.tensor_mul(yt[jj], sgx[:], yt[jj]),
                   wait=ws, inc="v")
                mark["sgm_used"] = fl.cnt["v"]
                on("vector", lambda e, jj=jj: e.tensor_mul(yt[jj], yt[jj], x2t[jj][:]), inc="v")
                on("vector", lambda e, jj=jj: e.tensor_add(yt[jj], yt[jj], xt[jj][:]), inc="v")
                wv = now("v")
                row0 = (b - 1) * BLK + jj * 128
                on("sync", lambda e, jj=jj, row0=row0: e.dma_start(
                    out=y_out[row0:row0 + 128, :], in_=yt[jj]), wait=wv, inc=f"qo{jj}", k=16)

    for b in range(NBLK):
        stageA(b)
        stageB(b)
        if DEV_STOP == "B":
            mark["x_users"] = fl.cnt["v"]
            continue
        stageS(b)
        if DEV_STOP == "S":
            mark["x_users"] = fl.cnt["v"]
            continue
        stageO(b)
        if DEV_STOP != "O":
            stageL2(b)
        if b >= 1 and DEV_STOP is None:
            stageF(b)
            stageM(b)
            on("vector", lambda e: None, wait=now("qo0", "qo1", "qo2", "qo3"))
        mark["x_users"] = fl.cnt["v"]





def _host_prep(inputs):
    x = np.ascontiguousarray(np.asarray(inputs["L0_attn_x"], np.float32).reshape(NTOK, D))
    pvec = np.stack([
        np.asarray(inputs["L0_attn_ln1_w"], np.float32),
        np.asarray(inputs["L0_attn_ln1_b"], np.float32),
        np.asarray(inputs["L0_attn_ln2_w"], np.float32),
        np.asarray(inputs["L0_attn_ln2_b"], np.float32),
        np.asarray(inputs["L0_attn_time_decay"], np.float32),
        np.asarray(inputs["L0_attn_time_first"], np.float32),
        np.asarray(inputs["L0_attn_time_mix_k"], np.float32).reshape(D),
        np.asarray(inputs["L0_attn_time_mix_v"], np.float32).reshape(D),
        np.asarray(inputs["L0_attn_time_mix_r"], np.float32).reshape(D),
        np.asarray(inputs["L0_attn_f_time_mix_k"], np.float32).reshape(D),
        np.asarray(inputs["L0_attn_f_time_mix_r"], np.float32).reshape(D),
        np.zeros(D, np.float32),
    ], axis=0)
    pvec = pvec.reshape(12, NCH, 128).transpose(2, 0, 1).reshape(128, 12 * NCH)
    shared = {
        "xall": x,
        "idn": np.eye(128, dtype=np.float32),
        "triu": np.triu(np.ones((128, 128), np.float32)),
        "onesc": np.ones((128, 1), np.float32),
        "onesr": np.ones((1, 128), np.float32),
        "wkey": np.ascontiguousarray(np.asarray(inputs["L0_attn_w_key"], np.float32)),
        "wval": np.ascontiguousarray(np.asarray(inputs["L0_attn_w_value"], np.float32)),
        "wrec": np.ascontiguousarray(np.asarray(inputs["L0_attn_w_recept"], np.float32)),
        "wout": np.ascontiguousarray(np.asarray(inputs["L0_attn_w_output"], np.float32)),
        "fwkey": np.ascontiguousarray(np.asarray(inputs["L0_attn_f_w_key"], np.float32)),
        "fwrec": np.ascontiguousarray(np.asarray(inputs["L0_attn_f_w_recept"], np.float32)),
        "fwval": np.ascontiguousarray(np.asarray(inputs["L0_attn_f_w_value"], np.float32)),
        "wgateT": np.ascontiguousarray(np.asarray(inputs["L0_ffn_w_gate"], np.float32).T),
        "wdownT": np.ascontiguousarray(np.asarray(inputs["L0_ffn_w_down"], np.float32).transpose(0, 2, 1)),
        "wrouter": np.ascontiguousarray(np.asarray(inputs["L0_ffn_w_router"], np.float32)),
        "pvec": np.ascontiguousarray(pvec),
        "fconst": np.tile(np.array(FCONSTS, np.float32), (128, 1)),
    }
    in_maps = []
    for c in range(NC):
        q = c % 4
        gs = c * OWN
        if q == 0:
            xph = np.concatenate([np.zeros((NH, D), np.float32), x[gs:gs + OWN]], axis=0)
            m = 0.0
        else:
            xph = x[gs - NH:gs + OWN]
            m = 1.0
        im = dict(shared)
        im["xph"] = np.ascontiguousarray(xph)
        im["mh"] = np.full((128, 1), m, np.float32)
        im["mhinv"] = np.full((128, 1), 1.0 - m, np.float32)
        im["rank"] = np.full((1, 1), c, np.int32)
        gs_ = np.zeros((128, 8), np.float32)
        gs_[:, c] = 1.0
        im["gsel"] = gs_
        in_maps.append(im)
    return in_maps


def kernel(**inputs):
    in_maps = _host_prep(inputs)
    nc, ctx = build_program(taps=TAPS)
    try:
        res = run_bass_kernel_spmd(nc, in_maps, core_ids=list(range(NC)))
    finally:
        ctx.close()
    parts = [res.results[c]["y"] for c in range(NC)]
    out = np.concatenate(parts, axis=0).reshape(NB, T, D).astype(np.float32)
    if TAPS:
        kernel.last_results = res.results
    return out


# version 21
# speedup vs baseline: 1.0789x; 1.0572x over previous
import numpy as np
import concourse.bass as bass
import concourse.mybir as mybir
from concourse.bass_utils import run_bass_kernel_spmd
from contextlib import ExitStack

F32 = mybir.dt.float32
I32 = mybir.dt.int32
Act = mybir.ActivationFunctionType
Alu = mybir.AluOpType

D = 1024
NCH = 8
DH = 4096
DM = 2048
NE = 8
CAP = 2560.0
NB = 2
T = 8192
NTOK = NB * T
NC = 8
OWN = 2048
NH = 512
NT = NH + OWN
BLK = 512
NBLK = NT // BLK
TPB = BLK // 128
NTILE = NT // 128
GT = NTOK // 128
EPS = 1e-5
R1024 = 1.0 / 1024.0

ENGINES = ("sync", "gpsimd", "tensor", "vector", "scalar")
FCONSTS = (1e-5, -0.5, -1.0, 1.0 / 1024.0, 1.0, 2560.5, 1e-6, -1e30)

TAPS = ()
DEV_SKIP = ()
DEV_STOP = None


class Flow:

    def __init__(self, nc, ctx, semnames):
        self.nc = nc
        self.prog = []
        self.sem = {}
        self.cnt = {}
        self.lastw = {e: {} for e in ENGINES}
        for n in semnames:
            self.sem[n] = ctx.enter_context(nc.semaphore(n))
            self.cnt[n] = 0

    def now(self, *names):
        return [(n, self.cnt[n]) for n in names]

    AUTOSELF = {"vector": "v", "scalar": "s"}

    def on(self, eng, fn, inc=None, k=1, wait=None):
        wait = list(wait or [])
        aself = self.AUTOSELF.get(eng)
        if aself is not None:
            wait.append((aself, self.cnt[aself]))
        w = []
        for sname, val in wait:
            if val > self.lastw[eng].get(sname, 0):
                w.append((sname, val))
                self.lastw[eng][sname] = val
        if inc is not None:
            self.cnt[inc] += k
        sems = self.sem

        def run(e, _fn=fn, _w=w, _inc=inc, _k=k):
            for sname, val in _w:
                e.wait_ge(sems[sname], val)
            ins = _fn(e)
            if _inc is not None:
                assert ins is not None
                ins.then_inc(sems[_inc], _k)
        self.prog.append((eng, run))

    def barrier_from(self, eng, *sems):
        self.on(eng, lambda e: None, wait=self.now(*sems))


def build_program(taps=()):
    nc = bass.Bass("TRN2", target_bir_lowering=False, debug=False)
    ctx = ExitStack()

    def dram_in(name, shape, dt=F32):
        return nc.dram_tensor(name, shape, dt, kind="ExternalInput").ap()

    def dram_out(name, shape, dt=F32):
        return nc.dram_tensor(name, shape, dt, kind="ExternalOutput").ap()

    def sb(name, shape, dt=F32):
        return ctx.enter_context(nc.sbuf_tensor(name, shape, dt)).ap()

    def ps(name, shape, dt=F32):
        return ctx.enter_context(nc.psum_tensor(name, shape, dt)).ap()

    xall = dram_in("xall", [NTOK, D])
    xph = dram_in("xph", [NT, D])
    mh_d = dram_in("mh", [128, 1])
    mhinv_d = dram_in("mhinv", [128, 1])
    rank_d = dram_in("rank", [1, 1], I32)
    gsel_d = dram_in("gsel", [128, 8])
    idn_d = dram_in("idn", [128, 128])
    triu_d = dram_in("triu", [128, 128])
    onesc_d = dram_in("onesc", [128, 1])
    onesr_d = dram_in("onesr", [1, 128])
    fconst_d = dram_in("fconst", [128, 8])
    wkey_d = dram_in("wkey", [D, D])
    wval_d = dram_in("wval", [D, D])
    wrec_d = dram_in("wrec", [D, D])
    wout_d = dram_in("wout", [D, D])
    fwkey_d = dram_in("fwkey", [D, DH])
    fwrec_d = dram_in("fwrec", [D, D])
    fwval_d = dram_in("fwval", [DH, D])
    wgateT_d = dram_in("wgateT", [D, DM])
    wdownT_d = dram_in("wdownT", [NE, DM, D])
    wrouter_d = dram_in("wrouter", [NE, D])
    pvec_d = dram_in("pvec", [128, 12 * NCH])

    y_out = dram_out("y", [OWN, D])
    BF16 = mybir.dt.bfloat16
    fwkeyHL_d = nc.dram_tensor("fwkeyHL", [D, 2 * DH], BF16, kind="Internal").ap()
    fwvalHL_d = nc.dram_tensor("fwvalHL", [DH, 2 * D], BF16, kind="Internal").ap()
    wgateHL_d = nc.dram_tensor("wgateHL", [D, 2 * DM], BF16, kind="Internal").ap()
    wdownHL_d = nc.dram_tensor("wdownHL", [NE, DM, 2 * D], BF16, kind="Internal").ap()

    idn = sb("idn_sb", [128, 128])
    triu = sb("triu_sb", [128, 128])
    onesc = sb("onesc_sb", [128, 1])
    onesr = sb("onesr_sb", [1, 128])
    mh = sb("mh_s", [128, 1])
    mhinv = sb("mhinv_s", [128, 1])
    rank_s = sb("rank_s", [1, 1], I32)
    gsel = sb("gsel_s", [128, 8])
    pvec = sb("pvec_s", [128, 12 * NCH])
    fconst = sb("fconst_s", [128, 8])
    for _i, _v in enumerate(FCONSTS):
        nc.const_aps.register(F32, _v, fconst[:, _i:_i + 1])
    wder = sb("wder", [128, 3 * NCH])
    negbig = sb("negbig", [128, 8])
    wr_raw = sb("wr_raw", [NE, D])
    wr_sum = sb("wr_sum", [NE, 1])
    wrcT = sb("wrcT", [128, NCH * NE])

    z_all = sb("z_all", [128, GT * NE])
    oh1k_all = sb("oh1k_all", [128, GT * NE], BF16)
    oh2k_all = sb("oh2k_all", [128, GT * NE], BF16)
    offrow = sb("offrow", [1, NE])
    z_own = sb("z_own", [128, 16 * NE])
    oh1k_own = sb("oh1k_own", [128, 16 * NE])
    oh2k_own = sb("oh2k_own", [128, 16 * NE])
    rxt = [sb(f"rxt{i}", [128, D]) for i in range(2)]
    rxT = [sb(f"rxT{i}", [128, D]) for i in range(2)]
    rsc = sb("rsc", [128, 64])
    rt_s = sb("rt_s", [128, 5 * 128])
    cbuf = sb("cbuf", [1, 136])
    gsel_d2 = None

    rsig0_all = sb("rsig0_all", [128, NTILE])
    m0_all = sb("m0_all", [128, NTILE])
    xt = [sb(f"xt{i}", [128, D]) for i in range(TPB)]
    xnt = sb("xnt", [128, D])
    xnt2 = sb("xnt2", [128, D])
    u1t = sb("u1t", [128, D])
    u1t2 = sb("u1t2", [128, D])
    stat = sb("stat", [128, 32])
    A_x1T = sb("A_x1T", [128, NCH * BLK])
    B_mix = sb("B_mix", [128, NCH * BLK])
    C_k = sb("C_k", [128, NCH * BLK])
    D_v = sb("D_v", [128, NCH * BLK])
    F_S = sb("F_S", [128, NCH * (BLK + 1)])
    G_Z = sb("G_Z", [128, NCH * (BLK + 1)])
    x2t = rxt + rxT
    bnd1 = sb("bnd1", [128, NCH])
    bnd2 = sb("bnd2", [128, NCH])
    XNTs = A_x1T[:, 0:NCH * 256]
    Hms = C_k[:, 0:16 * 256]

    wbuf = [sb(f"wbuf{i}", [128, D]) for i in range(2)]
    wbuf2 = [sb(f"wbuf2_{i}", [128, D]) for i in range(2)]
    wbufB = [w.bitcast(BF16) for w in wbuf]
    wbuf2B = [w.bitcast(BF16) for w in wbuf2]
    absF = sb("absF", [128, DM])
    absB = absF.bitcast(BF16)
    cnt_sb = sb("cnt_sb", [1, 1040])
    cnt2_sb = sb("cnt2_sb", [1, 1040])

    sgx = sb("sgx", [128, D])
    frs = [D_v[:, i * D:(i + 1) * D] for i in range(TPB)]
    yt = [B_mix[:, i * D:(i + 1) * D] for i in range(TPB)]
    cmb = sb("cmb", [128, 16 * NE])
    gsc = sb("gsc", [128, 16])

    P_A = ps("P_A", [128, 512])
    P_B = ps("P_B", [128, 512])
    P_C = ps("P_C", [128, 512])
    P_D = ps("P_D", [128, 512])
    P_E = ps("P_E", [128, 1024])
    P_F = ps("P_F", [128, 1024])

    fl = Flow(nc, ctx, ["qc", "qr0", "qr1", "qx0", "qx1", "qx2", "qx3",
                        "qw0", "qw1", "qv0", "qv1", "qo0", "qo1", "qo2", "qo3",
                        "qsi0", "qsi1", "qsi2", "qsi3", "qso0", "qso1", "qw2", "qv2", "t", "v", "s"])
    on, now = fl.on, fl.now

    def dma(e, dst, src):
        return e.dma_start(out=dst, in_=src)

    for dst, src in [(fconst, fconst_d), (gsel, gsel_d), (idn, idn_d), (triu, triu_d), (onesc, onesc_d),
                     (onesr, onesr_d), (mh, mh_d), (mhinv, mhinv_d),
                     (rank_s, rank_d), (wr_raw, wrouter_d)]:
        on("sync", lambda e, d=dst, s=src: dma(e, d, s), inc="qc", k=16)
    on("sync", lambda e: dma(e, pvec[:, 0:12 * NCH], pvec_d[:]), inc="qc", k=16)

    def pv(p, c):
        return pvec[:, p * NCH + c: p * NCH + c + 1]

    def der(i, c):
        return wder[:, i * NCH + c: i * NCH + c + 1]

    w_qx = now("qc")
    for c in range(NCH):
        on("scalar", lambda e, c=c: e.activation(der(2, c), pv(4, c), Act.Exp), wait=w_qx, inc="s")
        on("scalar", lambda e, c=c: e.activation(der(1, c), pv(5, c), Act.Exp), inc="s")
    on("scalar", lambda e: None, wait=now("s"))
    for c in range(NCH):
        on("scalar", lambda e, c=c: e.activation(der(0, c), der(2, c), Act.Exp, scale=-1.0), inc="s")
    on("vector", lambda e: e.memset(negbig[:], -1e30), inc="v")

    on("vector", lambda e: e.reduce_sum(wr_sum[:], wr_raw[:], axis=mybir.AxisListType.X), wait=w_qx, inc="v")
    on("scalar", lambda e: e.mul(wr_sum[:], wr_sum[:], R1024), wait=now("v"), inc="s")
    on("vector", lambda e: e.tensor_scalar(wr_raw[:], wr_raw[:], wr_sum[:, 0:1], None, Alu.subtract),
       wait=now("s"), inc="v")
    w_v = now("v", "qc")
    for c in range(NCH):
        on("tensor", lambda e, c=c: e.matmul(
            P_A[:, c * 8:(c + 1) * 8], wr_raw[:, c * 128:(c + 1) * 128], idn[0:NE, 0:NE],
            is_transpose=True, start=(c == 0), stop=(c == NCH - 1)),
           wait=w_v if c == 0 else None, inc="t" if c == NCH - 1 else None)
    on("scalar", lambda e: e.copy(wrcT[:, 0:NCH * NE], P_A[:, 0:NCH * NE]), wait=now("t"), inc="s")

    skip_routing = "routing" in DEV_SKIP
    GB = 16
    NG = GT // GB
    def rslot(i):
        return rt_s[:, i * 128:(i + 1) * 128]

    def rview(i):
        return rt_s[:, i * 128:(i + 1) * 128].rearrange("p (t e) -> p t e", e=NE)

    t_after_tr = {}
    s_after_ev = {}
    v_after_zev = {}

    def vsw():
        on("vector", lambda e: None, wait=now("v"))

    def _colhl(hl_ap, rs, cc, which):
        v = hl_ap[rs, cc * 2048:(cc + 1) * 2048].rearrange("p (co o) -> p co o", o=256)
        return v[:, :, 0:128] if which == 0 else v[:, :, 128:256]

    split_pieces = []
    for r in range(8):
        rs = slice(r * 128, (r + 1) * 128)
        for cc in range(4):
            sl = (rs, slice(cc * 1024, (cc + 1) * 1024))
            split_pieces.append((fwkey_d[sl], _colhl(fwkeyHL_d, rs, cc, 0),
                                 _colhl(fwkeyHL_d, rs, cc, 1)))
    for r in range(32):
        rs = slice(r * 128, (r + 1) * 128)
        split_pieces.append((fwval_d[rs, 0:D], fwvalHL_d[rs, 0:D], fwvalHL_d[rs, D:2 * D]))
    for r in range(8):
        rs = slice(r * 128, (r + 1) * 128)
        for cc in range(2):
            sl = (rs, slice(cc * 1024, (cc + 1) * 1024))
            split_pieces.append((wgateT_d[sl], _colhl(wgateHL_d, rs, cc, 0),
                                 _colhl(wgateHL_d, rs, cc, 1)))
    for e_ in range(NE):
        for mo in range(16):
            ms = slice(mo * 128, (mo + 1) * 128)
            split_pieces.append((wdownT_d[e_, ms, 0:D], wdownHL_d[e_, ms, 0:D],
                                 wdownHL_d[e_, ms, D:2 * D]))
    split_state = {"i": 0}
    SPL = 4

    def _issue_in(j, w):
        sl = j % SPL
        srcp = split_pieces[j][0]
        on("gpsimd", lambda e, sl=sl, srcp=srcp: e.dma_start(
            out=D_v[:, sl * D:(sl + 1) * D], in_=srcp),
           wait=w, inc=f"qsi{sl}", k=16)

    def emit_splits(n):
        for _ in range(n):
            i = split_state["i"]
            if i >= len(split_pieces):
                return
            split_state["i"] = i + 1
            if i == 0:
                for j in range(min(SPL, len(split_pieces))):
                    _issue_in(j, None)
            sl = i % SPL
            db = i % 2
            a0 = db * 2 * D
            _, hdst, ldst = split_pieces[i]
            on("scalar", lambda e, sl=sl, a0=a0: e.copy(absB[:, a0:a0 + D],
                                                        D_v[:, sl * D:(sl + 1) * D]),
               wait=now(f"qsi{sl}", f"qso{db}"), inc="s")
            on("vector", lambda e, sl=sl, a0=a0: e.tensor_tensor(
                absB[:, a0 + D:a0 + 2 * D], D_v[:, sl * D:(sl + 1) * D],
                absB[:, a0:a0 + D], Alu.subtract),
               wait=now("s"), inc="v")
            wv = now("v")
            on("gpsimd", lambda e, hdst=hdst, a0=a0: e.dma_start(out=hdst, in_=absB[:, a0:a0 + D]),
               wait=wv, inc=f"qso{db}", k=16)
            on("gpsimd", lambda e, ldst=ldst, a0=a0: e.dma_start(out=ldst, in_=absB[:, a0 + D:a0 + 2 * D]),
               inc=f"qso{db}", k=16)
            if i + SPL < len(split_pieces):
                _issue_in(i + SPL, wv)

    for g in range(GT if not skip_routing else 0):
        gi, ti = g // GB, g % GB
        db = g % 2
        on("sync", lambda e, g=g, db=db: dma(e, rxt[db][:], xall[g * 128:(g + 1) * 128, :]),
           wait=[("t", t_after_tr[g - 2])] if g >= 2 else None, inc=f"qr{db}", k=16)
        PT = P_E if g % 2 == 0 else P_F
        wt = now(f"qr{db}") + ([("s", s_after_ev[g - 2])] if g >= 2
                               else (now("qc") if g == 0 else []))
        for c in range(NCH):
            on("tensor", lambda e, c=c, db=db, PT=PT: e.matmul(
                PT[:, c * 128:(c + 1) * 128], rxt[db][:, c * 128:(c + 1) * 128], idn[:],
                is_transpose=True,
                start=(c % 4 == 0), stop=(c % 4 == 3)),
               wait=wt if c == 0 else None, inc="t" if c == NCH - 1 else None)
        t_after_tr[g] = fl.cnt["t"]
        wt = now("t")
        on("scalar", lambda e, db=db, PT=PT: e.copy(rxT[db][:, 0:512], PT[:, 0:512]), wait=wt, inc="s")
        on("scalar", lambda e, db=db, PT=PT: e.copy(rxT[db][:, 512:1024], PT[:, 512:1024]), inc="s")
        s_after_ev[g] = fl.cnt["s"]
        zb = P_A if gi % 2 == 0 else P_B
        ws = now("s") + ([("v", v_after_zev.get(gi - 2, 0))] if ti == 0 else [])
        for c in range(NCH):
            on("tensor", lambda e, c=c, db=db, ti=ti, zb=zb: e.matmul(
                zb[:, ti * NE:(ti + 1) * NE], rxT[db][:, c * 128:(c + 1) * 128],
                wrcT[:, c * 8:(c + 1) * 8],
                start=(ti == 0 and c == 0), stop=(ti == GB - 1 and c == NCH - 1)),
               wait=ws if c == 0 else None,
               inc="t" if (ti == GB - 1 and c == NCH - 1) else None)
        if ti == GB - 1:
            on("vector", lambda e, gi=gi, zb=zb: e.tensor_copy(
                z_all[:, gi * 128:(gi + 1) * 128], zb[:, 0:GB * NE]), wait=now("t"), inc="v")
            v_after_zev[gi] = fl.cnt["v"]
        emit_splits(2)

    if not skip_routing:
        O1 = A_x1T[:, 0:1024]
        S2 = A_x1T[:, 1024:2048]
        S3 = A_x1T[:, 2048:3072]
        ZM = A_x1T[:, 3072:4096]
        O2 = B_mix[:, 0:1024]
        SP = B_mix[:, 1024:2048]

        def wv3(ap):
            return ap.rearrange("p (t e) -> p t e", e=NE)

        def top1_wide(src, dst):
            vsw()
            on("vector", lambda e, src=src: e.tensor_reduce(
                rt_s[:, 0:GT], wv3(src), mybir.AxisListType.X, Alu.max), inc="v")
            vsw()
            on("vector", lambda e, src=src, dst=dst: e.tensor_tensor(
                wv3(dst), wv3(src),
                rt_s[:, 0:GT].unsqueeze(2).broadcast_to((128, GT, NE)), Alu.is_ge), inc="v")
            vsw()
            on("vector", lambda e, dst=dst: e.tensor_copy(S2, dst), inc="v")
            vsw()
            on("vector", lambda e, dst=dst: e.tensor_add(
                wv3(S2)[:, :, 1:8], wv3(dst)[:, :, 1:8], wv3(dst)[:, :, 0:7]), inc="v")
            vsw()
            on("vector", lambda e: e.tensor_copy(S3, S2), inc="v")
            vsw()
            on("vector", lambda e: e.tensor_add(
                wv3(S3)[:, :, 2:8], wv3(S2)[:, :, 2:8], wv3(S2)[:, :, 0:6]), inc="v")
            vsw()
            on("vector", lambda e: e.tensor_copy(S2, S3), inc="v")
            vsw()
            on("vector", lambda e: e.tensor_add(
                wv3(S2)[:, :, 4:8], wv3(S3)[:, :, 4:8], wv3(S3)[:, :, 0:4]), inc="v")
            vsw()
            on("vector", lambda e: e.tensor_single_scalar(S2, S2, 1.0, Alu.is_equal), inc="v")
            vsw()
            on("vector", lambda e, dst=dst: e.tensor_mul(dst, dst, S2), inc="v")

        top1_wide(z_all[:, 0:GT * NE], O1)
        vsw()
        on("vector", lambda e: e.scalar_tensor_tensor(
            ZM, O1, -1e30, z_all[:, 0:GT * NE], Alu.mult, Alu.add), inc="v")
        top1_wide(ZM, O2)
        vsw()
        on("vector", lambda e: e.tensor_add(S3, O1, O2), inc="v")
        wv = now("v")
        on("tensor", lambda e: e.matmul(P_C[0:1, 0:512], onesc[:], S3[:, 0:512],
                                        start=True, stop=True), wait=wv, inc="t")
        on("tensor", lambda e: e.matmul(P_D[0:1, 0:512], onesc[:], S3[:, 512:1024],
                                        start=True, stop=True), inc="t")
        wt = now("t")
        on("vector", lambda e: e.memset(cnt_sb[0:1, 0:8], 0.0), inc="v")
        on("vector", lambda e: e.tensor_copy(cnt_sb[0:1, 8:520], P_C[0:1, 0:512]), wait=wt, inc="v")
        on("vector", lambda e: e.tensor_copy(cnt_sb[0:1, 520:1032], P_D[0:1, 0:512]), inc="v")
        pp = [cnt_sb, cnt2_sb]
        for k, stp in enumerate((8, 16, 32, 64, 128, 256, 512)):
            sb_, db_ = pp[k % 2], pp[1 - k % 2]
            vsw()
            on("vector", lambda e, sb_=sb_, db_=db_, stp=stp: e.tensor_copy(
                db_[0:1, 0:8 + stp], sb_[0:1, 0:8 + stp]), inc="v")
            on("vector", lambda e, sb_=sb_, db_=db_, stp=stp: e.tensor_add(
                db_[0:1, 8 + stp:1032], sb_[0:1, 8 + stp:1032],
                sb_[0:1, 8:1032 - stp]), inc="v")
        cntF = pp[7 % 2]
        vsw()
        wv = now("v")
        firstb = {0: True, 1: True}
        for t in range(GT):
            bi = 0 if t < 64 else 1
            bk = P_A if bi == 0 else P_B
            col0 = (t % 64) * 8
            on("tensor", lambda e, t=t, bk=bk, col0=col0: e.matmul(
                bk[:, col0:col0 + 8], triu[:], S3[:, t * NE:(t + 1) * NE],
                start=(t % 64 == 0), stop=False),
               wait=wv if firstb[bi] else None)
            firstb[bi] = False
            on("tensor", lambda e, t=t, bk=bk, col0=col0: e.matmul(
                bk[:, col0:col0 + 8], onesr[:], cntF[0:1, t * NE:(t + 1) * NE],
                start=False, stop=(t % 64 == 63)),
               inc="t" if t % 64 == 63 else None)
        wt = now("t")
        on("vector", lambda e: e.tensor_single_scalar(
            S2[:, 0:512], P_A[:, 0:512], CAP + 0.5, Alu.is_le), wait=wt, inc="v")
        on("vector", lambda e: e.tensor_single_scalar(
            S2[:, 512:1024], P_B[:, 0:512], CAP + 0.5, Alu.is_le), inc="v")
        vsw()
        on("vector", lambda e: e.tensor_mul(O1, O1, S2), inc="v")
        on("vector", lambda e: e.tensor_mul(O2, O2, S2), inc="v")
        vsw()
        on("vector", lambda e: e.tensor_copy(oh1k_all[:, 0:GT * NE], O1), inc="v")
        on("vector", lambda e: e.tensor_copy(oh2k_all[:, 0:GT * NE], O2), inc="v")
        for srcb, dstb in ((z_all, z_own), (O1, oh1k_own), (O2, oh2k_own)):
            vsw()
            for gi in range(NG):
                on("vector", lambda e, srcb=srcb, gi=gi: e.tensor_scalar(
                    SP[:, gi * 128:(gi + 1) * 128], srcb[:, gi * 128:(gi + 1) * 128],
                    gsel[:, gi:gi + 1], None, Alu.mult), inc="v")
            vsw()
            on("vector", lambda e: e.tensor_add(ZM[:, 0:512], SP[:, 0:512], SP[:, 512:1024]), inc="v")
            vsw()
            on("vector", lambda e: e.tensor_add(ZM[:, 512:768], ZM[:, 0:256], ZM[:, 256:512]), inc="v")
            vsw()
            on("vector", lambda e, dstb=dstb: e.tensor_add(
                dstb[:, 0:128], ZM[:, 512:640], ZM[:, 640:768]), inc="v")

    emit_splits(10 ** 9)
    SPLITG = now("s", "v", "qso0", "qso1")

    _emit_blocks(nc, fl, locals())

    for tname in taps:
        if ":" in tname:
            nm, idx = tname.split(":")
            buf = locals()[nm][int(idx)]
        else:
            buf = locals()[tname]
        td = dram_out("tap_" + tname.replace(":", "_"), list(buf.shape), buf.dtype)
        on("sync", lambda e, b=buf, t=td: dma(e, t[:], b[:]),
           wait=now("v", "s", "t", "qw0", "qw1", "qv0", "qv1"), inc="qo0", k=16)

    on("sync", lambda e: None, wait=now("qo0", "qo1", "qo2", "qo3"))

    with nc.Block() as block:
        @block.sync
        def _(e):
            for eng, run in fl.prog:
                if eng == "sync":
                    run(e)

        @block.gpsimd
        def _(e):
            for eng, run in fl.prog:
                if eng == "gpsimd":
                    run(e)

        @block.tensor
        def _(e):
            for eng, run in fl.prog:
                if eng == "tensor":
                    run(e)

        @block.vector
        def _(e):
            for eng, run in fl.prog:
                if eng == "vector":
                    run(e)

        @block.scalar
        def _(e):
            for eng, run in fl.prog:
                if eng == "scalar":
                    run(e)

    return nc, ctx


def _emit_blocks(nc, fl, L):
    on, now = fl.on, fl.now
    g = lambda n: L[n]
    (xph, xt, xnt, u1t, stat, A_x1T, B_mix, C_k, D_v, F_S, G_Z, x2t, bnd1, bnd2,
     rsig0_all, m0_all, pvec, wder, mh, mhinv, idn, y_out) = (
        g("xph"), g("xt"), g("xnt"), g("u1t"), g("stat"), g("A_x1T"), g("B_mix"),
        g("C_k"), g("D_v"), g("F_S"), g("G_Z"), g("x2t"), g("bnd1"), g("bnd2"),
        g("rsig0_all"), g("m0_all"), g("pvec"), g("wder"), g("mh"), g("mhinv"),
        g("idn"), g("y_out"))
    P_A, P_B, P_C, P_D, P_E, P_F = g("P_A"), g("P_B"), g("P_C"), g("P_D"), g("P_E"), g("P_F")
    xnt2, u1t2 = g("xnt2"), g("u1t2")

    def pv(p, c):
        return pvec[:, p * NCH + c: p * NCH + c + 1]

    def der(i, c):
        return wder[:, i * NCH + c: i * NCH + c + 1]

    def st(i):
        return stat[:, i:i + 1]

    on("vector", lambda e: e.memset(bnd1[:], 0.0), inc="v")
    on("vector", lambda e: e.memset(bnd2[:], 0.0), inc="v")
    on("vector", lambda e: e.memset(stat[:], 0.0), inc="v")
    mark0_v = fl.cnt["v"]

    mark = {"x_users": 0, "pe_evac": fl.cnt["s"],
            "pf_evac0": fl.cnt["v"], "pf_evac1": fl.cnt["v"],
            "tile_v0": mark0_v, "tile_v1": mark0_v, "tile_t0": 0, "tile_t1": 0}

    def stageA(b):
        for tl in range(TPB):
            t = b * TPB + tl
            p = tl % 2
            xn_p = xnt if p == 0 else xnt2
            u1_p = u1t if p == 0 else u1t2
            PT = P_F if p == 0 else P_E
            pfn = "P_F" if p == 0 else "P_E"
            sto = p * 16

            def stp(i, sto=sto):
                return stat[:, sto + i:sto + i + 1]
            wprev = [("v", mark["x_users"])] if b > 0 else None
            on("sync", lambda e, t=t, tl=tl: e.dma_start(
                out=xt[tl][:], in_=xph[t * 128:(t + 1) * 128, :]),
               wait=wprev, inc=f"qx{tl}", k=16)
            wq = now(f"qx{tl}", "qc") + [("v", mark[f"tile_v{p}"]), ("t", mark[f"tile_t{p}"])]
            on("scalar", lambda e, tl=tl, u1_p=u1_p, stp=stp: e.activation(
                u1_p[:], xt[tl][:], Act.Copy, accum_out=stp(0)), wait=wq, inc="s")
            on("scalar", lambda e, tl=tl, u1_p=u1_p, stp=stp: e.activation(
                u1_p[:], xt[tl][:], Act.Square, accum_out=stp(1)), inc="s")
            on("scalar", lambda e, t=t, stp=stp: e.mul(m0_all[:, t:t + 1], stp(0), R1024), inc="s")
            on("scalar", lambda e, stp=stp: e.mul(stp(3), stp(1), R1024), inc="s")
            on("scalar", lambda e, t=t, stp=stp: e.mul(stp(4), m0_all[:, t:t + 1],
                                                       m0_all[:, t:t + 1]), inc="s")
            on("scalar", lambda e, stp=stp: e.activation(stp(5), stp(4), Act.Identity,
                                                         scale=-1.0, bias=stp(3)), inc="s")
            on("scalar", lambda e, stp=stp: e.activation(stp(6), stp(5), Act.Ln, bias=EPS), inc="s")
            on("scalar", lambda e, t=t, stp=stp: e.activation(rsig0_all[:, t:t + 1], stp(6), Act.Exp,
                                                              scale=-0.5), inc="s")
            ws = now("s")
            on("vector", lambda e, tl=tl, t=t, xn_p=xn_p: e.tensor_scalar(
                xn_p[:], xt[tl][:], m0_all[:, t:t + 1], rsig0_all[:, t:t + 1],
                Alu.subtract, Alu.mult), wait=ws, inc="v")
            wv = now("v")
            on("scalar", lambda e, u1_p=u1_p, xn_p=xn_p, stp=stp: e.activation(
                u1_p[:], xn_p[:], Act.Copy, accum_out=stp(8)), wait=wv, inc="s")
            on("scalar", lambda e, u1_p=u1_p, xn_p=xn_p, stp=stp: e.activation(
                u1_p[:], xn_p[:], Act.Square, accum_out=stp(9)), inc="s")
            on("scalar", lambda e, stp=stp: e.mul(stp(10), stp(8), R1024), inc="s")
            on("scalar", lambda e, stp=stp: e.mul(stp(11), stp(9), R1024), inc="s")
            on("scalar", lambda e, stp=stp: e.mul(stp(12), stp(10), stp(10)), inc="s")
            on("scalar", lambda e, stp=stp: e.activation(stp(13), stp(12), Act.Identity,
                                                         scale=-1.0, bias=stp(11)), inc="s")
            on("scalar", lambda e, stp=stp: e.activation(stp(14), stp(13), Act.Ln, bias=EPS), inc="s")
            on("scalar", lambda e, stp=stp: e.activation(stp(15), stp(14), Act.Exp, scale=-0.5), inc="s")
            ws = now("s")
            on("vector", lambda e, u1_p=u1_p, xn_p=xn_p, stp=stp: e.tensor_scalar(
                u1_p[:], xn_p[:], stp(10), stp(15), Alu.subtract, Alu.mult), wait=ws, inc="v")
            mark[f"tile_v{p}"] = fl.cnt["v"]
            wv = now("v") + [("v", mark[f"pf_evac{p}"]), pfree[pfn]]
            for c in range(NCH):
                on("tensor", lambda e, c=c, u1_p=u1_p, PT=PT: e.matmul(
                    PT[:, c * 128:(c + 1) * 128], u1_p[:, c * 128:(c + 1) * 128], idn[:],
                    is_transpose=True, start=(c % 4 == 0), stop=(c % 4 == 3)),
                   wait=wv if c == 0 else None, inc="t" if c == NCH - 1 else None)
            mark[f"tile_t{p}"] = fl.cnt["t"]
            wt = now("t")
            for c in range(NCH):
                on("vector", lambda e, c=c, tl=tl, PT=PT: e.tensor_scalar(
                    A_x1T[:, c * BLK + tl * 128: c * BLK + tl * 128 + 128],
                    PT[:, c * 128:(c + 1) * 128], pv(0, c), pv(1, c),
                    Alu.mult, Alu.add), wait=wt if c == 0 else None, inc="v")
            mark[f"pf_evac{p}"] = fl.cnt["v"]
            pfree[pfn] = ("v", fl.cnt["v"])

    wkey_d, wval_d, wrec_d, wout_d = g("wkey_d"), g("wval_d"), g("wrec_d"), g("wout_d")
    fwkey_d, fwrec_d, fwval_d = g("fwkey_d"), g("fwrec_d"), g("fwval_d")
    wgateT_d, wdownT_d = g("wgateT_d"), g("wdownT_d")
    wbuf, wbuf2 = g("wbuf"), g("wbuf2")
    sgx, frs, yt = g("sgx"), g("frs"), g("yt")
    XNTs, Hms = g("XNTs"), g("Hms")
    z_own, oh1k_own, oh2k_own, cmb, gsc, rsc = (g("z_own"), g("oh1k_own"),
                                                g("oh2k_own"), g("cmb"), g("gsc"), g("rsc"))
    wbufB, wbuf2B, absB, SPLITG = g("wbufB"), g("wbuf2B"), g("absB"), g("SPLITG")
    absF = g("absF")
    fwkeyHL_d, fwvalHL_d, wgateHL_d, wdownHL_d = (
        g("fwkeyHL_d"), g("fwvalHL_d"), g("wgateHL_d"), g("wdownHL_d"))
    BF16 = mybir.dt.bfloat16
    A_B = A_x1T.bitcast(BF16)
    C_B = C_k.bitcast(BF16)
    sgxB = sgx.bitcast(BF16)

    pfree = {n: ("v", 0) for n in ("P_A", "P_B", "P_C", "P_D", "P_E", "P_F")}
    PS = {"P_A": P_A, "P_B": P_B, "P_C": P_C, "P_D": P_D, "P_E": P_E, "P_F": P_F}
    wguard = {0: 0, 1: 0}
    wguard2 = {0: 0, 1: 0}
    wguard3 = [0]
    wguard4 = [0]
    wgF3k = [0]
    wgF3v = [0]
    WQE = {0: "gpsimd", 1: "sync", 2: "scalar"}
    wcnt = [0]
    wcnt2 = [0]

    def gs(i):
        return gsc[:, i:i + 1]

    def mix_into_B(pidx, c0, c1, wextra=None, src=None, bnd=None):
        src = A_x1T if src is None else src
        bnd = bnd1 if bnd is None else bnd
        for c in range(NCH):
            b0 = c * BLK
            on("vector", lambda e, c=c, b0=b0: e.tensor_sub(
                B_mix[:, b0 + max(c0, 1):b0 + c1], src[:, b0 + max(c0, 1):b0 + c1],
                src[:, b0 + max(c0, 1) - 1:b0 + c1 - 1]),
               wait=wextra if c == 0 else None, inc="v")
        if c0 == 0:
            for c in range(NCH):
                b0 = c * BLK
                on("vector", lambda e, c=c, b0=b0: e.tensor_sub(
                    B_mix[:, b0:b0 + 1], src[:, b0:b0 + 1], bnd[:, c:c + 1]), inc="v")
        for c in range(NCH):
            b0 = c * BLK
            on("vector", lambda e, c=c, b0=b0: e.scalar_tensor_tensor(
                B_mix[:, b0 + max(c0, 1):b0 + c1], B_mix[:, b0 + max(c0, 1):b0 + c1],
                pv(pidx, c), src[:, b0 + max(c0, 1) - 1:b0 + c1 - 1],
                Alu.mult, Alu.add), inc="v")
        if c0 == 0:
            on("vector", lambda e: None, wait=now("v"))
            for c in range(NCH):
                b0 = c * BLK
                on("vector", lambda e, c=c, b0=b0: e.scalar_tensor_tensor(
                    B_mix[:, b0:b0 + 1], B_mix[:, b0:b0 + 1], pv(pidx, c),
                    bnd[:, c:c + 1], Alu.mult, Alu.add), inc="v")

    def proj_from_B(Wd, dst, c0, c1, sigmoid=False, evac_wait=None, nsl=3):
        w = c1 - c0
        wsrc = now("v")
        for co in range(NCH):
            sl = wcnt[0] % nsl
            wcnt[0] += 1
            wb_ap = wbuf[sl] if sl < 2 else absF[:, 0:D]
            g0 = wguard[sl] if sl < 2 else wguard3[0]
            qn = "qw2" if sl == 2 else f"qw{sl}"
            on(WQE[sl], lambda e, co=co, wb_ap=wb_ap, Wd=Wd: e.dma_start(
                out=wb_ap.rearrange("p (ci o) -> p ci o", o=128),
                in_=Wd.rearrange("(ci p) o -> p ci o", p=128)[:, :, co * 128:(co + 1) * 128]),
               wait=[("t", g0)] + (SPLITG if sl == 2 else []), inc=qn, k=16)
            pname = "P_A" if co % 2 == 0 else "P_B"
            pa = PS[pname]
            wt = now(qn) + wsrc + [pfree[pname]]
            for ci in range(NCH):
                on("tensor", lambda e, ci=ci, co=co, wb_ap=wb_ap, pa=pa, w=w: e.matmul(
                    pa[:, 0:w], wb_ap[:, ci * 128:(ci + 1) * 128],
                    B_mix[:, ci * BLK + c0:ci * BLK + c1],
                    start=(ci == 0), stop=(ci == NCH - 1)),
                   wait=wt if ci == 0 else None, inc="t" if ci == NCH - 1 else None)
            if sl < 2:
                wguard[sl] = fl.cnt["t"]
            else:
                wguard3[0] = fl.cnt["t"]
            wte = now("t") + (evac_wait or [])
            if sigmoid:
                on("scalar", lambda e, co=co, pa=pa, w=w: e.activation(
                    dst[:, co * BLK + c0:co * BLK + c1], pa[:, 0:w], Act.Sigmoid),
                   wait=wte, inc="s")
            else:
                on("scalar", lambda e, co=co, pa=pa, w=w: e.copy(
                    dst[:, co * BLK + c0:co * BLK + c1], pa[:, 0:w]), wait=wte, inc="s")
            pfree[pname] = ("s", fl.cnt["s"])

    def stageB(b):
        rng = (384, 512) if b == 0 else (0, 512)
        wA = [("v", mark["pf_evac0"]), ("v", mark["pf_evac1"])]
        nsl = 2
        mix_into_B(6, 0, BLK, wextra=wA + now("t"))
        proj_from_B(wkey_d, C_k, 0, BLK, nsl=nsl)
        mix_into_B(7, 0, BLK, wextra=now("t"))
        proj_from_B(wval_d, D_v, 0, BLK, nsl=nsl)
        if rng is not None:
            mix_into_B(8, rng[0], rng[1], wextra=now("t"))
        on("vector", lambda e: e.tensor_copy(bnd1[:, 0:NCH], A_x1T[:, BLK - 1::BLK]), inc="v")
        if b == 0:
            on("vector", lambda e: e.tensor_scalar(bnd1[:, 0:NCH], bnd1[:, 0:NCH],
                                                   mh[:, 0:1], None, Alu.mult),
               wait=now("v"), inc="v")
        mark["bnd1"] = fl.cnt["v"]
        if rng is not None:
            proj_from_B(wrec_d, A_x1T, rng[0], rng[1], sigmoid=True,
                        evac_wait=[("v", mark["bnd1"])], nsl=nsl)

    def stageS(b):
        rng = (384, 512) if b == 0 else (0, 512)
        wfree = [("t", wguard[0]), ("t", wguard[1])]
        ws_sr = now("s")
        if b == 0:
            on("vector", lambda e: e.memset(F_S[:, 0::BLK + 1], 0.0), inc="v")
            on("vector", lambda e: e.memset(G_Z[:, 0::BLK + 1], 0.0), inc="v")
        for c in range(NCH):
            f0 = c * (BLK + 1)
            on("scalar", lambda e, c=c: e.activation(
                B_mix[:, c * BLK:(c + 1) * BLK], C_k[:, c * BLK:(c + 1) * BLK], Act.Exp),
               wait=wfree if c == 0 else None, inc="s")
            ws = now("s")
            if b < 1:
                on("vector", lambda e, c=c: e.tensor_scalar(
                    B_mix[:, c * BLK:(c + 1) * BLK], B_mix[:, c * BLK:(c + 1) * BLK],
                    mh[:, 0:1], None, Alu.mult), wait=ws, inc="v")
                ws = None
            on("vector", lambda e, c=c: e.tensor_mul(
                D_v[:, c * BLK:(c + 1) * BLK], B_mix[:, c * BLK:(c + 1) * BLK],
                D_v[:, c * BLK:(c + 1) * BLK]), wait=ws, inc="v")
            if b > 0:
                on("vector", lambda e, f0=f0: e.tensor_copy(F_S[:, f0:f0 + 1],
                                                            F_S[:, f0 + BLK:f0 + BLK + 1]), inc="v")
                on("vector", lambda e, f0=f0: e.tensor_copy(G_Z[:, f0:f0 + 1],
                                                            G_Z[:, f0 + BLK:f0 + BLK + 1]), inc="v")
            initF = 0.0 if b == 0 else F_S[:, f0 + BLK:f0 + BLK + 1]
            initG = 0.0 if b == 0 else G_Z[:, f0 + BLK:f0 + BLK + 1]
            on("vector", lambda e, c=c, f0=f0, initF=initF: e.tensor_tensor_scan(
                F_S[:, f0 + 1:f0 + 1 + BLK],
                der(0, c).broadcast_to((128, BLK)),
                D_v[:, c * BLK:(c + 1) * BLK], initF, Alu.mult, Alu.add), inc="v")
            on("vector", lambda e, c=c, f0=f0, initG=initG: e.tensor_tensor_scan(
                G_Z[:, f0 + 1:f0 + 1 + BLK],
                der(0, c).broadcast_to((128, BLK)),
                B_mix[:, c * BLK:(c + 1) * BLK], initG, Alu.mult, Alu.add), inc="v")
            if rng is None:
                continue
            c0, c1 = rng
            on("vector", lambda e, c=c, f0=f0, c0=c0, c1=c1: e.scalar_tensor_tensor(
                D_v[:, c * BLK + c0:c * BLK + c1], D_v[:, c * BLK + c0:c * BLK + c1],
                der(1, c), F_S[:, f0 + c0:f0 + c1], Alu.mult, Alu.add), inc="v")
            on("vector", lambda e, c=c, f0=f0, c0=c0, c1=c1: e.scalar_tensor_tensor(
                B_mix[:, c * BLK + c0:c * BLK + c1], B_mix[:, c * BLK + c0:c * BLK + c1],
                der(1, c), G_Z[:, f0 + c0:f0 + c1], Alu.mult, Alu.add), inc="v")
            if b == 0:
                on("vector", lambda e, c=c, c0=c0, c1=c1: e.tensor_scalar(
                    B_mix[:, c * BLK + c0:c * BLK + c1], B_mix[:, c * BLK + c0:c * BLK + c1],
                    mhinv[:, 0:1], None, Alu.add), inc="v")
            on("vector", lambda e, c=c, c0=c0, c1=c1: e.reciprocal(
                C_k[:, c * BLK + c0:c * BLK + c1], B_mix[:, c * BLK + c0:c * BLK + c1]), inc="v")
            on("vector", lambda e, c=c, c0=c0, c1=c1: e.tensor_mul(
                D_v[:, c * BLK + c0:c * BLK + c1], D_v[:, c * BLK + c0:c * BLK + c1],
                C_k[:, c * BLK + c0:c * BLK + c1]), inc="v")
            on("vector", lambda e, c=c, c0=c0, c1=c1: e.tensor_mul(
                D_v[:, c * BLK + c0:c * BLK + c1], D_v[:, c * BLK + c0:c * BLK + c1],
                A_x1T[:, c * BLK + c0:c * BLK + c1]), wait=ws_sr if c == 0 else None, inc="v")
            mark[f"osr{c}"] = fl.cnt["v"]

    OPS = {0: [("P_E", 0), ("P_E", 512)], 1: [("P_F", 0), ("P_F", 512)],
           2: [("P_A", 0), ("P_B", 0)], 3: [("P_C", 0), ("P_D", 0)]}

    def stageO(b):
        rng = (384, 512) if b == 0 else (0, 512)
        jjs = [3] if b == 0 else [0, 1, 2, 3]
        wps = [pfree[p] for p in ("P_A", "P_B", "P_C", "P_D", "P_E", "P_F")]
        for ci in range(NCH):
            sl = wcnt2[0] % 2
            wcnt2[0] += 1
            wb_ap = wbuf2[sl] if sl < 2 else absF[:, D:2 * D]
            g0 = wguard2[sl] if sl < 2 else wguard4[0]
            qn = "qv2" if sl == 2 else f"qv{sl}"
            on(WQE[sl], lambda e, ci=ci, wb_ap=wb_ap: e.dma_start(
                out=wb_ap[:], in_=wout_d[ci * 128:(ci + 1) * 128, :]),
               wait=[("t", g0)] + (SPLITG if sl == 2 else []), inc=qn, k=16)
            wt = now(qn) + [("v", mark[f"osr{ci}"])] + (wps if ci == 0 else [])
            first = True
            for jj in jjs:
                for half in (0, 1):
                    pn, off = OPS[jj][half]
                    pa = PS[pn]
                    on("tensor", lambda e, ci=ci, jj=jj, half=half, pa=pa, off=off, wb_ap=wb_ap: e.matmul(
                        pa[:, off:off + 512], D_v[:, ci * BLK + jj * 128:ci * BLK + jj * 128 + 128],
                        wb_ap[:, half * 512:(half + 1) * 512],
                        start=(ci == 0), stop=(ci == NCH - 1)),
                       wait=wt if first else None,
                       inc="t" if (jj == jjs[-1] and half == 1) else None)
                    first = False
            if sl < 2:
                wguard2[sl] = fl.cnt["t"]
            else:
                wguard4[0] = fl.cnt["t"]
        wte = now("t")
        for jj in jjs:
            t = b * TPB + jj
            (pn0, off0), (pn1, off1) = OPS[jj]
            on("vector", lambda e, jj=jj, t=t: e.tensor_scalar(
                sgx[:], xt[jj][:], m0_all[:, t:t + 1], rsig0_all[:, t:t + 1],
                Alu.subtract, Alu.mult), wait=wte, inc="v")
            on("vector", lambda e, jj=jj, pn0=pn0, off0=off0: e.tensor_add(
                x2t[jj][:, 0:512], PS[pn0][:, off0:off0 + 512], sgx[:, 0:512]), inc="v")
            on("vector", lambda e, jj=jj, pn1=pn1, off1=off1: e.tensor_add(
                x2t[jj][:, 512:1024], PS[pn1][:, off1:off1 + 512], sgx[:, 512:1024]), inc="v")
            pfree[OPS[jj][0][0]] = ("v", fl.cnt["v"])
            pfree[OPS[jj][1][0]] = ("v", fl.cnt["v"])

    def ln_norm(src_ap, wslot, bslot, dst_cols, mask_last=False, p=0):
        xn_p = xnt if p == 0 else xnt2
        u1_p = u1t if p == 0 else u1t2
        PT = P_F if p == 0 else P_E
        pfn = "P_F" if p == 0 else "P_E"
        sto = p * 16

        def stp(i, sto=sto):
            return stat[:, sto + i:sto + i + 1]
        on("scalar", lambda e: e.activation(u1_p[:], src_ap, Act.Copy, accum_out=stp(0)),
           wait=[("v", mark[f"tile_v{p}"]), ("t", mark[f"tile_t{p}"])] + now("v"), inc="s")
        on("scalar", lambda e: e.activation(u1_p[:], src_ap, Act.Square, accum_out=stp(1)), inc="s")
        on("scalar", lambda e: e.mul(stp(2), stp(0), R1024), inc="s")
        on("scalar", lambda e: e.mul(stp(3), stp(1), R1024), inc="s")
        on("scalar", lambda e: e.mul(stp(4), stp(2), stp(2)), inc="s")
        on("scalar", lambda e: e.activation(stp(5), stp(4), Act.Identity,
                                            scale=-1.0, bias=stp(3)), inc="s")
        on("scalar", lambda e: e.activation(stp(6), stp(5), Act.Ln, bias=EPS), inc="s")
        on("scalar", lambda e: e.activation(stp(7), stp(6), Act.Exp, scale=-0.5), inc="s")
        ws = now("s")
        on("vector", lambda e: e.tensor_scalar(xn_p[:], src_ap, stp(2), stp(7),
                                               Alu.subtract, Alu.mult), wait=ws, inc="v")
        mark[f"tile_v{p}"] = fl.cnt["v"]
        wv = now("v") + [("v", mark[f"pf_evac{p}"])] + [pfree[pfn]]
        for c in range(NCH):
            on("tensor", lambda e, c=c: e.matmul(
                PT[:, c * 128:(c + 1) * 128], xn_p[:, c * 128:(c + 1) * 128], idn[:],
                is_transpose=True, start=(c % 4 == 0), stop=(c % 4 == 3)),
               wait=wv if c == 0 else None, inc="t" if c == NCH - 1 else None)
        mark[f"tile_t{p}"] = fl.cnt["t"]
        wt = now("t")
        for c in range(NCH):
            on("vector", lambda e, c=c: e.tensor_scalar(
                C_k[:, c * BLK + dst_cols[0]: c * BLK + dst_cols[1]],
                PT[:, c * 128:(c + 1) * 128], pv(wslot, c), pv(bslot, c),
                Alu.mult, Alu.add), wait=wt if c == 0 else None, inc="v")
        mark[f"pf_evac{p}"] = fl.cnt["v"]
        pfree[pfn] = ("v", fl.cnt["v"])
        if mask_last:
            on("vector", lambda e: None, wait=now("v"))
            for c in range(NCH):
                on("vector", lambda e, c=c: e.tensor_scalar(
                    C_k[:, c * BLK + 511:c * BLK + 512], C_k[:, c * BLK + 511:c * BLK + 512],
                    mh[:, 0:1], None, Alu.mult), inc="v")

    def stageL2(b):
        jjs = [3] if b == 0 else [0, 1, 2, 3]
        for jj in jjs:
            ln_norm(x2t[jj][:], 2, 3, (jj * 128, jj * 128 + 128),
                    mask_last=(b == 0 and jj == 3), p=jj % 2)
        if b == 0:
            on("vector", lambda e: e.tensor_copy(bnd2[:, 0:NCH], C_k[:, BLK - 1::BLK]),
               wait=now("v"), inc="v")

    def stageF(b):
        wgF3k[0] = max(wgF3k[0], fl.cnt["t"])
        wgF3v[0] = max(wgF3v[0], fl.cnt["t"])
        mix_into_B(10, 0, BLK, wextra=now("t"), src=C_k, bnd=bnd2)
        wsrc = now("v")
        for ci in range(NCH):
            sl = wcnt2[0] % 2
            wcnt2[0] += 1
            wb_ap = wbuf2[sl] if sl < 2 else absF[:, D:2 * D]
            g0 = wguard2[sl] if sl < 2 else wguard4[0]
            qn = "qv2" if sl == 2 else f"qv{sl}"
            on(WQE[sl], lambda e, ci=ci, wb_ap=wb_ap: e.dma_start(
                out=wb_ap[:], in_=fwrec_d[ci * 128:(ci + 1) * 128, :]),
               wait=[("t", g0)] + (SPLITG if sl == 2 else []), inc=qn, k=16)
            wt = now(qn) + wsrc + ([pfree[p] for p in ("P_A", "P_B", "P_C", "P_D", "P_E", "P_F")] if ci == 0 else [])
            first = True
            for jj in range(TPB):
                for half in (0, 1):
                    pn, off = OPS[jj][half]
                    on("tensor", lambda e, ci=ci, jj=jj, half=half, pn=pn, off=off, wb_ap=wb_ap: e.matmul(
                        PS[pn][:, off:off + 512], B_mix[:, ci * BLK + jj * 128:ci * BLK + jj * 128 + 128],
                        wb_ap[:, half * 512:(half + 1) * 512],
                        start=(ci == 0), stop=(ci == NCH - 1)),
                       wait=wt if first else None, inc="t" if (jj == TPB - 1 and half == 1) else None)
                    first = False
            if sl < 2:
                wguard2[sl] = fl.cnt["t"]
            else:
                wguard4[0] = fl.cnt["t"]
        wte = now("t")
        for jj in range(TPB):
            (pn0, off0), (pn1, off1) = OPS[jj]
            on("scalar", lambda e, jj=jj, pn0=pn0, off0=off0: e.activation(
                frs[jj][:, 0:512], PS[pn0][:, off0:off0 + 512], Act.Sigmoid),
               wait=wte, inc="s")
            on("scalar", lambda e, jj=jj, pn1=pn1, off1=off1: e.activation(
                frs[jj][:, 512:1024], PS[pn1][:, off1:off1 + 512], Act.Sigmoid), inc="s")
            pfree[OPS[jj][0][0]] = ("s", fl.cnt["s"])
            pfree[OPS[jj][1][0]] = ("s", fl.cnt["s"])
        mix_into_B(9, 0, BLK, wextra=now("t"), src=C_k, bnd=bnd2)
        on("vector", lambda e: e.tensor_copy(bnd2[:, 0:NCH], C_k[:, BLK - 1::BLK]), inc="v")
        mark["vC"] = fl.cnt["v"]
        u1tB = u1t.bitcast(BF16)
        SCR = {0: (sgx[:, 0:256], sgx[:, 256:512], sgxB, 1024),
               1: (sgx[:, 768:1024], u1t[:, 0:256], u1tB, 512)}
        NCO = DH // 128
        for hh in (0, 1):
            h0 = hh * 256
            wb = now("t", "v") + SPLITG
            for c in range(NCH):
                on("scalar", lambda e, c=c, h0=h0: e.copy(
                    absB[:, c * 256:(c + 1) * 256], B_mix[:, c * BLK + h0:c * BLK + h0 + 256]),
                   wait=wb, inc="s")
                on("vector", lambda e, c=c, h0=h0: e.tensor_tensor(
                    absB[:, DM + c * 256:DM + (c + 1) * 256],
                    B_mix[:, c * BLK + h0:c * BLK + h0 + 256],
                    absB[:, c * 256:(c + 1) * 256], Alu.subtract),
                   wait=now("s"), inc="v")
            ws_x = now("s", "v")
            tfk = {}
            pm = {}

            def emit_fk(co):
                sl = wcnt[0] % 2
                wcnt[0] += 1
                wb_ap = wbufB[sl] if sl < 2 else C_B[:, 0:2 * D]
                g0 = wguard[sl] if sl < 2 else wgF3k[0]
                qn = "qw2" if sl == 2 else f"qw{sl}"
                gw = [("t", g0)] + SPLITG + ([("v", mark["vC"])] if sl == 2 else [])
                on(WQE[sl], lambda e, co=co, wb_ap=wb_ap: e.dma_start(
                    out=wb_ap[:, 0:2 * D].rearrange("p (ci o) -> p ci o", o=256),
                    in_=fwkeyHL_d.rearrange("(ci p) o -> p ci o", p=128)[:, :, co * 256:(co + 1) * 256]),
                   wait=gw, inc=qn, k=16)
                p = co % 2
                bk = P_A if p == 0 else P_B
                wt = now(qn) + (ws_x if co < 2 else []) \
                    + [pm.get(co - 2, pfree["P_A" if p == 0 else "P_B"])]
                first = True
                for ci in range(NCH):
                    for wo, ao in ((0, 0), (0, DM), (128, 0)):
                        on("tensor", lambda e, ci=ci, wb_ap=wb_ap, bk=bk, wo=wo, ao=ao: e.matmul(
                            bk[:, 0:256], wb_ap[:, ci * 256 + wo:ci * 256 + wo + 128],
                            absB[:, ao + ci * 256:ao + ci * 256 + 256],
                            start=(ci == 0 and wo == 0 and ao == 0),
                            stop=(ci == NCH - 1 and wo == 128)),
                           wait=wt if first else None,
                           inc="t" if (ci == NCH - 1 and wo == 128) else None)
                        first = False
                if sl < 2:
                    wguard[sl] = fl.cnt["t"]
                else:
                    wgF3k[0] = fl.cnt["t"]
                tfk[co] = fl.cnt["t"]

            def emit_chain_vk(co):
                p = co % 2
                R, M, PB, base = SCR[p]
                bk = P_A if p == 0 else P_B
                on("scalar", lambda e, R=R, bk=bk: e.activation(R, bk[:, 0:256], Act.Relu),
                   wait=[("t", tfk[co]), ("t", mark.get(f"fkscr{p}", 0))], inc="s")
                on("vector", lambda e, R=R, M=M, bk=bk: e.tensor_mul(M, R, bk[:, 0:256]),
                   wait=now("s"), inc="v")
                pm[co] = ("v", fl.cnt["v"])
                on("scalar", lambda e, M=M, PB=PB, base=base: e.copy(PB[:, base:base + 256], M),
                   wait=now("v"), inc="s")
                on("vector", lambda e, M=M, PB=PB, base=base: e.tensor_tensor(
                    PB[:, base + 256:base + 512], M, PB[:, base:base + 256], Alu.subtract),
                   wait=now("s"), inc="v")
                wv = now("v")
                sl2 = wcnt2[0] % 2
                wcnt2[0] += 1
                wb2_ap = wbuf2B[sl2] if sl2 < 2 else C_B[:, 2 * D:4 * D]
                g0 = wguard2[sl2] if sl2 < 2 else wgF3v[0]
                qn = "qv2" if sl2 == 2 else f"qv{sl2}"
                gw = [("t", g0)] + SPLITG + ([("v", mark["vC"])] if sl2 == 2 else [])
                on(WQE[sl2], lambda e, co=co, wb2_ap=wb2_ap: e.dma_start(
                    out=wb2_ap[:, 0:2 * D], in_=fwvalHL_d[co * 128:(co + 1) * 128, :]),
                   wait=gw, inc=qn, k=16)
                first = True
                for jl in (0, 1):
                    pvk = P_E if jl == 0 else P_F
                    for half in (0, 1):
                        for which, wo in ((0, 0), (0, D), (1, 0)):
                            on("tensor", lambda e, jl=jl, half=half, pvk=pvk, wb2_ap=wb2_ap,
                               which=which, wo=wo, PB=PB, base=base: e.matmul(
                                pvk[:, half * 512:(half + 1) * 512],
                                PB[:, base + which * 256 + jl * 128:base + which * 256 + jl * 128 + 128],
                                wb2_ap[:, wo + half * 512:wo + half * 512 + 512],
                                start=(co == 0 and which == 0 and wo == 0),
                                stop=(co == NCO - 1 and which == 1)),
                               wait=(wv + now(qn) + ([pfree["P_E"], pfree["P_F"]] if co == 0 else [])) if first else None,
                               inc="t" if (jl == 1 and half == 1 and which == 1) else None)
                            first = False
                if sl2 < 2:
                    wguard2[sl2] = fl.cnt["t"]
                else:
                    wgF3v[0] = fl.cnt["t"]
                mark[f"fkscr{p}"] = fl.cnt["t"]

            for co in range(NCO):
                emit_fk(co)
                if co >= 1:
                    emit_chain_vk(co - 1)
            emit_chain_vk(NCO - 1)
            pfree["P_A"], pfree["P_B"] = pm[NCO - 2], pm[NCO - 1]
            wte = now("t")
            for jl in (0, 1):
                jj = hh * 2 + jl
                pvk = P_E if jl == 0 else P_F
                on("vector", lambda e, jj=jj, pvk=pvk: e.tensor_mul(
                    u1t[:], frs[jj][:], pvk[:]), wait=wte, inc="v")
                on("vector", lambda e, jj=jj: e.tensor_add(
                    x2t[jj][:], u1t[:], x2t[jj][:]), wait=now("v"), inc="v")
                pfree["P_E" if jl == 0 else "P_F"] = ("v", fl.cnt["v"])

    def gates_combine(b):
        def slot(i):
            return rsc[:, i * 8:(i + 1) * 8]

        def vsw():
            on("vector", lambda e: None, wait=now("v"))

        def ssw():
            on("scalar", lambda e: None, wait=now("s"))
        for jj in range(TPB):
            t = b * TPB + jj
            to = (b - 1) * TPB + jj
            on("vector", lambda e, to=to, t=t: e.tensor_scalar(
                slot(0), z_own[:, to * 8:(to + 1) * 8], rsig0_all[:, t:t + 1], None,
                Alu.mult), wait=now("s"), inc="v")
            vsw()
            on("vector", lambda e: e.tensor_reduce(gs(0), slot(0), mybir.AxisListType.X,
                                                   Alu.max, negate=True), inc="v")
            wv = now("v")
            on("scalar", lambda e: e.activation(slot(1), slot(0), Act.Exp, bias=gs(0),
                                                accum_out=gs(1)), wait=wv, inc="s")
            ws = now("s")
            on("vector", lambda e: e.reciprocal(gs(2), gs(1)), wait=ws, inc="v")
            wv = now("v")
            on("scalar", lambda e: e.mul(slot(2), slot(1), gs(2)), wait=wv, inc="s")
            ws = now("s")
            on("vector", lambda e, to=to: e.tensor_mul(slot(3), oh1k_own[:, to * 8:(to + 1) * 8],
                                                       slot(2)), wait=ws, inc="v")
            vsw()
            on("vector", lambda e: e.tensor_reduce(gs(3), slot(3), mybir.AxisListType.X, Alu.add), inc="v")
            vsw()
            on("vector", lambda e, to=to: e.tensor_mul(slot(4), oh2k_own[:, to * 8:(to + 1) * 8],
                                                       slot(2)), inc="v")
            vsw()
            on("vector", lambda e: e.tensor_reduce(gs(4), slot(4), mybir.AxisListType.X, Alu.add), inc="v")
            wv = now("v")
            on("scalar", lambda e: e.activation(gs(5), gs(3), Act.Identity, bias=gs(4)),
               wait=wv, inc="s")
            ssw()
            on("scalar", lambda e: e.activation(gs(6), gs(5), Act.Identity, bias=1e-6), inc="s")
            ws = now("s")
            on("vector", lambda e: e.reciprocal(gs(7), gs(6)), wait=ws, inc="v")
            vsw()
            on("vector", lambda e: e.tensor_scalar(gs(8), gs(3), gs(7), None, Alu.mult), inc="v")
            vsw()
            on("vector", lambda e: e.tensor_scalar(gs(9), gs(4), gs(7), None, Alu.mult), inc="v")
            wv = now("v")
            on("scalar", lambda e, to=to: e.mul(slot(5), oh1k_own[:, to * 8:(to + 1) * 8], gs(8)),
               wait=wv, inc="s")
            ssw()
            on("scalar", lambda e, to=to: e.mul(slot(6), oh2k_own[:, to * 8:(to + 1) * 8], gs(9)), inc="s")
            ws = now("s")
            on("vector", lambda e, to=to: e.tensor_add(cmb[:, to * 8:(to + 1) * 8],
                                                       slot(5), slot(6)), wait=ws, inc="v")
            vsw()

    def stageM(b):
        wguard3[0] = max(wguard3[0], fl.cnt["t"])
        absB3 = absB[:, 0:2 * D]
        gates_combine(b)
        for hh in (0, 1):
            for jl in (0, 1):
                jj = hh * 2 + jl
                t = b * TPB + jj
                on("vector", lambda e, jj=jj, t=t: e.tensor_scalar(
                    xnt[:], xt[jj][:], m0_all[:, t:t + 1], rsig0_all[:, t:t + 1],
                    Alu.subtract, Alu.mult), wait=[("s", mark.get("xnt_used_s", 0))], inc="v")
                wv = now("v") + [pfree["P_F"]]
                for c in range(NCH):
                    on("tensor", lambda e, c=c: e.matmul(
                        P_F[:, c * 128:(c + 1) * 128], xnt[:, c * 128:(c + 1) * 128], idn[:],
                        is_transpose=True, start=(c % 4 == 0), stop=(c % 4 == 3)),
                       wait=wv if c == 0 else None, inc="t" if c == NCH - 1 else None)
                wt = now("t")
                for c in range(NCH):
                    on("scalar", lambda e, c=c, jl=jl: e.copy(
                        XNTs[:, c * 256 + jl * 128:c * 256 + jl * 128 + 128],
                        P_F[:, c * 128:(c + 1) * 128]), wait=wt if c == 0 else None, inc="s")
                mark["xnt_used_s"] = fl.cnt["s"]
                pfree["P_F"] = ("s", fl.cnt["s"])
            ws_x = now("s")
            for c in range(NCH):
                on("scalar", lambda e, c=c: e.copy(
                    A_B[:, 4096 + c * 256:4096 + (c + 1) * 256], XNTs[:, c * 256:(c + 1) * 256]),
                   wait=ws_x, inc="s")
                on("vector", lambda e, c=c: e.tensor_tensor(
                    A_B[:, 6144 + c * 256:6144 + (c + 1) * 256], XNTs[:, c * 256:(c + 1) * 256],
                    A_B[:, 4096 + c * 256:4096 + (c + 1) * 256], Alu.subtract),
                   wait=now("s"), inc="v")
            ws_x = now("s", "v")
            for mo in range(DM // 128):
                sl = mo % 3
                wb_ap = wbufB[sl] if sl < 2 else absB3
                g0 = wguard[sl] if sl < 2 else wguard3[0]
                on(WQE[sl], lambda e, mo=mo, wb_ap=wb_ap: e.dma_start(
                    out=wb_ap[:, 0:2 * D].rearrange("p (ci o) -> p ci o", o=256),
                    in_=wgateHL_d.rearrange("(ci p) o -> p ci o", p=128)[:, :, mo * 256:(mo + 1) * 256]),
                   wait=[("t", g0)] + SPLITG, inc=("qw2" if sl == 2 else f"qw{sl}"), k=16)
                pn = "P_A" if mo % 2 == 0 else "P_B"
                wt = now("qw2" if sl == 2 else f"qw{sl}") + ws_x + [pfree[pn]]
                first = True
                for ci in range(NCH):
                    for wo, ao in ((0, 4096), (0, 6144), (128, 4096)):
                        on("tensor", lambda e, ci=ci, wb_ap=wb_ap, pn=pn, wo=wo, ao=ao: e.matmul(
                            PS[pn][:, 0:256], wb_ap[:, ci * 256 + wo:ci * 256 + wo + 128],
                            A_B[:, ao + ci * 256:ao + ci * 256 + 256],
                            start=(ci == 0 and wo == 0 and ao == 4096),
                            stop=(ci == NCH - 1 and wo == 128)),
                           wait=wt if first else None,
                           inc="t" if (ci == NCH - 1 and wo == 128) else None)
                        first = False
                if sl < 2:
                    wguard[sl] = fl.cnt["t"]
                else:
                    wguard3[0] = fl.cnt["t"]
                wte = now("t")
                on("scalar", lambda e, pn=pn: e.activation(sgx[:, 0:256], PS[pn][:, 0:256],
                                                           Act.Sigmoid),
                   wait=wte + [("v", mark.get("sgm_used", 0))], inc="s")
                on("vector", lambda e, pn=pn: e.tensor_mul(
                    sgx[:, 256:512], sgx[:, 0:256], PS[pn][:, 0:256]),
                   wait=now("s"), inc="v")
                pfree[pn] = ("v", fl.cnt["v"])
                on("scalar", lambda e, mo=mo: e.copy(
                    C_B[:, mo * 256:(mo + 1) * 256], sgx[:, 256:512]), wait=now("v"), inc="s")
                on("vector", lambda e, mo=mo: e.tensor_tensor(
                    C_B[:, 4096 + mo * 256:4096 + (mo + 1) * 256], sgx[:, 256:512],
                    C_B[:, mo * 256:(mo + 1) * 256], Alu.subtract),
                   wait=now("s"), inc="v")
                mark["sgm_used"] = fl.cnt["v"]
                mark[f"hms{mo}"] = fl.cnt["v"]
            DB = {0: [["P_E", "P_E"], ["P_F", "P_F"]],
                  1: [["P_A", "P_B"], ["P_C", "P_D"]]}

            def dslice(p, jl, half):
                nm = DB[p][jl][half]
                off = half * 512 if p == 0 else 0
                return PS[nm][:, off:off + 512]
            for e_ in range(NE):
                p = e_ % 2
                bset = ["P_E", "P_F"] if p == 0 else ["P_A", "P_B", "P_C", "P_D"]
                for mo in range(DM // 128):
                    k = e_ * (DM // 128) + mo
                    sl = k % 3
                    wb_ap = wbuf2B[sl] if sl < 2 else absB3
                    g0 = wguard2[sl] if sl < 2 else wguard3[0]
                    on(WQE[sl], lambda e, e_=e_, mo=mo, wb_ap=wb_ap: e.dma_start(
                        out=wb_ap[:, 0:2 * D], in_=wdownHL_d[e_, mo * 128:(mo + 1) * 128, :]),
                       wait=[("t", g0)] + SPLITG, inc=("qv2" if sl == 2 else f"qv{sl}"), k=16)
                    wt = now("qv2" if sl == 2 else f"qv{sl}") + [("v", mark[f"hms{mo}"])] \
                        + ([pfree[nm] for nm in bset] if mo == 0 else [])
                    first = True
                    for jl in (0, 1):
                        for half in (0, 1):
                            for ho, wo in ((0, 0), (0, D), (4096, 0)):
                                on("tensor", lambda e, jl=jl, half=half, p=p, wb_ap=wb_ap, mo=mo, ho=ho, wo=wo: e.matmul(
                                    dslice(p, jl, half),
                                    C_B[:, ho + mo * 256 + jl * 128:ho + mo * 256 + jl * 128 + 128],
                                    wb_ap[:, wo + half * 512:wo + half * 512 + 512],
                                    start=(mo == 0 and ho == 0 and wo == 0),
                                    stop=(mo == DM // 128 - 1 and ho == 4096)),
                                   wait=wt if first else None,
                                   inc="t" if (jl == 1 and half == 1 and ho == 4096) else None)
                                first = False
                    if sl < 2:
                        wguard2[sl] = fl.cnt["t"]
                    else:
                        wguard3[0] = fl.cnt["t"]
                wte = now("t")
                for jl in (0, 1):
                    jj = hh * 2 + jl
                    to = (b - 1) * TPB + jj
                    cc = cmb[:, to * 8 + e_:to * 8 + e_ + 1]
                    if p == 0:
                        pvk = P_E if jl == 0 else P_F
                        if e_ == 0:
                            on("vector", lambda e, jj=jj, pvk=pvk, cc=cc: e.tensor_scalar(
                                yt[jj], pvk[:], cc, None, Alu.mult), wait=wte, inc="v")
                        else:
                            on("vector", lambda e, jj=jj, pvk=pvk, cc=cc: e.scalar_tensor_tensor(
                                yt[jj], pvk[:], cc, yt[jj], Alu.mult, Alu.add), wait=wte, inc="v")
                        pfree[DB[0][jl][0]] = ("v", fl.cnt["v"])
                    else:
                        for half in (0, 1):
                            on("vector", lambda e, jj=jj, jl=jl, half=half, cc=cc: e.scalar_tensor_tensor(
                                yt[jj][:, half * 512:(half + 1) * 512], dslice(1, jl, half), cc,
                                yt[jj][:, half * 512:(half + 1) * 512], Alu.mult, Alu.add),
                               wait=wte if (jl == 0 and half == 0) else None, inc="v")
                            pfree[DB[1][jl][half]] = ("v", fl.cnt["v"])
            for jl in (0, 1):
                jj = hh * 2 + jl
                wv = now("v")
                on("scalar", lambda e, jj=jj: e.activation(sgx[:], yt[jj], Act.Sigmoid),
                   wait=wv + [("v", mark.get("sgm_used", 0))], inc="s")
                ws = now("s")
                on("vector", lambda e, jj=jj: e.tensor_mul(yt[jj], sgx[:], yt[jj]),
                   wait=ws, inc="v")
                mark["sgm_used"] = fl.cnt["v"]
                on("vector", lambda e, jj=jj: e.tensor_mul(yt[jj], yt[jj], x2t[jj][:]), inc="v")
                on("vector", lambda e, jj=jj: e.tensor_add(yt[jj], yt[jj], xt[jj][:]), inc="v")
                wv = now("v")
                row0 = (b - 1) * BLK + jj * 128
                on("sync", lambda e, jj=jj, row0=row0: e.dma_start(
                    out=y_out[row0:row0 + 128, :], in_=yt[jj]), wait=wv, inc=f"qo{jj}", k=16)

    for b in range(NBLK):
        stageA(b)
        stageB(b)
        if DEV_STOP == "B":
            mark["x_users"] = fl.cnt["v"]
            continue
        stageS(b)
        if DEV_STOP == "S":
            mark["x_users"] = fl.cnt["v"]
            continue
        stageO(b)
        if DEV_STOP != "O":
            stageL2(b)
        if b >= 1 and DEV_STOP is None:
            stageF(b)
            stageM(b)
            on("vector", lambda e: None, wait=now("qo0", "qo1", "qo2", "qo3"))
        mark["x_users"] = fl.cnt["v"]





def _host_prep(inputs):
    x = np.ascontiguousarray(np.asarray(inputs["L0_attn_x"], np.float32).reshape(NTOK, D))
    pvec = np.stack([
        np.asarray(inputs["L0_attn_ln1_w"], np.float32),
        np.asarray(inputs["L0_attn_ln1_b"], np.float32),
        np.asarray(inputs["L0_attn_ln2_w"], np.float32),
        np.asarray(inputs["L0_attn_ln2_b"], np.float32),
        np.asarray(inputs["L0_attn_time_decay"], np.float32),
        np.asarray(inputs["L0_attn_time_first"], np.float32),
        np.asarray(inputs["L0_attn_time_mix_k"], np.float32).reshape(D),
        np.asarray(inputs["L0_attn_time_mix_v"], np.float32).reshape(D),
        np.asarray(inputs["L0_attn_time_mix_r"], np.float32).reshape(D),
        np.asarray(inputs["L0_attn_f_time_mix_k"], np.float32).reshape(D),
        np.asarray(inputs["L0_attn_f_time_mix_r"], np.float32).reshape(D),
        np.zeros(D, np.float32),
    ], axis=0)
    pvec = pvec.reshape(12, NCH, 128).transpose(2, 0, 1).reshape(128, 12 * NCH)
    shared = {
        "xall": x,
        "idn": np.eye(128, dtype=np.float32),
        "triu": np.triu(np.ones((128, 128), np.float32)),
        "onesc": np.ones((128, 1), np.float32),
        "onesr": np.ones((1, 128), np.float32),
        "wkey": np.ascontiguousarray(np.asarray(inputs["L0_attn_w_key"], np.float32)),
        "wval": np.ascontiguousarray(np.asarray(inputs["L0_attn_w_value"], np.float32)),
        "wrec": np.ascontiguousarray(np.asarray(inputs["L0_attn_w_recept"], np.float32)),
        "wout": np.ascontiguousarray(np.asarray(inputs["L0_attn_w_output"], np.float32)),
        "fwkey": np.ascontiguousarray(np.asarray(inputs["L0_attn_f_w_key"], np.float32)),
        "fwrec": np.ascontiguousarray(np.asarray(inputs["L0_attn_f_w_recept"], np.float32)),
        "fwval": np.ascontiguousarray(np.asarray(inputs["L0_attn_f_w_value"], np.float32)),
        "wgateT": np.ascontiguousarray(np.asarray(inputs["L0_ffn_w_gate"], np.float32).T),
        "wdownT": np.ascontiguousarray(np.asarray(inputs["L0_ffn_w_down"], np.float32).transpose(0, 2, 1)),
        "wrouter": np.ascontiguousarray(np.asarray(inputs["L0_ffn_w_router"], np.float32)),
        "pvec": np.ascontiguousarray(pvec),
        "fconst": np.tile(np.array(FCONSTS, np.float32), (128, 1)),
    }
    in_maps = []
    for c in range(NC):
        q = c % 4
        gs = c * OWN
        if q == 0:
            xph = np.concatenate([np.zeros((NH, D), np.float32), x[gs:gs + OWN]], axis=0)
            m = 0.0
        else:
            xph = x[gs - NH:gs + OWN]
            m = 1.0
        im = dict(shared)
        im["xph"] = np.ascontiguousarray(xph)
        im["mh"] = np.full((128, 1), m, np.float32)
        im["mhinv"] = np.full((128, 1), 1.0 - m, np.float32)
        im["rank"] = np.full((1, 1), c, np.int32)
        gs_ = np.zeros((128, 8), np.float32)
        gs_[:, c] = 1.0
        im["gsel"] = gs_
        in_maps.append(im)
    return in_maps


def kernel(**inputs):
    in_maps = _host_prep(inputs)
    nc, ctx = build_program(taps=TAPS)
    try:
        res = run_bass_kernel_spmd(nc, in_maps, core_ids=list(range(NC)))
    finally:
        ctx.close()
    parts = [res.results[c]["y"] for c in range(NC)]
    out = np.concatenate(parts, axis=0).reshape(NB, T, D).astype(np.float32)
    if TAPS:
        kernel.last_results = res.results
    return out


# version 22
# speedup vs baseline: 1.1079x; 1.0268x over previous
import numpy as np
import concourse.bass as bass
import concourse.mybir as mybir
from concourse.bass_utils import run_bass_kernel_spmd
from contextlib import ExitStack

F32 = mybir.dt.float32
I32 = mybir.dt.int32
Act = mybir.ActivationFunctionType
Alu = mybir.AluOpType

D = 1024
NCH = 8
DH = 4096
DM = 2048
NE = 8
CAP = 2560.0
NB = 2
T = 8192
NTOK = NB * T
NC = 8
OWN = 2048
NH = 512
NT = NH + OWN
BLK = 512
NBLK = NT // BLK
TPB = BLK // 128
NTILE = NT // 128
GT = NTOK // 128
EPS = 1e-5
R1024 = 1.0 / 1024.0

ENGINES = ("sync", "gpsimd", "tensor", "vector", "scalar")
FCONSTS = (1e-5, -0.5, -1.0, 1.0 / 1024.0, 1.0, 2560.5, 1e-6, -1e30)

TAPS = ()
DEV_SKIP = ()
DEV_STOP = None


class Flow:

    def __init__(self, nc, ctx, semnames):
        self.nc = nc
        self.prog = []
        self.sem = {}
        self.cnt = {}
        self.lastw = {e: {} for e in ENGINES}
        for n in semnames:
            self.sem[n] = ctx.enter_context(nc.semaphore(n))
            self.cnt[n] = 0

    def now(self, *names):
        return [(n, self.cnt[n]) for n in names]

    AUTOSELF = {"vector": "v", "scalar": "s"}

    def on(self, eng, fn, inc=None, k=1, wait=None):
        wait = list(wait or [])
        aself = self.AUTOSELF.get(eng)
        if aself is not None:
            wait.append((aself, self.cnt[aself]))
        w = []
        for sname, val in wait:
            if val > self.lastw[eng].get(sname, 0):
                w.append((sname, val))
                self.lastw[eng][sname] = val
        if inc is not None:
            self.cnt[inc] += k
        sems = self.sem

        def run(e, _fn=fn, _w=w, _inc=inc, _k=k):
            for sname, val in _w:
                e.wait_ge(sems[sname], val)
            ins = _fn(e)
            if _inc is not None:
                assert ins is not None
                ins.then_inc(sems[_inc], _k)
        self.prog.append((eng, run))

    def barrier_from(self, eng, *sems):
        self.on(eng, lambda e: None, wait=self.now(*sems))


def build_program(taps=()):
    nc = bass.Bass("TRN2", target_bir_lowering=False, debug=False)
    ctx = ExitStack()

    def dram_in(name, shape, dt=F32):
        return nc.dram_tensor(name, shape, dt, kind="ExternalInput").ap()

    def dram_out(name, shape, dt=F32):
        return nc.dram_tensor(name, shape, dt, kind="ExternalOutput").ap()

    def sb(name, shape, dt=F32):
        return ctx.enter_context(nc.sbuf_tensor(name, shape, dt)).ap()

    def ps(name, shape, dt=F32):
        return ctx.enter_context(nc.psum_tensor(name, shape, dt)).ap()

    xall = dram_in("xall", [NTOK, D])
    xph = dram_in("xph", [NT, D])
    mh_d = dram_in("mh", [128, 1])
    mhinv_d = dram_in("mhinv", [128, 1])
    rank_d = dram_in("rank", [1, 1], I32)
    gsel_d = dram_in("gsel", [128, 8])
    idn_d = dram_in("idn", [128, 128])
    triu_d = dram_in("triu", [128, 128])
    onesc_d = dram_in("onesc", [128, 1])
    onesr_d = dram_in("onesr", [1, 128])
    fconst_d = dram_in("fconst", [128, 8])
    wkey_d = dram_in("wkey", [D, D])
    wval_d = dram_in("wval", [D, D])
    wrec_d = dram_in("wrec", [D, D])
    wout_d = dram_in("wout", [D, D])
    fwkey_d = dram_in("fwkey", [D, DH])
    fwrec_d = dram_in("fwrec", [D, D])
    fwval_d = dram_in("fwval", [DH, D])
    wgateT_d = dram_in("wgateT", [D, DM])
    wdownT_d = dram_in("wdownT", [NE, DM, D])
    wrouter_d = dram_in("wrouter", [NE, D])
    pvec_d = dram_in("pvec", [128, 12 * NCH])

    y_out = dram_out("y", [OWN, D])
    BF16 = mybir.dt.bfloat16
    fwkeyHL_d = nc.dram_tensor("fwkeyHL", [D, 2 * DH], BF16, kind="Internal").ap()
    fwvalHL_d = nc.dram_tensor("fwvalHL", [DH, 2 * D], BF16, kind="Internal").ap()
    wgateHL_d = nc.dram_tensor("wgateHL", [D, 2 * DM], BF16, kind="Internal").ap()
    wdownHL_d = nc.dram_tensor("wdownHL", [NE, DM, 2 * D], BF16, kind="Internal").ap()

    idn = sb("idn_sb", [128, 128])
    triu = sb("triu_sb", [128, 128])
    onesc = sb("onesc_sb", [128, 1])
    onesr = sb("onesr_sb", [1, 128])
    mh = sb("mh_s", [128, 1])
    mhinv = sb("mhinv_s", [128, 1])
    rank_s = sb("rank_s", [1, 1], I32)
    gsel = sb("gsel_s", [128, 8])
    pvec = sb("pvec_s", [128, 12 * NCH])
    fconst = sb("fconst_s", [128, 8])
    for _i, _v in enumerate(FCONSTS):
        nc.const_aps.register(F32, _v, fconst[:, _i:_i + 1])
    wder = sb("wder", [128, 3 * NCH])
    negbig = sb("negbig", [128, 8])
    wr_raw = sb("wr_raw", [NE, D])
    wr_sum = sb("wr_sum", [NE, 1])
    wrcT = sb("wrcT", [128, NCH * NE])

    z_all = sb("z_all", [128, GT * NE])
    oh1k_all = sb("oh1k_all", [128, GT * NE], BF16)
    oh2k_all = sb("oh2k_all", [128, GT * NE], BF16)
    offrow = sb("offrow", [1, NE])
    z_own = sb("z_own", [128, 16 * NE])
    oh1k_own = sb("oh1k_own", [128, 16 * NE])
    oh2k_own = sb("oh2k_own", [128, 16 * NE])
    rxt = [sb(f"rxt{i}", [128, D]) for i in range(2)]
    rxT = [sb(f"rxT{i}", [128, D]) for i in range(2)]
    rsc = sb("rsc", [128, 64])
    rt_s = sb("rt_s", [128, 5 * 128])
    cbuf = sb("cbuf", [1, 136])
    gsel_d2 = None

    rsig0_all = sb("rsig0_all", [128, NTILE])
    m0_all = sb("m0_all", [128, NTILE])
    xt = [sb(f"xt{i}", [128, D]) for i in range(TPB)]
    xnt = sb("xnt", [128, D])
    xnt2 = sb("xnt2", [128, D])
    u1t = sb("u1t", [128, D])
    u1t2 = sb("u1t2", [128, D])
    stat = sb("stat", [128, 32])
    A_x1T = sb("A_x1T", [128, NCH * BLK])
    B_mix = sb("B_mix", [128, NCH * BLK])
    C_k = sb("C_k", [128, NCH * BLK])
    D_v = sb("D_v", [128, NCH * BLK])
    F_S = sb("F_S", [128, NCH * (BLK + 1)])
    G_Z = sb("G_Z", [128, NCH * (BLK + 1)])
    x2t = rxt + rxT
    bnd1 = sb("bnd1", [128, NCH])
    bnd2 = sb("bnd2", [128, NCH])
    XNTs = A_x1T[:, 0:NCH * 256]
    Hms = C_k[:, 0:16 * 256]

    wbuf = [sb(f"wbuf{i}", [128, D]) for i in range(2)]
    wbuf2 = [sb(f"wbuf2_{i}", [128, D]) for i in range(2)]
    wbufB = [w.bitcast(BF16) for w in wbuf]
    wbuf2B = [w.bitcast(BF16) for w in wbuf2]
    absF = sb("absF", [128, DM])
    absB = absF.bitcast(BF16)
    cnt_sb = sb("cnt_sb", [1, 1040])
    cnt2_sb = sb("cnt2_sb", [1, 1040])

    sgx = sb("sgx", [128, D])
    frs = [D_v[:, i * D:(i + 1) * D] for i in range(TPB)]
    yt = [B_mix[:, i * D:(i + 1) * D] for i in range(TPB)]
    cmb = sb("cmb", [128, 16 * NE])
    gsc = sb("gsc", [128, 16])

    P_A = ps("P_A", [128, 512])
    P_B = ps("P_B", [128, 512])
    P_C = ps("P_C", [128, 512])
    P_D = ps("P_D", [128, 512])
    P_E = ps("P_E", [128, 1024])
    P_F = ps("P_F", [128, 1024])

    fl = Flow(nc, ctx, ["qc", "qr0", "qr1", "qx0", "qx1", "qx2", "qx3",
                        "qw0", "qw1", "qv0", "qv1", "qo0", "qo1", "qo2", "qo3",
                        "qsi0", "qsi1", "qsi2", "qsi3", "qso0", "qso1", "qw2", "qv2", "t", "v", "s"])
    on, now = fl.on, fl.now

    def dma(e, dst, src):
        return e.dma_start(out=dst, in_=src)

    for dst, src in [(fconst, fconst_d), (gsel, gsel_d), (idn, idn_d), (triu, triu_d), (onesc, onesc_d),
                     (onesr, onesr_d), (mh, mh_d), (mhinv, mhinv_d),
                     (rank_s, rank_d), (wr_raw, wrouter_d)]:
        on("sync", lambda e, d=dst, s=src: dma(e, d, s), inc="qc", k=16)
    on("sync", lambda e: dma(e, pvec[:, 0:12 * NCH], pvec_d[:]), inc="qc", k=16)

    def pv(p, c):
        return pvec[:, p * NCH + c: p * NCH + c + 1]

    def der(i, c):
        return wder[:, i * NCH + c: i * NCH + c + 1]

    w_qx = now("qc")
    for c in range(NCH):
        on("scalar", lambda e, c=c: e.activation(der(2, c), pv(4, c), Act.Exp), wait=w_qx, inc="s")
        on("scalar", lambda e, c=c: e.activation(der(1, c), pv(5, c), Act.Exp), inc="s")
    on("scalar", lambda e: None, wait=now("s"))
    for c in range(NCH):
        on("scalar", lambda e, c=c: e.activation(der(0, c), der(2, c), Act.Exp, scale=-1.0), inc="s")
    on("vector", lambda e: e.memset(negbig[:], -1e30), inc="v")

    on("vector", lambda e: e.reduce_sum(wr_sum[:], wr_raw[:], axis=mybir.AxisListType.X), wait=w_qx, inc="v")
    on("scalar", lambda e: e.mul(wr_sum[:], wr_sum[:], R1024), wait=now("v"), inc="s")
    on("vector", lambda e: e.tensor_scalar(wr_raw[:], wr_raw[:], wr_sum[:, 0:1], None, Alu.subtract),
       wait=now("s"), inc="v")
    w_v = now("v", "qc")
    for c in range(NCH):
        on("tensor", lambda e, c=c: e.matmul(
            P_A[:, c * 8:(c + 1) * 8], wr_raw[:, c * 128:(c + 1) * 128], idn[0:NE, 0:NE],
            is_transpose=True, start=(c == 0), stop=(c == NCH - 1)),
           wait=w_v if c == 0 else None, inc="t" if c == NCH - 1 else None)
    on("scalar", lambda e: e.copy(wrcT[:, 0:NCH * NE], P_A[:, 0:NCH * NE]), wait=now("t"), inc="s")

    skip_routing = "routing" in DEV_SKIP
    GB = 16
    NG = GT // GB
    def rslot(i):
        return rt_s[:, i * 128:(i + 1) * 128]

    def rview(i):
        return rt_s[:, i * 128:(i + 1) * 128].rearrange("p (t e) -> p t e", e=NE)

    t_after_tr = {}
    s_after_ev = {}
    v_after_zev = {}

    def vsw():
        on("vector", lambda e: None, wait=now("v"))

    def _colhl(hl_ap, rs, cc, which):
        v = hl_ap[rs, cc * 2048:(cc + 1) * 2048].rearrange("p (co o) -> p co o", o=256)
        return v[:, :, 0:128] if which == 0 else v[:, :, 128:256]

    split_pieces = []
    for r in range(8):
        rs = slice(r * 128, (r + 1) * 128)
        for cc in range(4):
            sl = (rs, slice(cc * 1024, (cc + 1) * 1024))
            split_pieces.append((fwkey_d[sl], _colhl(fwkeyHL_d, rs, cc, 0),
                                 _colhl(fwkeyHL_d, rs, cc, 1)))
    for r in range(32):
        rs = slice(r * 128, (r + 1) * 128)
        split_pieces.append((fwval_d[rs, 0:D], fwvalHL_d[rs, 0:D], fwvalHL_d[rs, D:2 * D]))
    for r in range(8):
        rs = slice(r * 128, (r + 1) * 128)
        for cc in range(2):
            sl = (rs, slice(cc * 1024, (cc + 1) * 1024))
            split_pieces.append((wgateT_d[sl], _colhl(wgateHL_d, rs, cc, 0),
                                 _colhl(wgateHL_d, rs, cc, 1)))
    for e_ in range(NE):
        for mo in range(16):
            ms = slice(mo * 128, (mo + 1) * 128)
            split_pieces.append((wdownT_d[e_, ms, 0:D], wdownHL_d[e_, ms, 0:D],
                                 wdownHL_d[e_, ms, D:2 * D]))
    split_state = {"i": 0}
    SPL = 4

    def _issue_in(j, w):
        sl = j % SPL
        srcp = split_pieces[j][0]
        on("gpsimd", lambda e, sl=sl, srcp=srcp: e.dma_start(
            out=D_v[:, sl * D:(sl + 1) * D], in_=srcp),
           wait=w, inc=f"qsi{sl}", k=16)

    def emit_splits(n):
        for _ in range(n):
            i = split_state["i"]
            if i >= len(split_pieces):
                return
            split_state["i"] = i + 1
            if i == 0:
                for j in range(min(SPL, len(split_pieces))):
                    _issue_in(j, None)
            sl = i % SPL
            db = i % 2
            a0 = db * 2 * D
            _, hdst, ldst = split_pieces[i]
            on("scalar", lambda e, sl=sl, a0=a0: e.copy(absB[:, a0:a0 + D],
                                                        D_v[:, sl * D:(sl + 1) * D]),
               wait=now(f"qsi{sl}", f"qso{db}"), inc="s")
            on("vector", lambda e, sl=sl, a0=a0: e.tensor_tensor(
                absB[:, a0 + D:a0 + 2 * D], D_v[:, sl * D:(sl + 1) * D],
                absB[:, a0:a0 + D], Alu.subtract),
               wait=now("s"), inc="v")
            wv = now("v")
            on("gpsimd", lambda e, hdst=hdst, a0=a0: e.dma_start(out=hdst, in_=absB[:, a0:a0 + D]),
               wait=wv, inc=f"qso{db}", k=16)
            on("gpsimd", lambda e, ldst=ldst, a0=a0: e.dma_start(out=ldst, in_=absB[:, a0 + D:a0 + 2 * D]),
               inc=f"qso{db}", k=16)
            if i + SPL < len(split_pieces):
                _issue_in(i + SPL, wv)

    for g in range(GT if not skip_routing else 0):
        gi, ti = g // GB, g % GB
        db = g % 2
        on("sync", lambda e, g=g, db=db: dma(e, rxt[db][:], xall[g * 128:(g + 1) * 128, :]),
           wait=[("t", t_after_tr[g - 2])] if g >= 2 else None, inc=f"qr{db}", k=16)
        PT = P_E if g % 2 == 0 else P_F
        wt = now(f"qr{db}") + ([("s", s_after_ev[g - 2])] if g >= 2
                               else (now("qc") if g == 0 else []))
        for c in range(NCH):
            on("tensor", lambda e, c=c, db=db, PT=PT: e.matmul(
                PT[:, c * 128:(c + 1) * 128], rxt[db][:, c * 128:(c + 1) * 128], idn[:],
                is_transpose=True,
                start=(c % 4 == 0), stop=(c % 4 == 3)),
               wait=wt if c == 0 else None, inc="t" if c == NCH - 1 else None)
        t_after_tr[g] = fl.cnt["t"]
        wt = now("t")
        on("scalar", lambda e, db=db, PT=PT: e.copy(rxT[db][:, 0:512], PT[:, 0:512]), wait=wt, inc="s")
        on("scalar", lambda e, db=db, PT=PT: e.copy(rxT[db][:, 512:1024], PT[:, 512:1024]), inc="s")
        s_after_ev[g] = fl.cnt["s"]
        zb = P_A if gi % 2 == 0 else P_B
        ws = now("s") + ([("v", v_after_zev.get(gi - 2, 0))] if ti == 0 else [])
        for c in range(NCH):
            on("tensor", lambda e, c=c, db=db, ti=ti, zb=zb: e.matmul(
                zb[:, ti * NE:(ti + 1) * NE], rxT[db][:, c * 128:(c + 1) * 128],
                wrcT[:, c * 8:(c + 1) * 8],
                start=(ti == 0 and c == 0), stop=(ti == GB - 1 and c == NCH - 1)),
               wait=ws if c == 0 else None,
               inc="t" if (ti == GB - 1 and c == NCH - 1) else None)
        if ti == GB - 1:
            on("vector", lambda e, gi=gi, zb=zb: e.tensor_copy(
                z_all[:, gi * 128:(gi + 1) * 128], zb[:, 0:GB * NE]), wait=now("t"), inc="v")
            v_after_zev[gi] = fl.cnt["v"]
        emit_splits(2)

    if not skip_routing:
        O1 = A_x1T[:, 0:1024]
        S2 = A_x1T[:, 1024:2048]
        S3 = A_x1T[:, 2048:3072]
        ZM = A_x1T[:, 3072:4096]
        O2 = B_mix[:, 0:1024]
        SP = B_mix[:, 1024:2048]

        def wv3(ap):
            return ap.rearrange("p (t e) -> p t e", e=NE)

        def top1_wide(src, dst):
            vsw()
            on("vector", lambda e, src=src: e.tensor_reduce(
                rt_s[:, 0:GT], wv3(src), mybir.AxisListType.X, Alu.max), inc="v")
            vsw()
            on("vector", lambda e, src=src, dst=dst: e.tensor_tensor(
                wv3(dst), wv3(src),
                rt_s[:, 0:GT].unsqueeze(2).broadcast_to((128, GT, NE)), Alu.is_ge), inc="v")
            vsw()
            on("vector", lambda e, dst=dst: e.tensor_copy(S2, dst), inc="v")
            vsw()
            on("vector", lambda e, dst=dst: e.tensor_add(
                wv3(S2)[:, :, 1:8], wv3(dst)[:, :, 1:8], wv3(dst)[:, :, 0:7]), inc="v")
            vsw()
            on("vector", lambda e: e.tensor_copy(S3, S2), inc="v")
            vsw()
            on("vector", lambda e: e.tensor_add(
                wv3(S3)[:, :, 2:8], wv3(S2)[:, :, 2:8], wv3(S2)[:, :, 0:6]), inc="v")
            vsw()
            on("vector", lambda e: e.tensor_copy(S2, S3), inc="v")
            vsw()
            on("vector", lambda e: e.tensor_add(
                wv3(S2)[:, :, 4:8], wv3(S3)[:, :, 4:8], wv3(S3)[:, :, 0:4]), inc="v")
            vsw()
            on("vector", lambda e: e.tensor_single_scalar(S2, S2, 1.0, Alu.is_equal), inc="v")
            vsw()
            on("vector", lambda e, dst=dst: e.tensor_mul(dst, dst, S2), inc="v")

        top1_wide(z_all[:, 0:GT * NE], O1)
        vsw()
        on("vector", lambda e: e.scalar_tensor_tensor(
            ZM, O1, -1e30, z_all[:, 0:GT * NE], Alu.mult, Alu.add), inc="v")
        top1_wide(ZM, O2)
        vsw()
        on("vector", lambda e: e.tensor_add(S3, O1, O2), inc="v")
        wv = now("v")
        on("tensor", lambda e: e.matmul(P_C[0:1, 0:512], onesc[:], S3[:, 0:512],
                                        start=True, stop=True), wait=wv, inc="t")
        on("tensor", lambda e: e.matmul(P_D[0:1, 0:512], onesc[:], S3[:, 512:1024],
                                        start=True, stop=True), inc="t")
        wt = now("t")
        on("vector", lambda e: e.memset(cnt_sb[0:1, 0:8], 0.0), inc="v")
        on("vector", lambda e: e.tensor_copy(cnt_sb[0:1, 8:520], P_C[0:1, 0:512]), wait=wt, inc="v")
        on("vector", lambda e: e.tensor_copy(cnt_sb[0:1, 520:1032], P_D[0:1, 0:512]), inc="v")
        pp = [cnt_sb, cnt2_sb]
        for k, stp in enumerate((8, 16, 32, 64, 128, 256, 512)):
            sb_, db_ = pp[k % 2], pp[1 - k % 2]
            vsw()
            on("vector", lambda e, sb_=sb_, db_=db_, stp=stp: e.tensor_copy(
                db_[0:1, 0:8 + stp], sb_[0:1, 0:8 + stp]), inc="v")
            on("vector", lambda e, sb_=sb_, db_=db_, stp=stp: e.tensor_add(
                db_[0:1, 8 + stp:1032], sb_[0:1, 8 + stp:1032],
                sb_[0:1, 8:1032 - stp]), inc="v")
        cntF = pp[7 % 2]
        vsw()
        wv = now("v")
        firstb = {0: True, 1: True}
        for t in range(GT):
            bi = 0 if t < 64 else 1
            bk = P_A if bi == 0 else P_B
            col0 = (t % 64) * 8
            on("tensor", lambda e, t=t, bk=bk, col0=col0: e.matmul(
                bk[:, col0:col0 + 8], triu[:], S3[:, t * NE:(t + 1) * NE],
                start=(t % 64 == 0), stop=False),
               wait=wv if firstb[bi] else None)
            firstb[bi] = False
            on("tensor", lambda e, t=t, bk=bk, col0=col0: e.matmul(
                bk[:, col0:col0 + 8], onesr[:], cntF[0:1, t * NE:(t + 1) * NE],
                start=False, stop=(t % 64 == 63)),
               inc="t" if t % 64 == 63 else None)
        wt = now("t")
        on("vector", lambda e: e.tensor_single_scalar(
            S2[:, 0:512], P_A[:, 0:512], CAP + 0.5, Alu.is_le), wait=wt, inc="v")
        on("vector", lambda e: e.tensor_single_scalar(
            S2[:, 512:1024], P_B[:, 0:512], CAP + 0.5, Alu.is_le), inc="v")
        vsw()
        on("vector", lambda e: e.tensor_mul(O1, O1, S2), inc="v")
        on("vector", lambda e: e.tensor_mul(O2, O2, S2), inc="v")
        vsw()
        on("vector", lambda e: e.tensor_copy(oh1k_all[:, 0:GT * NE], O1), inc="v")
        on("vector", lambda e: e.tensor_copy(oh2k_all[:, 0:GT * NE], O2), inc="v")
        for srcb, dstb in ((z_all, z_own), (O1, oh1k_own), (O2, oh2k_own)):
            vsw()
            for gi in range(NG):
                on("vector", lambda e, srcb=srcb, gi=gi: e.tensor_scalar(
                    SP[:, gi * 128:(gi + 1) * 128], srcb[:, gi * 128:(gi + 1) * 128],
                    gsel[:, gi:gi + 1], None, Alu.mult), inc="v")
            vsw()
            on("vector", lambda e: e.tensor_add(ZM[:, 0:512], SP[:, 0:512], SP[:, 512:1024]), inc="v")
            vsw()
            on("vector", lambda e: e.tensor_add(ZM[:, 512:768], ZM[:, 0:256], ZM[:, 256:512]), inc="v")
            vsw()
            on("vector", lambda e, dstb=dstb: e.tensor_add(
                dstb[:, 0:128], ZM[:, 512:640], ZM[:, 640:768]), inc="v")

    emit_splits(10 ** 9)
    SPLITG = now("s", "v", "qso0", "qso1")

    _emit_blocks(nc, fl, locals())

    for tname in taps:
        if ":" in tname:
            nm, idx = tname.split(":")
            buf = locals()[nm][int(idx)]
        else:
            buf = locals()[tname]
        td = dram_out("tap_" + tname.replace(":", "_"), list(buf.shape), buf.dtype)
        on("sync", lambda e, b=buf, t=td: dma(e, t[:], b[:]),
           wait=now("v", "s", "t", "qw0", "qw1", "qv0", "qv1"), inc="qo0", k=16)

    on("sync", lambda e: None, wait=now("qo0", "qo1", "qo2", "qo3"))

    with nc.Block() as block:
        @block.sync
        def _(e):
            for eng, run in fl.prog:
                if eng == "sync":
                    run(e)

        @block.gpsimd
        def _(e):
            for eng, run in fl.prog:
                if eng == "gpsimd":
                    run(e)

        @block.tensor
        def _(e):
            for eng, run in fl.prog:
                if eng == "tensor":
                    run(e)

        @block.vector
        def _(e):
            for eng, run in fl.prog:
                if eng == "vector":
                    run(e)

        @block.scalar
        def _(e):
            for eng, run in fl.prog:
                if eng == "scalar":
                    run(e)

    return nc, ctx


def _emit_blocks(nc, fl, L):
    on, now = fl.on, fl.now
    g = lambda n: L[n]
    (xph, xt, xnt, u1t, stat, A_x1T, B_mix, C_k, D_v, F_S, G_Z, x2t, bnd1, bnd2,
     rsig0_all, m0_all, pvec, wder, mh, mhinv, idn, y_out) = (
        g("xph"), g("xt"), g("xnt"), g("u1t"), g("stat"), g("A_x1T"), g("B_mix"),
        g("C_k"), g("D_v"), g("F_S"), g("G_Z"), g("x2t"), g("bnd1"), g("bnd2"),
        g("rsig0_all"), g("m0_all"), g("pvec"), g("wder"), g("mh"), g("mhinv"),
        g("idn"), g("y_out"))
    P_A, P_B, P_C, P_D, P_E, P_F = g("P_A"), g("P_B"), g("P_C"), g("P_D"), g("P_E"), g("P_F")
    xnt2, u1t2 = g("xnt2"), g("u1t2")

    def pv(p, c):
        return pvec[:, p * NCH + c: p * NCH + c + 1]

    def der(i, c):
        return wder[:, i * NCH + c: i * NCH + c + 1]

    def st(i):
        return stat[:, i:i + 1]

    on("vector", lambda e: e.memset(bnd1[:], 0.0), inc="v")
    on("vector", lambda e: e.memset(bnd2[:], 0.0), inc="v")
    on("vector", lambda e: e.memset(stat[:], 0.0), inc="v")
    mark0_v = fl.cnt["v"]

    mark = {"x_users": 0, "pe_evac": fl.cnt["s"],
            "pf_evac0": fl.cnt["v"], "pf_evac1": fl.cnt["v"],
            "tile_v0": mark0_v, "tile_v1": mark0_v, "tile_t0": 0, "tile_t1": 0}

    def stageA(b):
        for tl in range(TPB):
            t = b * TPB + tl
            p = tl % 2
            xn_p = xnt if p == 0 else xnt2
            u1_p = u1t if p == 0 else u1t2
            PT = P_F if p == 0 else P_E
            pfn = "P_F" if p == 0 else "P_E"
            sto = p * 16

            def stp(i, sto=sto):
                return stat[:, sto + i:sto + i + 1]
            wprev = [("v", mark["x_users"])] if b > 0 else None
            on("sync", lambda e, t=t, tl=tl: e.dma_start(
                out=xt[tl][:], in_=xph[t * 128:(t + 1) * 128, :]),
               wait=wprev, inc=f"qx{tl}", k=16)
            wq = now(f"qx{tl}", "qc") + [("v", mark[f"tile_v{p}"]), ("t", mark[f"tile_t{p}"])]
            on("scalar", lambda e, tl=tl, u1_p=u1_p, stp=stp: e.activation(
                u1_p[:], xt[tl][:], Act.Copy, accum_out=stp(0)), wait=wq, inc="s")
            on("scalar", lambda e, tl=tl, u1_p=u1_p, stp=stp: e.activation(
                u1_p[:], xt[tl][:], Act.Square, accum_out=stp(1)), inc="s")
            on("scalar", lambda e, t=t, stp=stp: e.mul(m0_all[:, t:t + 1], stp(0), R1024), inc="s")
            on("scalar", lambda e, stp=stp: e.mul(stp(3), stp(1), R1024), inc="s")
            on("scalar", lambda e, t=t, stp=stp: e.mul(stp(4), m0_all[:, t:t + 1],
                                                       m0_all[:, t:t + 1]), inc="s")
            on("scalar", lambda e, stp=stp: e.activation(stp(5), stp(4), Act.Identity,
                                                         scale=-1.0, bias=stp(3)), inc="s")
            on("scalar", lambda e, stp=stp: e.activation(stp(6), stp(5), Act.Ln, bias=EPS), inc="s")
            on("scalar", lambda e, t=t, stp=stp: e.activation(rsig0_all[:, t:t + 1], stp(6), Act.Exp,
                                                              scale=-0.5), inc="s")
            ws = now("s")
            on("vector", lambda e, tl=tl, t=t, xn_p=xn_p: e.tensor_scalar(
                xn_p[:], xt[tl][:], m0_all[:, t:t + 1], rsig0_all[:, t:t + 1],
                Alu.subtract, Alu.mult), wait=ws, inc="v")
            wv = now("v")
            on("scalar", lambda e, u1_p=u1_p, xn_p=xn_p, stp=stp: e.activation(
                u1_p[:], xn_p[:], Act.Copy, accum_out=stp(8)), wait=wv, inc="s")
            on("scalar", lambda e, u1_p=u1_p, xn_p=xn_p, stp=stp: e.activation(
                u1_p[:], xn_p[:], Act.Square, accum_out=stp(9)), inc="s")
            on("scalar", lambda e, stp=stp: e.mul(stp(10), stp(8), R1024), inc="s")
            on("scalar", lambda e, stp=stp: e.mul(stp(11), stp(9), R1024), inc="s")
            on("scalar", lambda e, stp=stp: e.mul(stp(12), stp(10), stp(10)), inc="s")
            on("scalar", lambda e, stp=stp: e.activation(stp(13), stp(12), Act.Identity,
                                                         scale=-1.0, bias=stp(11)), inc="s")
            on("scalar", lambda e, stp=stp: e.activation(stp(14), stp(13), Act.Ln, bias=EPS), inc="s")
            on("scalar", lambda e, stp=stp: e.activation(stp(15), stp(14), Act.Exp, scale=-0.5), inc="s")
            ws = now("s")
            on("vector", lambda e, u1_p=u1_p, xn_p=xn_p, stp=stp: e.tensor_scalar(
                u1_p[:], xn_p[:], stp(10), stp(15), Alu.subtract, Alu.mult), wait=ws, inc="v")
            mark[f"tile_v{p}"] = fl.cnt["v"]
            wv = now("v") + [("v", mark[f"pf_evac{p}"]), pfree[pfn]]
            for c in range(NCH):
                on("tensor", lambda e, c=c, u1_p=u1_p, PT=PT: e.matmul(
                    PT[:, c * 128:(c + 1) * 128], u1_p[:, c * 128:(c + 1) * 128], idn[:],
                    is_transpose=True, start=(c % 4 == 0), stop=(c % 4 == 3)),
                   wait=wv if c == 0 else None, inc="t" if c == NCH - 1 else None)
            mark[f"tile_t{p}"] = fl.cnt["t"]
            wt = now("t")
            for c in range(NCH):
                on("vector", lambda e, c=c, tl=tl, PT=PT: e.tensor_scalar(
                    A_x1T[:, c * BLK + tl * 128: c * BLK + tl * 128 + 128],
                    PT[:, c * 128:(c + 1) * 128], pv(0, c), pv(1, c),
                    Alu.mult, Alu.add), wait=wt if c == 0 else None, inc="v")
            mark[f"pf_evac{p}"] = fl.cnt["v"]
            pfree[pfn] = ("v", fl.cnt["v"])

    wkey_d, wval_d, wrec_d, wout_d = g("wkey_d"), g("wval_d"), g("wrec_d"), g("wout_d")
    fwkey_d, fwrec_d, fwval_d = g("fwkey_d"), g("fwrec_d"), g("fwval_d")
    wgateT_d, wdownT_d = g("wgateT_d"), g("wdownT_d")
    wbuf, wbuf2 = g("wbuf"), g("wbuf2")
    sgx, frs, yt = g("sgx"), g("frs"), g("yt")
    XNTs, Hms = g("XNTs"), g("Hms")
    z_own, oh1k_own, oh2k_own, cmb, gsc, rsc = (g("z_own"), g("oh1k_own"),
                                                g("oh2k_own"), g("cmb"), g("gsc"), g("rsc"))
    wbufB, wbuf2B, absB, SPLITG = g("wbufB"), g("wbuf2B"), g("absB"), g("SPLITG")
    absF = g("absF")
    fwkeyHL_d, fwvalHL_d, wgateHL_d, wdownHL_d = (
        g("fwkeyHL_d"), g("fwvalHL_d"), g("wgateHL_d"), g("wdownHL_d"))
    BF16 = mybir.dt.bfloat16
    A_B = A_x1T.bitcast(BF16)
    C_B = C_k.bitcast(BF16)
    sgxB = sgx.bitcast(BF16)

    pfree = {n: ("v", 0) for n in ("P_A", "P_B", "P_C", "P_D", "P_E", "P_F")}
    PS = {"P_A": P_A, "P_B": P_B, "P_C": P_C, "P_D": P_D, "P_E": P_E, "P_F": P_F}
    wguard = {0: 0, 1: 0}
    wguard2 = {0: 0, 1: 0}
    wguard3 = [0]
    wguard4 = [0]
    wgF3k = [0]
    wgF3v = [0]
    WQE = {0: "gpsimd", 1: "sync", 2: "scalar"}
    wcnt = [0]
    wcnt2 = [0]

    def gs(i):
        return gsc[:, i:i + 1]

    def mix_into_B(pidx, c0, c1, wextra=None, src=None, bnd=None):
        src = A_x1T if src is None else src
        bnd = bnd1 if bnd is None else bnd
        for c in range(NCH):
            b0 = c * BLK
            on("vector", lambda e, c=c, b0=b0: e.tensor_sub(
                B_mix[:, b0 + max(c0, 1):b0 + c1], src[:, b0 + max(c0, 1):b0 + c1],
                src[:, b0 + max(c0, 1) - 1:b0 + c1 - 1]),
               wait=wextra if c == 0 else None, inc="v")
        if c0 == 0:
            for c in range(NCH):
                b0 = c * BLK
                on("vector", lambda e, c=c, b0=b0: e.tensor_sub(
                    B_mix[:, b0:b0 + 1], src[:, b0:b0 + 1], bnd[:, c:c + 1]), inc="v")
        for c in range(NCH):
            b0 = c * BLK
            on("vector", lambda e, c=c, b0=b0: e.scalar_tensor_tensor(
                B_mix[:, b0 + max(c0, 1):b0 + c1], B_mix[:, b0 + max(c0, 1):b0 + c1],
                pv(pidx, c), src[:, b0 + max(c0, 1) - 1:b0 + c1 - 1],
                Alu.mult, Alu.add), inc="v")
        if c0 == 0:
            on("vector", lambda e: None, wait=now("v"))
            for c in range(NCH):
                b0 = c * BLK
                on("vector", lambda e, c=c, b0=b0: e.scalar_tensor_tensor(
                    B_mix[:, b0:b0 + 1], B_mix[:, b0:b0 + 1], pv(pidx, c),
                    bnd[:, c:c + 1], Alu.mult, Alu.add), inc="v")

    def proj_from_B(Wd, dst, c0, c1, sigmoid=False, evac_wait=None, nsl=3):
        w = c1 - c0
        wsrc = now("v")
        for co in range(NCH):
            sl = wcnt[0] % nsl
            wcnt[0] += 1
            wb_ap = wbuf[sl] if sl < 2 else absF[:, 0:D]
            g0 = wguard[sl] if sl < 2 else wguard3[0]
            qn = "qw2" if sl == 2 else f"qw{sl}"
            on(WQE[sl], lambda e, co=co, wb_ap=wb_ap, Wd=Wd: e.dma_start(
                out=wb_ap.rearrange("p (ci o) -> p ci o", o=128),
                in_=Wd.rearrange("(ci p) o -> p ci o", p=128)[:, :, co * 128:(co + 1) * 128]),
               wait=[("t", g0)] + (SPLITG if sl == 2 else []), inc=qn, k=16)
            pname = "P_A" if co % 2 == 0 else "P_B"
            pa = PS[pname]
            wt = now(qn) + wsrc + [pfree[pname]]
            for ci in range(NCH):
                on("tensor", lambda e, ci=ci, co=co, wb_ap=wb_ap, pa=pa, w=w: e.matmul(
                    pa[:, 0:w], wb_ap[:, ci * 128:(ci + 1) * 128],
                    B_mix[:, ci * BLK + c0:ci * BLK + c1],
                    start=(ci == 0), stop=(ci == NCH - 1)),
                   wait=wt if ci == 0 else None, inc="t" if ci == NCH - 1 else None)
            if sl < 2:
                wguard[sl] = fl.cnt["t"]
            else:
                wguard3[0] = fl.cnt["t"]
            wte = now("t") + (evac_wait or [])
            if sigmoid:
                on("scalar", lambda e, co=co, pa=pa, w=w: e.activation(
                    dst[:, co * BLK + c0:co * BLK + c1], pa[:, 0:w], Act.Sigmoid),
                   wait=wte, inc="s")
            else:
                on("scalar", lambda e, co=co, pa=pa, w=w: e.copy(
                    dst[:, co * BLK + c0:co * BLK + c1], pa[:, 0:w]), wait=wte, inc="s")
            pfree[pname] = ("s", fl.cnt["s"])

    def stageB(b):
        rng = (384, 512) if b == 0 else (0, 512)
        wA = [("v", mark["pf_evac0"]), ("v", mark["pf_evac1"])]
        nsl = 2
        mix_into_B(6, 0, BLK, wextra=wA + now("t"))
        proj_from_B(wkey_d, C_k, 0, BLK, nsl=nsl)
        mix_into_B(7, 0, BLK, wextra=now("t"))
        proj_from_B(wval_d, D_v, 0, BLK, nsl=nsl)
        if rng is not None:
            mix_into_B(8, rng[0], rng[1], wextra=now("t"))
        on("vector", lambda e: e.tensor_copy(bnd1[:, 0:NCH], A_x1T[:, BLK - 1::BLK]), inc="v")
        if b == 0:
            on("vector", lambda e: e.tensor_scalar(bnd1[:, 0:NCH], bnd1[:, 0:NCH],
                                                   mh[:, 0:1], None, Alu.mult),
               wait=now("v"), inc="v")
        mark["bnd1"] = fl.cnt["v"]
        if rng is not None:
            proj_from_B(wrec_d, A_x1T, rng[0], rng[1], sigmoid=True,
                        evac_wait=[("v", mark["bnd1"])], nsl=nsl)

    def stageS(b):
        rng = (384, 512) if b == 0 else (0, 512)
        wfree = [("t", wguard[0]), ("t", wguard[1])]
        ws_sr = now("s")
        if b == 0:
            on("vector", lambda e: e.memset(F_S[:, 0::BLK + 1], 0.0), inc="v")
            on("vector", lambda e: e.memset(G_Z[:, 0::BLK + 1], 0.0), inc="v")
        for c in range(NCH):
            f0 = c * (BLK + 1)
            on("scalar", lambda e, c=c: e.activation(
                B_mix[:, c * BLK:(c + 1) * BLK], C_k[:, c * BLK:(c + 1) * BLK], Act.Exp),
               wait=wfree if c == 0 else None, inc="s")
            ws = now("s")
            if b < 1:
                on("vector", lambda e, c=c: e.tensor_scalar(
                    B_mix[:, c * BLK:(c + 1) * BLK], B_mix[:, c * BLK:(c + 1) * BLK],
                    mh[:, 0:1], None, Alu.mult), wait=ws, inc="v")
                ws = None
            on("vector", lambda e, c=c: e.tensor_mul(
                D_v[:, c * BLK:(c + 1) * BLK], B_mix[:, c * BLK:(c + 1) * BLK],
                D_v[:, c * BLK:(c + 1) * BLK]), wait=ws, inc="v")
            if b > 0:
                on("vector", lambda e, f0=f0: e.tensor_copy(F_S[:, f0:f0 + 1],
                                                            F_S[:, f0 + BLK:f0 + BLK + 1]), inc="v")
                on("vector", lambda e, f0=f0: e.tensor_copy(G_Z[:, f0:f0 + 1],
                                                            G_Z[:, f0 + BLK:f0 + BLK + 1]), inc="v")
            initF = 0.0 if b == 0 else F_S[:, f0 + BLK:f0 + BLK + 1]
            initG = 0.0 if b == 0 else G_Z[:, f0 + BLK:f0 + BLK + 1]
            on("vector", lambda e, c=c, f0=f0, initF=initF: e.tensor_tensor_scan(
                F_S[:, f0 + 1:f0 + 1 + BLK],
                der(0, c).broadcast_to((128, BLK)),
                D_v[:, c * BLK:(c + 1) * BLK], initF, Alu.mult, Alu.add), inc="v")
            on("vector", lambda e, c=c, f0=f0, initG=initG: e.tensor_tensor_scan(
                G_Z[:, f0 + 1:f0 + 1 + BLK],
                der(0, c).broadcast_to((128, BLK)),
                B_mix[:, c * BLK:(c + 1) * BLK], initG, Alu.mult, Alu.add), inc="v")
            if rng is None:
                continue
            c0, c1 = rng
            on("vector", lambda e, c=c, f0=f0, c0=c0, c1=c1: e.scalar_tensor_tensor(
                D_v[:, c * BLK + c0:c * BLK + c1], D_v[:, c * BLK + c0:c * BLK + c1],
                der(1, c), F_S[:, f0 + c0:f0 + c1], Alu.mult, Alu.add), inc="v")
            on("vector", lambda e, c=c, f0=f0, c0=c0, c1=c1: e.scalar_tensor_tensor(
                B_mix[:, c * BLK + c0:c * BLK + c1], B_mix[:, c * BLK + c0:c * BLK + c1],
                der(1, c), G_Z[:, f0 + c0:f0 + c1], Alu.mult, Alu.add), inc="v")
            if b == 0:
                on("vector", lambda e, c=c, c0=c0, c1=c1: e.tensor_scalar(
                    B_mix[:, c * BLK + c0:c * BLK + c1], B_mix[:, c * BLK + c0:c * BLK + c1],
                    mhinv[:, 0:1], None, Alu.add), inc="v")
            on("vector", lambda e, c=c, c0=c0, c1=c1: e.reciprocal(
                C_k[:, c * BLK + c0:c * BLK + c1], B_mix[:, c * BLK + c0:c * BLK + c1]), inc="v")
            on("vector", lambda e, c=c, c0=c0, c1=c1: e.tensor_mul(
                D_v[:, c * BLK + c0:c * BLK + c1], D_v[:, c * BLK + c0:c * BLK + c1],
                C_k[:, c * BLK + c0:c * BLK + c1]), inc="v")
            on("vector", lambda e, c=c, c0=c0, c1=c1: e.tensor_mul(
                D_v[:, c * BLK + c0:c * BLK + c1], D_v[:, c * BLK + c0:c * BLK + c1],
                A_x1T[:, c * BLK + c0:c * BLK + c1]), wait=ws_sr if c == 0 else None, inc="v")
            mark[f"osr{c}"] = fl.cnt["v"]

    OPS = {0: [("P_E", 0), ("P_E", 512)], 1: [("P_F", 0), ("P_F", 512)],
           2: [("P_A", 0), ("P_B", 0)], 3: [("P_C", 0), ("P_D", 0)]}

    def stageO(b):
        rng = (384, 512) if b == 0 else (0, 512)
        jjs = [3] if b == 0 else [0, 1, 2, 3]
        wps = [pfree[p] for p in ("P_A", "P_B", "P_C", "P_D", "P_E", "P_F")]
        for ci in range(NCH):
            sl = wcnt2[0] % 2
            wcnt2[0] += 1
            wb_ap = wbuf2[sl] if sl < 2 else absF[:, D:2 * D]
            g0 = wguard2[sl] if sl < 2 else wguard4[0]
            qn = "qv2" if sl == 2 else f"qv{sl}"
            on(WQE[sl], lambda e, ci=ci, wb_ap=wb_ap: e.dma_start(
                out=wb_ap[:], in_=wout_d[ci * 128:(ci + 1) * 128, :]),
               wait=[("t", g0)] + (SPLITG if sl == 2 else []), inc=qn, k=16)
            wt = now(qn) + [("v", mark[f"osr{ci}"])] + (wps if ci == 0 else [])
            first = True
            for jj in jjs:
                for half in (0, 1):
                    pn, off = OPS[jj][half]
                    pa = PS[pn]
                    on("tensor", lambda e, ci=ci, jj=jj, half=half, pa=pa, off=off, wb_ap=wb_ap: e.matmul(
                        pa[:, off:off + 512], D_v[:, ci * BLK + jj * 128:ci * BLK + jj * 128 + 128],
                        wb_ap[:, half * 512:(half + 1) * 512],
                        start=(ci == 0), stop=(ci == NCH - 1)),
                       wait=wt if first else None,
                       inc="t" if (jj == jjs[-1] and half == 1) else None)
                    first = False
            if sl < 2:
                wguard2[sl] = fl.cnt["t"]
            else:
                wguard4[0] = fl.cnt["t"]
        wte = now("t")
        for jj in jjs:
            t = b * TPB + jj
            (pn0, off0), (pn1, off1) = OPS[jj]
            on("vector", lambda e, jj=jj, t=t: e.tensor_scalar(
                sgx[:], xt[jj][:], m0_all[:, t:t + 1], rsig0_all[:, t:t + 1],
                Alu.subtract, Alu.mult), wait=wte, inc="v")
            on("vector", lambda e, jj=jj, pn0=pn0, off0=off0: e.tensor_add(
                x2t[jj][:, 0:512], PS[pn0][:, off0:off0 + 512], sgx[:, 0:512]), inc="v")
            on("vector", lambda e, jj=jj, pn1=pn1, off1=off1: e.tensor_add(
                x2t[jj][:, 512:1024], PS[pn1][:, off1:off1 + 512], sgx[:, 512:1024]), inc="v")
            pfree[OPS[jj][0][0]] = ("v", fl.cnt["v"])
            pfree[OPS[jj][1][0]] = ("v", fl.cnt["v"])

    def ln_norm(src_ap, wslot, bslot, dst_cols, mask_last=False, p=0):
        xn_p = xnt if p == 0 else xnt2
        u1_p = u1t if p == 0 else u1t2
        PT = P_F if p == 0 else P_E
        pfn = "P_F" if p == 0 else "P_E"
        sto = p * 16

        def stp(i, sto=sto):
            return stat[:, sto + i:sto + i + 1]
        on("scalar", lambda e: e.activation(u1_p[:], src_ap, Act.Copy, accum_out=stp(0)),
           wait=[("v", mark[f"tile_v{p}"]), ("t", mark[f"tile_t{p}"])] + now("v"), inc="s")
        on("scalar", lambda e: e.activation(u1_p[:], src_ap, Act.Square, accum_out=stp(1)), inc="s")
        on("scalar", lambda e: e.mul(stp(2), stp(0), R1024), inc="s")
        on("scalar", lambda e: e.mul(stp(3), stp(1), R1024), inc="s")
        on("scalar", lambda e: e.mul(stp(4), stp(2), stp(2)), inc="s")
        on("scalar", lambda e: e.activation(stp(5), stp(4), Act.Identity,
                                            scale=-1.0, bias=stp(3)), inc="s")
        on("scalar", lambda e: e.activation(stp(6), stp(5), Act.Ln, bias=EPS), inc="s")
        on("scalar", lambda e: e.activation(stp(7), stp(6), Act.Exp, scale=-0.5), inc="s")
        ws = now("s")
        on("vector", lambda e: e.tensor_scalar(xn_p[:], src_ap, stp(2), stp(7),
                                               Alu.subtract, Alu.mult), wait=ws, inc="v")
        mark[f"tile_v{p}"] = fl.cnt["v"]
        wv = now("v") + [("v", mark[f"pf_evac{p}"])] + [pfree[pfn]]
        for c in range(NCH):
            on("tensor", lambda e, c=c: e.matmul(
                PT[:, c * 128:(c + 1) * 128], xn_p[:, c * 128:(c + 1) * 128], idn[:],
                is_transpose=True, start=(c % 4 == 0), stop=(c % 4 == 3)),
               wait=wv if c == 0 else None, inc="t" if c == NCH - 1 else None)
        mark[f"tile_t{p}"] = fl.cnt["t"]
        wt = now("t")
        for c in range(NCH):
            on("vector", lambda e, c=c: e.tensor_scalar(
                C_k[:, c * BLK + dst_cols[0]: c * BLK + dst_cols[1]],
                PT[:, c * 128:(c + 1) * 128], pv(wslot, c), pv(bslot, c),
                Alu.mult, Alu.add), wait=wt if c == 0 else None, inc="v")
        mark[f"pf_evac{p}"] = fl.cnt["v"]
        pfree[pfn] = ("v", fl.cnt["v"])
        if mask_last:
            on("vector", lambda e: None, wait=now("v"))
            for c in range(NCH):
                on("vector", lambda e, c=c: e.tensor_scalar(
                    C_k[:, c * BLK + 511:c * BLK + 512], C_k[:, c * BLK + 511:c * BLK + 512],
                    mh[:, 0:1], None, Alu.mult), inc="v")

    def stageL2(b):
        jjs = [3] if b == 0 else [0, 1, 2, 3]
        for jj in jjs:
            ln_norm(x2t[jj][:], 2, 3, (jj * 128, jj * 128 + 128),
                    mask_last=(b == 0 and jj == 3), p=jj % 2)
        if b == 0:
            on("vector", lambda e: e.tensor_copy(bnd2[:, 0:NCH], C_k[:, BLK - 1::BLK]),
               wait=now("v"), inc="v")

    def stageF(b):
        wgF3k[0] = max(wgF3k[0], fl.cnt["t"])
        wgF3v[0] = max(wgF3v[0], fl.cnt["t"])
        mix_into_B(10, 0, BLK, wextra=now("t"), src=C_k, bnd=bnd2)
        wsrc = now("v")
        for ci in range(NCH):
            sl = wcnt2[0] % 2
            wcnt2[0] += 1
            wb_ap = wbuf2[sl] if sl < 2 else absF[:, D:2 * D]
            g0 = wguard2[sl] if sl < 2 else wguard4[0]
            qn = "qv2" if sl == 2 else f"qv{sl}"
            on(WQE[sl], lambda e, ci=ci, wb_ap=wb_ap: e.dma_start(
                out=wb_ap[:], in_=fwrec_d[ci * 128:(ci + 1) * 128, :]),
               wait=[("t", g0)] + (SPLITG if sl == 2 else []), inc=qn, k=16)
            wt = now(qn) + wsrc + ([pfree[p] for p in ("P_A", "P_B", "P_C", "P_D", "P_E", "P_F")] if ci == 0 else [])
            first = True
            for jj in range(TPB):
                for half in (0, 1):
                    pn, off = OPS[jj][half]
                    on("tensor", lambda e, ci=ci, jj=jj, half=half, pn=pn, off=off, wb_ap=wb_ap: e.matmul(
                        PS[pn][:, off:off + 512], B_mix[:, ci * BLK + jj * 128:ci * BLK + jj * 128 + 128],
                        wb_ap[:, half * 512:(half + 1) * 512],
                        start=(ci == 0), stop=(ci == NCH - 1)),
                       wait=wt if first else None, inc="t" if (jj == TPB - 1 and half == 1) else None)
                    first = False
            if sl < 2:
                wguard2[sl] = fl.cnt["t"]
            else:
                wguard4[0] = fl.cnt["t"]
        wte = now("t")
        for jj in range(TPB):
            (pn0, off0), (pn1, off1) = OPS[jj]
            on("scalar", lambda e, jj=jj, pn0=pn0, off0=off0: e.activation(
                frs[jj][:, 0:512], PS[pn0][:, off0:off0 + 512], Act.Sigmoid),
               wait=wte, inc="s")
            on("scalar", lambda e, jj=jj, pn1=pn1, off1=off1: e.activation(
                frs[jj][:, 512:1024], PS[pn1][:, off1:off1 + 512], Act.Sigmoid), inc="s")
            pfree[OPS[jj][0][0]] = ("s", fl.cnt["s"])
            pfree[OPS[jj][1][0]] = ("s", fl.cnt["s"])
        mix_into_B(9, 0, BLK, wextra=now("t"), src=C_k, bnd=bnd2)
        on("vector", lambda e: e.tensor_copy(bnd2[:, 0:NCH], C_k[:, BLK - 1::BLK]), inc="v")
        mark["vC"] = fl.cnt["v"]
        u1tB = u1t.bitcast(BF16)
        SCR = {0: (sgx[:, 0:256], sgx[:, 256:512], sgxB, 1024),
               1: (sgx[:, 768:1024], u1t[:, 0:256], u1tB, 512)}
        NCO = DH // 128
        for hh in (0, 1):
            h0 = hh * 256
            wb = now("t", "v") + SPLITG
            for c in range(NCH):
                on("scalar", lambda e, c=c, h0=h0: e.copy(
                    absB[:, c * 256:(c + 1) * 256], B_mix[:, c * BLK + h0:c * BLK + h0 + 256]),
                   wait=wb, inc="s")
                on("vector", lambda e, c=c, h0=h0: e.tensor_tensor(
                    absB[:, DM + c * 256:DM + (c + 1) * 256],
                    B_mix[:, c * BLK + h0:c * BLK + h0 + 256],
                    absB[:, c * 256:(c + 1) * 256], Alu.subtract),
                   wait=now("s"), inc="v")
            ws_x = now("s", "v")
            tfk = {}
            pm = {}

            def emit_fk(co):
                sl = wcnt[0] % 2
                wcnt[0] += 1
                wb_ap = wbufB[sl] if sl < 2 else C_B[:, 0:2 * D]
                g0 = wguard[sl] if sl < 2 else wgF3k[0]
                qn = "qw2" if sl == 2 else f"qw{sl}"
                gw = [("t", g0)] + SPLITG + ([("v", mark["vC"])] if sl == 2 else [])
                on(WQE[sl], lambda e, co=co, wb_ap=wb_ap: e.dma_start(
                    out=wb_ap[:, 0:2 * D].rearrange("p (ci o) -> p ci o", o=256),
                    in_=fwkeyHL_d.rearrange("(ci p) o -> p ci o", p=128)[:, :, co * 256:(co + 1) * 256]),
                   wait=gw, inc=qn, k=16)
                p = co % 2
                bk = P_A if p == 0 else P_B
                wt = now(qn) + (ws_x if co < 2 else []) \
                    + [pm.get(co - 2, pfree["P_A" if p == 0 else "P_B"])]
                first = True
                for ci in range(NCH):
                    for wo, ao in ((0, 0), (0, DM), (128, 0)):
                        on("tensor", lambda e, ci=ci, wb_ap=wb_ap, bk=bk, wo=wo, ao=ao: e.matmul(
                            bk[:, 0:256], wb_ap[:, ci * 256 + wo:ci * 256 + wo + 128],
                            absB[:, ao + ci * 256:ao + ci * 256 + 256],
                            start=(ci == 0 and wo == 0 and ao == 0),
                            stop=(ci == NCH - 1 and wo == 128)),
                           wait=wt if first else None,
                           inc="t" if (ci == NCH - 1 and wo == 128) else None)
                        first = False
                if sl < 2:
                    wguard[sl] = fl.cnt["t"]
                else:
                    wgF3k[0] = fl.cnt["t"]
                tfk[co] = fl.cnt["t"]

            def emit_chain_vk(co):
                p = co % 2
                R, M, PB, base = SCR[p]
                bk = P_A if p == 0 else P_B
                on("scalar", lambda e, R=R, bk=bk: e.activation(R, bk[:, 0:256], Act.Relu),
                   wait=[("t", tfk[co]), ("t", mark.get(f"fkscr{p}", 0))], inc="s")
                on("vector", lambda e, R=R, M=M, bk=bk: e.tensor_mul(M, R, bk[:, 0:256]),
                   wait=now("s"), inc="v")
                pm[co] = ("v", fl.cnt["v"])
                on("scalar", lambda e, M=M, PB=PB, base=base: e.copy(PB[:, base:base + 256], M),
                   wait=now("v"), inc="s")
                on("vector", lambda e, M=M, PB=PB, base=base: e.tensor_tensor(
                    PB[:, base + 256:base + 512], M, PB[:, base:base + 256], Alu.subtract),
                   wait=now("s"), inc="v")
                wv = now("v")
                sl2 = wcnt2[0] % 2
                wcnt2[0] += 1
                wb2_ap = wbuf2B[sl2] if sl2 < 2 else C_B[:, 2 * D:4 * D]
                g0 = wguard2[sl2] if sl2 < 2 else wgF3v[0]
                qn = "qv2" if sl2 == 2 else f"qv{sl2}"
                gw = [("t", g0)] + SPLITG + ([("v", mark["vC"])] if sl2 == 2 else [])
                on(WQE[sl2], lambda e, co=co, wb2_ap=wb2_ap: e.dma_start(
                    out=wb2_ap[:, 0:2 * D], in_=fwvalHL_d[co * 128:(co + 1) * 128, :]),
                   wait=gw, inc=qn, k=16)
                first = True
                for jl in (0, 1):
                    pvk = P_E if jl == 0 else P_F
                    for half in (0, 1):
                        for which, wo in ((0, 0), (0, D), (1, 0)):
                            on("tensor", lambda e, jl=jl, half=half, pvk=pvk, wb2_ap=wb2_ap,
                               which=which, wo=wo, PB=PB, base=base: e.matmul(
                                pvk[:, half * 512:(half + 1) * 512],
                                PB[:, base + which * 256 + jl * 128:base + which * 256 + jl * 128 + 128],
                                wb2_ap[:, wo + half * 512:wo + half * 512 + 512],
                                start=(co == 0 and which == 0 and wo == 0),
                                stop=(co == NCO - 1 and which == 1)),
                               wait=(wv + now(qn) + ([pfree["P_E"], pfree["P_F"]] if co == 0 else [])) if first else None,
                               inc="t" if (jl == 1 and half == 1 and which == 1) else None)
                            first = False
                if sl2 < 2:
                    wguard2[sl2] = fl.cnt["t"]
                else:
                    wgF3v[0] = fl.cnt["t"]
                mark[f"fkscr{p}"] = fl.cnt["t"]

            for co in range(NCO):
                emit_fk(co)
                if co >= 1:
                    emit_chain_vk(co - 1)
            emit_chain_vk(NCO - 1)
            pfree["P_A"], pfree["P_B"] = pm[NCO - 2], pm[NCO - 1]
            wte = now("t")
            for jl in (0, 1):
                jj = hh * 2 + jl
                pvk = P_E if jl == 0 else P_F
                on("vector", lambda e, jj=jj, pvk=pvk: e.tensor_mul(
                    u1t[:], frs[jj][:], pvk[:]), wait=wte, inc="v")
                on("vector", lambda e, jj=jj: e.tensor_add(
                    x2t[jj][:], u1t[:], x2t[jj][:]), wait=now("v"), inc="v")
                pfree["P_E" if jl == 0 else "P_F"] = ("v", fl.cnt["v"])

    def gates_combine(b):
        def slot(i):
            return rsc[:, i * 8:(i + 1) * 8]

        def vsw():
            on("vector", lambda e: None, wait=now("v"))

        def ssw():
            on("scalar", lambda e: None, wait=now("s"))
        for jj in range(TPB):
            t = b * TPB + jj
            to = (b - 1) * TPB + jj
            on("vector", lambda e, to=to, t=t: e.tensor_scalar(
                slot(0), z_own[:, to * 8:(to + 1) * 8], rsig0_all[:, t:t + 1], None,
                Alu.mult), wait=now("s"), inc="v")
            vsw()
            on("vector", lambda e: e.tensor_reduce(gs(0), slot(0), mybir.AxisListType.X,
                                                   Alu.max, negate=True), inc="v")
            wv = now("v")
            on("scalar", lambda e: e.activation(slot(1), slot(0), Act.Exp, bias=gs(0),
                                                accum_out=gs(1)), wait=wv, inc="s")
            ws = now("s")
            on("vector", lambda e: e.reciprocal(gs(2), gs(1)), wait=ws, inc="v")
            wv = now("v")
            on("scalar", lambda e: e.mul(slot(2), slot(1), gs(2)), wait=wv, inc="s")
            ws = now("s")
            on("vector", lambda e, to=to: e.tensor_mul(slot(3), oh1k_own[:, to * 8:(to + 1) * 8],
                                                       slot(2)), wait=ws, inc="v")
            vsw()
            on("vector", lambda e: e.tensor_reduce(gs(3), slot(3), mybir.AxisListType.X, Alu.add), inc="v")
            vsw()
            on("vector", lambda e, to=to: e.tensor_mul(slot(4), oh2k_own[:, to * 8:(to + 1) * 8],
                                                       slot(2)), inc="v")
            vsw()
            on("vector", lambda e: e.tensor_reduce(gs(4), slot(4), mybir.AxisListType.X, Alu.add), inc="v")
            wv = now("v")
            on("scalar", lambda e: e.activation(gs(5), gs(3), Act.Identity, bias=gs(4)),
               wait=wv, inc="s")
            ssw()
            on("scalar", lambda e: e.activation(gs(6), gs(5), Act.Identity, bias=1e-6), inc="s")
            ws = now("s")
            on("vector", lambda e: e.reciprocal(gs(7), gs(6)), wait=ws, inc="v")
            vsw()
            on("vector", lambda e: e.tensor_scalar(gs(8), gs(3), gs(7), None, Alu.mult), inc="v")
            vsw()
            on("vector", lambda e: e.tensor_scalar(gs(9), gs(4), gs(7), None, Alu.mult), inc="v")
            wv = now("v")
            on("scalar", lambda e, to=to: e.mul(slot(5), oh1k_own[:, to * 8:(to + 1) * 8], gs(8)),
               wait=wv, inc="s")
            ssw()
            on("scalar", lambda e, to=to: e.mul(slot(6), oh2k_own[:, to * 8:(to + 1) * 8], gs(9)), inc="s")
            ws = now("s")
            on("vector", lambda e, to=to: e.tensor_add(cmb[:, to * 8:(to + 1) * 8],
                                                       slot(5), slot(6)), wait=ws, inc="v")
            vsw()

    def stageM(b):
        wguard3[0] = max(wguard3[0], fl.cnt["t"])
        absB3 = absB[:, 0:2 * D]
        for hh in (0, 1):
            for jl in (0, 1):
                jj = hh * 2 + jl
                t = b * TPB + jj
                on("vector", lambda e, jj=jj, t=t: e.tensor_scalar(
                    xnt[:], xt[jj][:], m0_all[:, t:t + 1], rsig0_all[:, t:t + 1],
                    Alu.subtract, Alu.mult), wait=[("s", mark.get("xnt_used_s", 0))], inc="v")
                wv = now("v") + [pfree["P_F"]]
                for c in range(NCH):
                    on("tensor", lambda e, c=c: e.matmul(
                        P_F[:, c * 128:(c + 1) * 128], xnt[:, c * 128:(c + 1) * 128], idn[:],
                        is_transpose=True, start=(c % 4 == 0), stop=(c % 4 == 3)),
                       wait=wv if c == 0 else None, inc="t" if c == NCH - 1 else None)
                wt = now("t")
                for c in range(NCH):
                    on("scalar", lambda e, c=c, jl=jl: e.copy(
                        XNTs[:, c * 256 + jl * 128:c * 256 + jl * 128 + 128],
                        P_F[:, c * 128:(c + 1) * 128]), wait=wt if c == 0 else None, inc="s")
                mark["xnt_used_s"] = fl.cnt["s"]
                pfree["P_F"] = ("s", fl.cnt["s"])
            ws_x = now("s")
            for c in range(NCH):
                on("scalar", lambda e, c=c: e.copy(
                    A_B[:, 4096 + c * 256:4096 + (c + 1) * 256], XNTs[:, c * 256:(c + 1) * 256]),
                   wait=ws_x, inc="s")
                on("vector", lambda e, c=c: e.tensor_tensor(
                    A_B[:, 6144 + c * 256:6144 + (c + 1) * 256], XNTs[:, c * 256:(c + 1) * 256],
                    A_B[:, 4096 + c * 256:4096 + (c + 1) * 256], Alu.subtract),
                   wait=now("s"), inc="v")
            ws_x = now("s", "v")
            for mo in range(DM // 128):
                sl = mo % 3
                wb_ap = wbufB[sl] if sl < 2 else absB3
                g0 = wguard[sl] if sl < 2 else wguard3[0]
                on(WQE[sl], lambda e, mo=mo, wb_ap=wb_ap: e.dma_start(
                    out=wb_ap[:, 0:2 * D].rearrange("p (ci o) -> p ci o", o=256),
                    in_=wgateHL_d.rearrange("(ci p) o -> p ci o", p=128)[:, :, mo * 256:(mo + 1) * 256]),
                   wait=[("t", g0)] + SPLITG, inc=("qw2" if sl == 2 else f"qw{sl}"), k=16)
                pn = "P_A" if mo % 2 == 0 else "P_B"
                wt = now("qw2" if sl == 2 else f"qw{sl}") + ws_x + [pfree[pn]]
                first = True
                for ci in range(NCH):
                    for wo, ao in ((0, 4096), (0, 6144), (128, 4096)):
                        on("tensor", lambda e, ci=ci, wb_ap=wb_ap, pn=pn, wo=wo, ao=ao: e.matmul(
                            PS[pn][:, 0:256], wb_ap[:, ci * 256 + wo:ci * 256 + wo + 128],
                            A_B[:, ao + ci * 256:ao + ci * 256 + 256],
                            start=(ci == 0 and wo == 0 and ao == 4096),
                            stop=(ci == NCH - 1 and wo == 128)),
                           wait=wt if first else None,
                           inc="t" if (ci == NCH - 1 and wo == 128) else None)
                        first = False
                if sl < 2:
                    wguard[sl] = fl.cnt["t"]
                else:
                    wguard3[0] = fl.cnt["t"]
                wte = now("t")
                on("scalar", lambda e, pn=pn: e.activation(sgx[:, 0:256], PS[pn][:, 0:256],
                                                           Act.Sigmoid),
                   wait=wte + [("v", mark.get("sgm_used", 0))], inc="s")
                on("vector", lambda e, pn=pn: e.tensor_mul(
                    sgx[:, 256:512], sgx[:, 0:256], PS[pn][:, 0:256]),
                   wait=now("s"), inc="v")
                pfree[pn] = ("v", fl.cnt["v"])
                on("scalar", lambda e, mo=mo: e.copy(
                    C_B[:, mo * 256:(mo + 1) * 256], sgx[:, 256:512]), wait=now("v"), inc="s")
                on("vector", lambda e, mo=mo: e.tensor_tensor(
                    C_B[:, 4096 + mo * 256:4096 + (mo + 1) * 256], sgx[:, 256:512],
                    C_B[:, mo * 256:(mo + 1) * 256], Alu.subtract),
                   wait=now("s"), inc="v")
                mark["sgm_used"] = fl.cnt["v"]
                mark[f"hms{mo}"] = fl.cnt["v"]
            DB = {0: [["P_E", "P_E"], ["P_F", "P_F"]],
                  1: [["P_A", "P_B"], ["P_C", "P_D"]]}

            def dslice(p, jl, half):
                nm = DB[p][jl][half]
                off = half * 512 if p == 0 else 0
                return PS[nm][:, off:off + 512]
            for e_ in range(NE):
                p = e_ % 2
                bset = ["P_E", "P_F"] if p == 0 else ["P_A", "P_B", "P_C", "P_D"]
                for mo in range(DM // 128):
                    k = e_ * (DM // 128) + mo
                    sl = k % 3
                    wb_ap = wbuf2B[sl] if sl < 2 else absB3
                    g0 = wguard2[sl] if sl < 2 else wguard3[0]
                    on(WQE[sl], lambda e, e_=e_, mo=mo, wb_ap=wb_ap: e.dma_start(
                        out=wb_ap[:, 0:2 * D], in_=wdownHL_d[e_, mo * 128:(mo + 1) * 128, :]),
                       wait=[("t", g0)] + SPLITG, inc=("qv2" if sl == 2 else f"qv{sl}"), k=16)
                    wt = now("qv2" if sl == 2 else f"qv{sl}") + [("v", mark[f"hms{mo}"])] \
                        + ([pfree[nm] for nm in bset] if mo == 0 else [])
                    first = True
                    for jl in (0, 1):
                        for half in (0, 1):
                            for ho, wo in ((0, 0), (0, D), (4096, 0)):
                                on("tensor", lambda e, jl=jl, half=half, p=p, wb_ap=wb_ap, mo=mo, ho=ho, wo=wo: e.matmul(
                                    dslice(p, jl, half),
                                    C_B[:, ho + mo * 256 + jl * 128:ho + mo * 256 + jl * 128 + 128],
                                    wb_ap[:, wo + half * 512:wo + half * 512 + 512],
                                    start=(mo == 0 and ho == 0 and wo == 0),
                                    stop=(mo == DM // 128 - 1 and ho == 4096)),
                                   wait=wt if first else None,
                                   inc="t" if (jl == 1 and half == 1 and ho == 4096) else None)
                                first = False
                    if sl < 2:
                        wguard2[sl] = fl.cnt["t"]
                    else:
                        wguard3[0] = fl.cnt["t"]
                wte = now("t")
                for jl in (0, 1):
                    jj = hh * 2 + jl
                    to = (b - 1) * TPB + jj
                    cc = cmb[:, to * 8 + e_:to * 8 + e_ + 1]
                    if p == 0:
                        pvk = P_E if jl == 0 else P_F
                        if e_ == 0:
                            on("vector", lambda e, jj=jj, pvk=pvk, cc=cc: e.tensor_scalar(
                                yt[jj], pvk[:], cc, None, Alu.mult), wait=wte, inc="v")
                        else:
                            on("vector", lambda e, jj=jj, pvk=pvk, cc=cc: e.scalar_tensor_tensor(
                                yt[jj], pvk[:], cc, yt[jj], Alu.mult, Alu.add), wait=wte, inc="v")
                        pfree[DB[0][jl][0]] = ("v", fl.cnt["v"])
                    else:
                        for half in (0, 1):
                            on("vector", lambda e, jj=jj, jl=jl, half=half, cc=cc: e.scalar_tensor_tensor(
                                yt[jj][:, half * 512:(half + 1) * 512], dslice(1, jl, half), cc,
                                yt[jj][:, half * 512:(half + 1) * 512], Alu.mult, Alu.add),
                               wait=wte if (jl == 0 and half == 0) else None, inc="v")
                            pfree[DB[1][jl][half]] = ("v", fl.cnt["v"])
            for jl in (0, 1):
                jj = hh * 2 + jl
                wv = now("v")
                on("scalar", lambda e, jj=jj: e.activation(sgx[:], yt[jj], Act.Sigmoid),
                   wait=wv + [("v", mark.get("sgm_used", 0))], inc="s")
                ws = now("s")
                on("vector", lambda e, jj=jj: e.tensor_mul(yt[jj], sgx[:], yt[jj]),
                   wait=ws, inc="v")
                mark["sgm_used"] = fl.cnt["v"]
                on("vector", lambda e, jj=jj: e.tensor_mul(yt[jj], yt[jj], x2t[jj][:]), inc="v")
                on("vector", lambda e, jj=jj: e.tensor_add(yt[jj], yt[jj], xt[jj][:]), inc="v")
                wv = now("v")
                row0 = (b - 1) * BLK + jj * 128
                on("sync", lambda e, jj=jj, row0=row0: e.dma_start(
                    out=y_out[row0:row0 + 128, :], in_=yt[jj]), wait=wv, inc=f"qo{jj}", k=16)

    for b in range(NBLK):
        stageA(b)
        stageB(b)
        if DEV_STOP == "B":
            mark["x_users"] = fl.cnt["v"]
            continue
        stageS(b)
        if DEV_STOP == "S":
            mark["x_users"] = fl.cnt["v"]
            continue
        stageO(b)
        if b >= 1 and DEV_STOP is None:
            gates_combine(b)
        if DEV_STOP != "O":
            stageL2(b)
        if b >= 1 and DEV_STOP is None:
            stageF(b)
            stageM(b)
            on("vector", lambda e: None, wait=now("qo0", "qo1", "qo2", "qo3"))
        mark["x_users"] = fl.cnt["v"]





def _host_prep(inputs):
    x = np.ascontiguousarray(np.asarray(inputs["L0_attn_x"], np.float32).reshape(NTOK, D))
    pvec = np.stack([
        np.asarray(inputs["L0_attn_ln1_w"], np.float32),
        np.asarray(inputs["L0_attn_ln1_b"], np.float32),
        np.asarray(inputs["L0_attn_ln2_w"], np.float32),
        np.asarray(inputs["L0_attn_ln2_b"], np.float32),
        np.asarray(inputs["L0_attn_time_decay"], np.float32),
        np.asarray(inputs["L0_attn_time_first"], np.float32),
        np.asarray(inputs["L0_attn_time_mix_k"], np.float32).reshape(D),
        np.asarray(inputs["L0_attn_time_mix_v"], np.float32).reshape(D),
        np.asarray(inputs["L0_attn_time_mix_r"], np.float32).reshape(D),
        np.asarray(inputs["L0_attn_f_time_mix_k"], np.float32).reshape(D),
        np.asarray(inputs["L0_attn_f_time_mix_r"], np.float32).reshape(D),
        np.zeros(D, np.float32),
    ], axis=0)
    pvec = pvec.reshape(12, NCH, 128).transpose(2, 0, 1).reshape(128, 12 * NCH)
    shared = {
        "xall": x,
        "idn": np.eye(128, dtype=np.float32),
        "triu": np.triu(np.ones((128, 128), np.float32)),
        "onesc": np.ones((128, 1), np.float32),
        "onesr": np.ones((1, 128), np.float32),
        "wkey": np.ascontiguousarray(np.asarray(inputs["L0_attn_w_key"], np.float32)),
        "wval": np.ascontiguousarray(np.asarray(inputs["L0_attn_w_value"], np.float32)),
        "wrec": np.ascontiguousarray(np.asarray(inputs["L0_attn_w_recept"], np.float32)),
        "wout": np.ascontiguousarray(np.asarray(inputs["L0_attn_w_output"], np.float32)),
        "fwkey": np.ascontiguousarray(np.asarray(inputs["L0_attn_f_w_key"], np.float32)),
        "fwrec": np.ascontiguousarray(np.asarray(inputs["L0_attn_f_w_recept"], np.float32)),
        "fwval": np.ascontiguousarray(np.asarray(inputs["L0_attn_f_w_value"], np.float32)),
        "wgateT": np.ascontiguousarray(np.asarray(inputs["L0_ffn_w_gate"], np.float32).T),
        "wdownT": np.ascontiguousarray(np.asarray(inputs["L0_ffn_w_down"], np.float32).transpose(0, 2, 1)),
        "wrouter": np.ascontiguousarray(np.asarray(inputs["L0_ffn_w_router"], np.float32)),
        "pvec": np.ascontiguousarray(pvec),
        "fconst": np.tile(np.array(FCONSTS, np.float32), (128, 1)),
    }
    in_maps = []
    for c in range(NC):
        q = c % 4
        gs = c * OWN
        if q == 0:
            xph = np.concatenate([np.zeros((NH, D), np.float32), x[gs:gs + OWN]], axis=0)
            m = 0.0
        else:
            xph = x[gs - NH:gs + OWN]
            m = 1.0
        im = dict(shared)
        im["xph"] = np.ascontiguousarray(xph)
        im["mh"] = np.full((128, 1), m, np.float32)
        im["mhinv"] = np.full((128, 1), 1.0 - m, np.float32)
        im["rank"] = np.full((1, 1), c, np.int32)
        gs_ = np.zeros((128, 8), np.float32)
        gs_[:, c] = 1.0
        im["gsel"] = gs_
        in_maps.append(im)
    return in_maps


def kernel(**inputs):
    in_maps = _host_prep(inputs)
    nc, ctx = build_program(taps=TAPS)
    try:
        res = run_bass_kernel_spmd(nc, in_maps, core_ids=list(range(NC)))
    finally:
        ctx.close()
    parts = [res.results[c]["y"] for c in range(NC)]
    out = np.concatenate(parts, axis=0).reshape(NB, T, D).astype(np.float32)
    if TAPS:
        kernel.last_results = res.results
    return out
